# version 08
# baseline (speedup 1.0000x reference)
import contextlib
import numpy as np
import concourse.bass as bass
import concourse.mybir as mybir
from concourse.bass_utils import run_bass_kernel_spmd

F32 = mybir.dt.float32
BF16 = mybir.dt.bfloat16
AF = mybir.ActivationFunctionType
ALU = mybir.AluOpType
AX = mybir.AxisListType

D = 1024
NQ, NKV, HD, SW = 16, 4, 64, 128
INTER = 1024
NE = 8
FF = 4096
NS = 2
EPS = 1e-5
LIMIT = 7.0
BIG = 1e9
SCALE = HD ** (-0.25)
NPASS = NS + NE


def head_of(pb, half):
    g2, j4 = divmod(pb, 4)
    return (2 * g2 + half) * 4 + j4


def build_nc(NT=16, debug=False):
    T = NT * 128
    Tq = 1024 if T >= 2048 else T
    n_half = T // Tq
    ntt_q = Tq // 128
    n_tc = T // 512
    nc = bass.Bass("TRN2", target_bir_lowering=False,
                   dynamic_dma_scratch_size=16384)
    dram = {}

    def din(name, shape):
        dram[name] = nc.dram_tensor(name, list(shape), F32, kind="ExternalInput")
        return dram[name]

    x_in = din("x_in", [T, D])
    din("attn_norm_scale", [D])
    din("Wq", [D, NQ * HD]); din("bq", [NQ * HD])
    din("Wk", [D, NKV * HD]); din("bk", [NKV * HD])
    din("Wv", [D, NKV * HD]); din("bv", [NKV * HD])
    din("Wo", [NQ, HD, D]); din("bo", [D])
    din("sinks", [NQ])
    din("moe_norm_scale", [D])
    din("router_W", [D, NE]); din("router_b", [NE])
    din("gate_up_proj", [NE, D, 2 * INTER]); din("gate_up_bias", [NE, 2 * INTER])
    din("down_proj", [NE, INTER, D]); din("down_bias", [NE, D])
    din("centroids", [NE, D]); din("routing_bias", [NE])
    din("Wfc_r", [NE, FF, D]); din("Wproj_r", [NE, D, FF])
    din("Wfc_sh", [NS, FF, D]); din("Wproj_sh", [NS, D, FF])
    din("COS", [T, 32]); din("SIN", [T, 32])
    din("MASKS", [3, 128, 384])
    din("IDENT", [128, 128])

    out_t = nc.dram_tensor("out", [T, D], F32, kind="ExternalOutput")
    dbg = {}
    if debug:
        for n in ["dbg_A1", "dbg_A3", "dbg_A4"]:
            dbg[n] = nc.dram_tensor(n, [T, D], F32, kind="ExternalOutput")
        dbg["dbg_SC"] = nc.dram_tensor("dbg_SC", [128, NT * 8], F32, kind="ExternalOutput")
        dbg["dbg_KFM"] = nc.dram_tensor("dbg_KFM", [128, 2 * T], F32, kind="ExternalOutput")
        dbg["dbg_VT"] = nc.dram_tensor("dbg_VT", [128, NT * 256], F32, kind="ExternalOutput")
        dbg["dbg_SM"] = nc.dram_tensor("dbg_SM", [128, 768], F32, kind="ExternalOutput")
        dbg["dbg_PB"] = nc.dram_tensor("dbg_PB", [128, 768], F32, kind="ExternalOutput")
        dbg["dbg_OF"] = nc.dram_tensor("dbg_OF", [128, 1024], F32, kind="ExternalOutput")
        dbg["dbg_CB"] = nc.dram_tensor("dbg_CB", [128, NT * 8], F32, kind="ExternalOutput")
    st_fc = nc.dram_tensor("st_fc", [NPASS, FF, D], BF16, kind="Internal")
    st_pj = nc.dram_tensor("st_pj", [NPASS, D, FF], BF16, kind="Internal")
    st_fcT = nc.dram_tensor("st_fcT", [NPASS, 16, 128, 8, 256], BF16, kind="Internal")
    st_pjT = nc.dram_tensor("st_pjT", [NPASS, 32, 128, 1024], BF16, kind="Internal")

    SB = nc.alloc_sbuf_tensor
    A = SB("A", [128, NT * 1024], F32)
    BFAC = 57344
    BFA = SB("BFA", [128, BFAC], BF16)

    KFM_o = 0
    VTOK_o = 2 * T
    QROPE_o = VTOK_o + NT * 256
    KROPE_o = QROPE_o + 1024
    QR_o = KROPE_o + 256
    OFMS_o = QR_o + 2048
    XHATS_o = OFMS_o + 1024
    XF1_o = XHATS_o + 2048
    DUMP_o = XF1_o + 2048
    PTS_o = DUMP_o + 1024
    WQKV_off = max(16 * T, PTS_o + 768)
    WO_off = WQKV_off + 12288
    assert WO_off + 8192 <= BFAC
    XF_o = 0
    GATED_o = 8 * T
    WG_o, WU_o, WD_o = WQKV_off, WQKV_off + 8192, WQKV_off + 16384
    assert WD_o + 8192 <= BFAC
    SC2_o = GATED_o
    HR_o = 8 * T
    WFC_o = HR_o + 32 * Tq
    WPJ_o = WFC_o + 4096
    assert WPJ_o + 4096 <= BFAC

    XR = SB("XR", [128, 3 * 1024], F32)
    if WQKV_off - (PTS_o + 768) >= 7680:
        B1S_o = PTS_o + 768
    else:
        B1S_o = WD_o + 8192
        assert BFAC - B1S_o >= 7680
    MASKS = BFA[:, B1S_o:B1S_o + 2304].bitcast(F32)
    COSB = BFA[:, B1S_o + 2304:B1S_o + 2304 + NT * 32]
    SINB = BFA[:, B1S_o + 2816:B1S_o + 2816 + NT * 32]
    SM = BFA[:, B1S_o + 3328:B1S_o + 4864].bitcast(F32)
    PBUF = BFA[:, B1S_o + 4864:B1S_o + 5632]
    G2T = BFA[:, B1S_o + 5632:B1S_o + 6656]
    assert B1S_o + 6656 <= BFAC
    IDENTB = SB("IDENTB", [128, 128], BF16)
    IDENTF = SB("IDENTF", [128, 128], F32)
    NEGSINK = SB("NEGSINK", [128, 16], F32)
    SINKB = SB("SINKB", [128, 16], F32)
    GUPB = SB("GUPB", [128, 128], F32)
    RBB = SB("RBB", [128, 8], F32)
    SCORES = SB("SCORES", [128, NT * 8], F32)
    COMB = SB("COMB", [128, NT * 8], F32)
    STAT = SB("STAT", [128, 48], F32)
    LSC = SB("LSC", [128, 32], F32)
    GTS = SB("GTS", [128, 2 * 512], BF16)
    GLUS = SB("GLUS", [128, 2 * 512], BF16)
    UP2S = SB("UP2S", [128, 2 * 512], BF16)
    ROWS = SB("ROWS", [1, 1544], BF16)
    BOROW = SB("BOROW", [1, 1024], BF16)
    DBROW = SB("DBROW", [1, 1024], BF16)
    SROW = SB("SROW", [1, 64], F32)
    G1C = SB("G1C", [128, 8], F32)
    ONESC = SB("ONESC", [1, 128], BF16)
    ONESCF = SB("ONESCF", [1, 128], F32)
    CENTT = SB("CENTT", [128, 64], BF16)
    ROUTW = SB("ROUTW", [128, 64], BF16)
    EPSC = SB("EPSC", [128, 1], F32)
    G2ROW = XR[32:33, 0:1024]
    SROW = XR[64:65, 0:64]
    CENTN = XR[0:8, 2048:2560].bitcast(BF16)
    ONESF32 = XR[32:33, 1024:1152]
    ONESF64 = XR[64:65, 64:192]

    names = ["sv_eps", "ss_rstd", "sv_xhat", "sp_trx", "ss_cpx", "sp_qkv", "sv_ropek", "sv_ropeq", "sv_ropeall", "sp_trq", "ss_cpq", "sp_trk", "ss_cpk", "sp_s", "sv_mask", "sv_negm", "sp_pt", "ss_pt", "ss_exp", "sv_pn", "sp_av", "ss_ofm", "sp_wo", "sv_evac", "s_wsmall", "s_const", "s_constg", "s_b0", "ev", "es", "b2_eps", "b2_rstd", "b2_xhat", "b2_trx", "b2_cpx", "b2_rt", "b2_sc", "b2_lsel", "b2_exp", "sw_g", "sw_u", "sw_d", "sp_g", "sp_u", "sv_gt", "sv_up", "ss_glu", "sv_gated", "sp_d", "sv_devac", "b4_eps", "b4_rstd", "b4_xhat", "b4_trx", "b4_cpx", "b4_rt", "b4_sc", "b4_sig", "b4_rsel", "b4_exp", "s_cent", "sp_fc", "ss_hr", "sp_pj", "sv_pevac", "s_t2a", "s_t2b", "s_t2c", "s_wba", "s_wbb", "s_wbc", "b6_eps", "b6_rstd", "b6_out", "s_out", "s_dbg", "swp3", "sx0", "sx1", "sx2", "b4x0", "b4x1", "b4x2", "so0", "so1", "so2", "swf0", "swf1", "swp0", "swp1", "swp2", "s_stg0", "s_stg1", "s_stg2", "s_stg3", "s_stg4", "s_stg5", "s_stg6", "s_stg7", "s_stg8", "s_stg9"]
    S_ = {n: nc.alloc_semaphore(n) for n in names}

    ecnt = {"v": [0], "s": [0]}

    def vedge(v, ins):
        ecnt["v"][0] += 1
        ins.then_inc(S_["ev"], 1)
        v.wait_ge(S_["ev"], ecnt["v"][0])

    def sedge(s, ins):
        ecnt["s"][0] += 1
        ins.then_inc(S_["es"], 1)
        s.wait_ge(S_["es"], ecnt["s"][0])

    def mask_ap(j):
        if j == 0:
            return MASKS[:, 384:768]
        if j == NT - 1:
            return MASKS[:, 768:1152]
        return MASKS[:, 0:384]

    XHATS = lambda r: BFA[:, XHATS_o + r * 1024:XHATS_o + (r + 1) * 1024]
    XF1 = lambda r, dc: BFA[:, XF1_o + r * 1024 + dc * 128:XF1_o + r * 1024 + (dc + 1) * 128]
    QRS = lambda r, pb: BFA[:, QR_o + (pb * 2 + r) * 128:QR_o + (pb * 2 + r + 1) * 128]
    OFMS = lambda pb: BFA[:, OFMS_o + pb * 128:OFMS_o + (pb + 1) * 128]
    QROPE = BFA[:, QROPE_o:QROPE_o + 1024]
    KROPE = BFA[:, KROPE_o:KROPE_o + 256]
    DUMP = BFA[:, DUMP_o:DUMP_o + 1024]
    PTS = lambda r: BFA[:, PTS_o + r * 384:PTS_o + (r + 1) * 384]
    XH2 = lambda r: BFA[:, SC2_o + r * 1024:SC2_o + (r + 1) * 1024]
    DUMP2 = BFA[:, SC2_o + 2048:SC2_o + 3072]
    XF2t = lambda dc, i: BFA[:, XF_o + dc * T + i * 128:XF_o + dc * T + (i + 1) * 128]
    XF2c = lambda dc, c0, c1: BFA[:, XF_o + dc * T + c0:XF_o + dc * T + c1]
    GATED = lambda ic, c0, c1: BFA[:, GATED_o + ic * T + c0:GATED_o + ic * T + c1]
    HR = lambda fc, c0, c1: BFA[:, HR_o + fc * Tq + c0:HR_o + fc * Tq + c1]
    WFCs = lambda r: BFA[:, WFC_o + r * 2048:WFC_o + (r + 1) * 2048]
    WPJs = lambda r: BFA[:, WPJ_o + r * 1024:WPJ_o + (r + 1) * 1024]

    N_CONST = 16 * 14
    N_CONSTG = 16 * 3
    N_WSMALL = 16 * 26

    with nc.Block("b0") as blk:
        @blk.sync
        def _(sy):
            sy.dma_start(MASKS.rearrange("p (m c) -> p m c", m=3),
                         dram["MASKS"].ap().rearrange("m p c -> p m c")).then_inc(S_["s_const"], 16)
            sy.dma_start(SROW[0:1, 0:16], dram["sinks"].ap().unsqueeze(0)).then_inc(S_["s_const"], 16)
            sy.dma_start(SROW[0:1, 16:24], dram["routing_bias"].ap().unsqueeze(0)).then_inc(S_["s_const"], 16)
            sy.dma_start(G2ROW, dram["moe_norm_scale"].ap().unsqueeze(0)).then_inc(S_["s_const"], 16)
            sy.dma_start(IDENTF[:, :], dram["IDENT"].ap()).then_inc(S_["s_const"], 16)
            for c in range(8):
                sy.dma_start(G1C[:, c:c + 1],
                             dram["attn_norm_scale"].ap()[c * 128:(c + 1) * 128].unsqueeze(1)).then_inc(S_["s_const"], 16)
            sy.dma_start(XR[0:8, 0:2048], dram["gate_up_bias"].ap()).then_inc(S_["s_const"], 16)

        @blk.gpsimd
        def _(g):
            g.dma_start(IDENTB[:, :], dram["IDENT"].ap()).then_inc(S_["s_constg"], 16)
            g.dma_start(COSB[:, :].rearrange("p (n j) -> p n j", j=32),
                        dram["COS"].ap().rearrange("(n p) j -> p n j", p=128)).then_inc(S_["s_constg"], 16)
            g.dma_start(SINB[:, :].rearrange("p (n j) -> p n j", j=32),
                        dram["SIN"].ap().rearrange("(n p) j -> p n j", p=128)).then_inc(S_["s_constg"], 16)
            g.dma_start(ROWS[0:1, 0:1024], dram["bq"].ap().unsqueeze(0)).then_inc(S_["s_wsmall"], 16)
            g.dma_start(ROWS[0:1, 1024:1280], dram["bk"].ap().unsqueeze(0)).then_inc(S_["s_wsmall"], 16)
            g.dma_start(ROWS[0:1, 1280:1536], dram["bv"].ap().unsqueeze(0)).then_inc(S_["s_wsmall"], 16)
            g.dma_start(ROWS[0:1, 1536:1544], dram["router_b"].ap().unsqueeze(0)).then_inc(S_["s_wsmall"], 16)
            g.dma_start(BOROW[0:1, :], dram["bo"].ap().unsqueeze(0)).then_inc(S_["s_wsmall"], 16)
            wq_v = BFA[:, WQKV_off:WQKV_off + 12288].rearrange("p (c f) -> p c f", c=8)
            g.dma_start(wq_v[:, :, 0:1024],
                        dram["Wq"].ap().rearrange("(c p) f -> p c f", p=128)).then_inc(S_["s_wsmall"], 16)
            g.dma_start(wq_v[:, :, 1024:1280],
                        dram["Wk"].ap().rearrange("(c p) f -> p c f", p=128)).then_inc(S_["s_wsmall"], 16)
            g.dma_start(wq_v[:, :, 1280:1536],
                        dram["Wv"].ap().rearrange("(c p) f -> p c f", p=128)).then_inc(S_["s_wsmall"], 16)
            for pb in range(8):
                for half in range(2):
                    h = head_of(pb, half)
                    g.dma_start(BFA[half * 64:half * 64 + 64,
                                    WO_off + pb * 1024:WO_off + (pb + 1) * 1024],
                                dram["Wo"].ap()[h, :, :]).then_inc(S_["s_wsmall"], 16)
            g.dma_start(ROUTW[:, :].rearrange("p (c e) -> p c e", c=8),
                        dram["router_W"].ap().rearrange("(c p) e -> p c e", p=128)).then_inc(S_["s_wsmall"], 16)
            g.dma_start(CENTN, dram["centroids"].ap()).then_inc(S_["s_wsmall"], 16)

        @blk.vector
        def _(v):
            v.memset(ONESC[:, :], 1.0)
            v.memset(ONESCF[:, :], 1.0)
            v.memset(ONESF32, 1.0)
            v.memset(ONESF64, 1.0)
            v.memset(EPSC[:, :], EPS)

    with contextlib.ExitStack() as ps:
        P_B0 = ps.enter_context(nc.psum_tensor("P_B0", [128, 128], F32))
        P_B0B = ps.enter_context(nc.psum_tensor("P_B0B", [128, 64], BF16))
        with nc.Block("b0b") as blk:
            @blk.tensor
            def _(t):
                t.wait_ge(S_["s_const"], N_CONST)
                t.wait_ge(S_["s_constg"], N_CONSTG)
                t.matmul(P_B0[:, 0:16], ONESF64, SROW[0:1, 0:16],
                         start=True, stop=True)
                t.matmul(P_B0[:, 16:24], ONESF64, SROW[0:1, 16:24],
                         start=True, stop=True).then_inc(S_["s_b0"], 1)

            @blk.vector
            def _(v):
                v.wait_ge(S_["s_b0"], 1)
                v.tensor_copy(SINKB[:, :], P_B0[:, 0:16])
                v.tensor_scalar(NEGSINK[:, :], P_B0[:, 0:16], -1.0, None, ALU.mult)
                v.tensor_copy(RBB[:, :], P_B0[:, 16:24])
                v.engine_nop().then_inc(S_["s_b0"], 1)

        with nc.Block("b0c") as blk:
            @blk.tensor
            def _(t):
                t.wait_ge(S_["s_b0"], 2)
                for c in range(16):
                    ins = t.transpose(P_B0[:, c * 8:(c + 1) * 8],
                                      XR[0:8, c * 128:(c + 1) * 128],
                                      IDENTF[0:8, 0:8])
                ins.then_inc(S_["s_b0"], 1)
                t.wait_ge(S_["s_wsmall"], N_WSMALL)
                for c in range(8):
                    ins = t.transpose(P_B0B[:, c * 8:(c + 1) * 8],
                                      CENTN[0:8, c * 128:(c + 1) * 128],
                                      IDENTB[0:8, 0:8])
                ins.then_inc(S_["s_b0"], 1)

            @blk.vector
            def _(v):
                v.wait_ge(S_["s_b0"], 3)
                v.tensor_copy(GUPB[:, :], P_B0[:, 0:128])
                v.wait_ge(S_["s_b0"], 4)
                v.tensor_copy(CENTT[:, :], P_B0B[:, :])
                v.engine_nop().then_inc(S_["s_cent"], 1)
                v.engine_nop().then_inc(S_["s_b0"], 1)

        with nc.Block("b0d") as blk:
            @blk.tensor
            def _(t):
                for c in range(8):
                    t.wait_ge(S_["s_b0"], 5 + 2 * c)
                    t.matmul(P_B0[:, :], ONESF32,
                             G2ROW[0:1, c * 128:(c + 1) * 128],
                             start=True, stop=True).then_inc(S_["s_b0"], 1)

            @blk.vector
            def _(v):
                for c in range(8):
                    v.wait_ge(S_["s_b0"], 6 + 2 * c)
                    v.tensor_copy(G2T[:, c * 128:(c + 1) * 128], P_B0[:, :])
                    v.engine_nop().then_inc(S_["s_b0"], 1)

        with nc.Block("b0e") as blk:
            @blk.vector
            def _(v):
                v.wait_ge(S_["s_wsmall"], N_WSMALL)
                v.wait_ge(S_["s_const"], N_CONST)
                v.wait_ge(S_["s_constg"], N_CONSTG)
                for dc in range(8):
                    ins = v.tensor_scalar(
                        BFA[:, WQKV_off + dc * 1536:WQKV_off + (dc + 1) * 1536],
                        BFA[:, WQKV_off + dc * 1536:WQKV_off + (dc + 1) * 1536],
                        G1C[:, dc:dc + 1], None, ALU.mult)
                ins.then_inc(S_["s_b0"], 1)
    N_B0 = 22

    with contextlib.ExitStack() as ps:
        P_QKV = ps.enter_context(nc.psum_tensor("P_QKV", [128, 1536], F32))
        P_TR = ps.enter_context(nc.psum_tensor("P_TR", [128, 1024], BF16))
        P_S = ps.enter_context(nc.psum_tensor("P_S", [128, 384], F32))
        P_PTK = ps.enter_context(nc.psum_tensor("P_PTK", [128, 1024], BF16))
        P_AV = ps.enter_context(nc.psum_tensor("P_AV", [128, 128], F32))
        P_WO = ps.enter_context(nc.psum_tensor("P_WO", [128, 512], F32))

        with nc.Block("b1") as blk:
            @blk.sync
            def _(sy):
                H2o = B1S_o + 6656
                slots = lambda r: BFA[:, H2o + r * 2048:H2o + (r + 1) * 2048]
                jobs = []
                for p in range(NPASS):
                    for dc in range(8):
                        for fh in range(2):
                            jobs.append(("fc", p, dc, fh))
                for p in range(NPASS):
                    for fc in range(32):
                        jobs.append(("pj", p, fc))
                n_jobs = len(jobs)

                def tr(k):
                    kind = jobs[k]
                    r = k % 3
                    if kind[0] == "fc":
                        _, p, dc, fh = kind
                        sy.wait_ge(S_["s_stg%d" % p], 32)
                        sy.dma_start(slots(r),
                                     st_fc.ap()[p][fh * 2048:(fh + 1) * 2048,
                                                   dc * 128:(dc + 1) * 128],
                                     transpose=True).then_inc(S_["s_t2" + "abc"[r]], 16)
                    else:
                        _, p, fc = kind
                        sy.wait_ge(S_["s_stg%d" % p], 32)
                        sy.dma_start(slots(r)[:, 0:1024],
                                     st_pj.ap()[p][:, fc * 128:(fc + 1) * 128],
                                     transpose=True).then_inc(S_["s_t2" + "abc"[r]], 16)

                def wb(k):
                    kind = jobs[k]
                    r = k % 3
                    if kind[0] == "fc":
                        _, p, dc, fh = kind
                        dst = st_fcT.ap()[p][fh * 8:(fh + 1) * 8, :, dc, :].transpose([1, 0, 2])
                        sy.dma_start(dst, slots(r).rearrange("p (s f) -> p s f", s=8)).then_inc(S_["s_wb" + "abc"[r]], 16)
                    else:
                        _, p, fc = kind
                        sy.dma_start(st_pjT.ap()[p][fc], slots(r)[:, 0:1024]).then_inc(S_["s_wb" + "abc"[r]], 16)

                kdone = [0]

                def hop2_step():
                    k = kdone[0]
                    if k >= n_jobs:
                        return False
                    if k < 3:
                        tr(k)
                    else:
                        sy.wait_ge(S_["s_t2" + "abc"[(k - 3) % 3]], 16 * ((k - 3) // 3 + 1))
                        wb(k - 3)
                        sy.wait_ge(S_["s_wb" + "abc"[(k - 3) % 3]], 16 * ((k - 3) // 3 + 1))
                        tr(k)
                    kdone[0] += 1
                    return True

                per_tile = (n_jobs + 3) // max(NT - 4, 1) + 1
                for i in range(NT):
                    if i >= 3:
                        sy.wait_ge(S_["sv_evac"], 2 * (i - 3) + 2)
                    sy.dma_start(XR[:, (i % 3) * 1024:(i % 3 + 1) * 1024],
                                 x_in.ap()[i * 128:(i + 1) * 128, :]).then_inc(S_["sx%d" % (i % 3)], 16)
                    if i >= 4:
                        for _ in range(per_tile):
                            hop2_step()
                while kdone[0] < n_jobs + 3:
                    if kdone[0] < n_jobs:
                        hop2_step()
                    else:
                        k = kdone[0]
                        sy.wait_ge(S_["s_t2" + "abc"[(k - 3) % 3]], 16 * ((k - 3) // 3 + 1))
                        wb(k - 3)
                        kdone[0] += 1
                for r_ in range(3):
                    sy.wait_ge(S_["s_wb" + "abc"[r_]],
                               16 * len([k for k in range(n_jobs) if k % 3 == r_]))

            @blk.gpsimd
            def _(g):
                def stage(k):
                    p, kind = divmod(k, 2)
                    if p < NS:
                        src = dram["Wfc_sh"].ap()[p] if kind == 0 else dram["Wproj_sh"].ap()[p]
                    else:
                        src = dram["Wfc_r"].ap()[p - NS] if kind == 0 else dram["Wproj_r"].ap()[p - NS]
                    dst = st_fc.ap()[p] if kind == 0 else st_pj.ap()[p]
                    g.dma_start(dst, src).then_inc(S_["s_stg%d" % p], 16)
                for k in range(2 * NPASS):
                    stage(k)
                g.wait_ge(S_["sp_qkv"], NT)
                gv = BFA[:, WG_o:WG_o + 8192].rearrange("p (c i) -> p c i", c=8)
                g.dma_start(gv, dram["gate_up_proj"].ap()[0].rearrange("(c p) i -> p c i", p=128)[:, :, 0:1024]).then_inc(S_["sw_g"], 16)
                g.wait_ge(S_["sp_wo"], 2 * NT)
                uv = BFA[:, WU_o:WU_o + 8192].rearrange("p (c i) -> p c i", c=8)
                g.dma_start(uv, dram["gate_up_proj"].ap()[0].rearrange("(c p) i -> p c i", p=128)[:, :, 1024:2048]).then_inc(S_["sw_u"], 16)
                dv = BFA[:, WD_o:WD_o + 8192].rearrange("p (c m) -> p c m", c=8)
                g.dma_start(dv, dram["down_proj"].ap()[0].rearrange("(c p) m -> p c m", p=128)).then_inc(S_["sw_d"], 16)
                g.dma_start(DBROW[0:1, :], dram["down_bias"].ap()[0].unsqueeze(0)).then_inc(S_["sw_d"], 16)

            @blk.tensor
            def _(t):
                t.wait_ge(S_["s_wsmall"], N_WSMALL)
                t.wait_ge(S_["s_b0"], N_B0)
                for i in range(NT + 1):
                    j = i - 1
                    if i < NT:
                        t.wait_ge(S_["sv_xhat"], i + 1)
                        if i >= 1:
                            t.wait_ge(S_["ss_cpq"], i)
                        for dc in range(8):
                            ins = t.transpose(P_TR[:, dc * 128:(dc + 1) * 128],
                                              XHATS(i % 2)[:, dc * 128:(dc + 1) * 128],
                                              IDENTB[:, :])
                        ins.then_inc(S_["sp_trx"], 1)
                        t.wait_ge(S_["ss_cpx"], i + 1)
                        if i >= 1:
                            t.wait_ge(S_["sv_ropeall"], i)
                        for nb in range(3):
                            t.matmul(P_QKV[:, nb * 512:(nb + 1) * 512], ONESC[0:1, :],
                                     ROWS[0:1, nb * 512:(nb + 1) * 512],
                                     start=True, stop=False)
                        for dc in range(8):
                            for nb in range(3):
                                ins = t.matmul(P_QKV[:, nb * 512:(nb + 1) * 512],
                                               XF1(i % 2, dc),
                                               BFA[:, WQKV_off + dc * 1536 + nb * 512:
                                                   WQKV_off + dc * 1536 + (nb + 1) * 512],
                                               start=False, stop=(dc == 7))
                        ins.then_inc(S_["sp_qkv"], 1)
                        t.wait_ge(S_["sv_ropek"], i + 1)
                        if i >= 1:
                            t.wait_ge(S_["ss_cpk"], i)
                        t.transpose(P_TR[:, 0:128], KROPE[:, 0:128], IDENTB[:, :])
                        t.transpose(P_TR[:, 128:256], KROPE[:, 128:256], IDENTB[:, :]).then_inc(S_["sp_trk"], 1)

                    if j >= 0:
                        KB = 0 if j == 0 else j - 1
                        nk = 2 if (j == 0 or j == NT - 1) else 3

                        def smm(hh):
                            pb2, half2 = divmod(hh - j * 16, 2)
                            h2 = head_of(pb2, half2)
                            g2_ = (h2 // 4) // 2
                            t.wait_ge(S_["ss_cpq"], j + 1)
                            t.wait_ge(S_["ss_cpk"], min(j + 2, NT))
                            t.wait_ge(S_["sv_ropeall"], min(j + 2, NT))
                            if hh > 0:
                                t.wait_ge(S_["sv_mask"], hh)
                            b = half2 * 64
                            t.matmul(P_S[:, 0:nk * 128],
                                     QRS(j % 2, pb2)[b:b + 64, :],
                                     BFA[b:b + 64,
                                         KFM_o + g2_ * T + KB * 128:
                                         KFM_o + g2_ * T + (KB + nk) * 128],
                                     start=True, stop=True).then_inc(S_["sp_s"], 1)

                        smm(j * 16)
                        for hl in range(16):
                            pb, half = divmod(hl, 2)
                            h = head_of(pb, half)
                            gk = h // 4
                            hh = j * 16 + hl
                            if hl + 1 < 16:
                                smm(hh + 1)
                            t.wait_ge(S_["sv_pn"], hh + 1)
                            if hh >= 1:
                                t.wait_ge(S_["ss_pt"], hh)
                            for k in range(nk):
                                ins = t.transpose(P_PTK[:, k * 128:(k + 1) * 128],
                                                  PBUF[:, (hh % 2) * 384 + k * 128:
                                                       (hh % 2) * 384 + (k + 1) * 128],
                                                  IDENTB[:, :])
                            ins.then_inc(S_["sp_pt"], 1)
                            if half == 0:
                                t.wait_ge(S_["ss_ofm"], j * 8 + pb)
                            t.wait_ge(S_["ss_pt"], hh + 1)
                            first = True
                            for k in range(nk):
                                ktg = KB + k
                                outv = P_AV[64:128, :] if half == 1 else P_AV[0:64, :]
                                tp = (0, 64) if half == 1 else None
                                ins = t.matmul(outv,
                                               BFA[:, VTOK_o + ktg * 256 + gk * 64:
                                                   VTOK_o + ktg * 256 + (gk + 1) * 64],
                                               PTS(hh % 2)[:, k * 128:(k + 1) * 128],
                                               start=first, stop=(k == nk - 1),
                                               tile_position=tp)
                                first = False
                            ins.then_inc(S_["sp_av"], 1)
                        t.wait_ge(S_["ss_ofm"], 8 * (j + 1))
                        for mh in range(2):
                            if 2 * j + mh >= 1:
                                t.wait_ge(S_["sv_evac"], 2 * j + mh)
                            t.matmul(P_WO[:, :], ONESC[0:1, :],
                                     BOROW[0:1, mh * 512:(mh + 1) * 512],
                                     start=True, stop=False)
                            for pb in range(8):
                                ins = t.matmul(P_WO[:, :], OFMS(pb),
                                               BFA[:, WO_off + pb * 1024 + mh * 512:
                                                   WO_off + pb * 1024 + (mh + 1) * 512],
                                               start=False, stop=(pb == 7))
                            ins.then_inc(S_["sp_wo"], 1)

                    if i < NT:
                        t.wait_ge(S_["sv_ropeq"], i + 1)
                        t.wait_ge(S_["ss_cpx"], i + 1)
                        t.wait_ge(S_["ss_cpk"], i + 1)
                        for pb in range(8):
                            ins = t.transpose(P_TR[:, pb * 128:(pb + 1) * 128],
                                              QROPE[:, pb * 128:(pb + 1) * 128],
                                              IDENTB[:, :])
                        ins.then_inc(S_["sp_trq"], 1)

            @blk.vector
            def _(v):
                for i in range(NT + 1):
                    j = i - 1
                    if i < NT:
                        xr = XR[:, (i % 3) * 1024:(i % 3 + 1) * 1024]
                        v.wait_ge(S_["sx%d" % (i % 3)], 16 * (i // 3 + 1))
                        if i >= 1:
                            v.wait_ge(S_["sv_ropeq"], i)
                        vedge(v, v.scalar_tensor_tensor(DUMP[:, :], xr, 1.0, xr,
                                                        ALU.bypass, ALU.mult,
                                                        accum_out=STAT[:, 0:1]))
                        v.tensor_scalar(STAT[:, 1:2], STAT[:, 0:1], 1.0 / D, EPS,
                                        ALU.mult, ALU.add).then_inc(S_["sv_eps"], 1)
                        v.wait_ge(S_["ss_rstd"], i + 1)
                        if i >= 2:
                            v.wait_ge(S_["sp_trx"], i - 1)
                        v.tensor_scalar(XHATS(i % 2), xr, STAT[:, 2:3], None,
                                        ALU.mult).then_inc(S_["sv_xhat"], 1)
                        v.wait_ge(S_["sp_qkv"], i + 1)
                        if i >= 1:
                            v.wait_ge(S_["sp_trq"], i)
                            v.wait_ge(S_["sp_trk"], i)
                        cosb = COSB[:, i * 32:(i + 1) * 32]
                        sinb = SINB[:, i * 32:(i + 1) * 32]
                        ksrc = P_QKV[:, 1024:1280].rearrange(
                            "p (gk hk dd two) -> p gk hk dd two", gk=2, hk=2, two=2)
                        kxe = ksrc[:, :, :, :, 0:1].squeeze(4)
                        kxo = ksrc[:, :, :, :, 1:2].squeeze(4)
                        kcos = cosb.unsqueeze(1).unsqueeze(1).broadcast_to([128, 2, 2, 32])
                        ksin = sinb.unsqueeze(1).unsqueeze(1).broadcast_to([128, 2, 2, 32])
                        kdst = KROPE[:, :].rearrange("p (gk hk ha dd) -> p gk hk ha dd",
                                                     gk=2, hk=2, ha=2)
                        kde = kdst[:, :, :, 0:1, :].squeeze(3)
                        kdo = kdst[:, :, :, 1:2, :].squeeze(3)
                        kt_ = DUMP[:, 0:128].rearrange("p (gk hk dd) -> p gk hk dd",
                                                       gk=2, hk=2)
                        vedge(v, v.tensor_tensor(kde, kxe, kcos, ALU.mult))
                        vedge(v, v.tensor_tensor(kt_, kxo, ksin, ALU.mult))
                        vedge(v, v.tensor_tensor(kde, kde, kt_, ALU.subtract))
                        vedge(v, v.tensor_tensor(kdo, kxo, kcos, ALU.mult))
                        vedge(v, v.tensor_tensor(kt_, kxe, ksin, ALU.mult))
                        v.tensor_tensor(kdo, kdo, kt_, ALU.add).then_inc(S_["sv_ropek"], 1)
                        qcos = cosb.unsqueeze(1).unsqueeze(1).broadcast_to([128, 2, 4, 32])
                        qsin = sinb.unsqueeze(1).unsqueeze(1).broadcast_to([128, 2, 4, 32])
                        for g2i in range(2):
                            qsrc = P_QKV[:, g2i * 512:(g2i + 1) * 512].rearrange(
                                "p (half j4 dd two) -> p half j4 dd two",
                                half=2, j4=4, two=2)
                            qxe = qsrc[:, :, :, :, 0:1].squeeze(4)
                            qxo = qsrc[:, :, :, :, 1:2].squeeze(4)
                            qdst = QROPE[:, g2i * 512:(g2i + 1) * 512].rearrange(
                                "p (j4 half ha dd) -> p j4 half ha dd",
                                j4=4, half=2, ha=2)
                            qde = qdst[:, :, :, 0:1, :].squeeze(3).transpose([0, 2, 1, 3])
                            qdo = qdst[:, :, :, 1:2, :].squeeze(3).transpose([0, 2, 1, 3])
                            qt_ = DUMP[:, 0:256].rearrange(
                                "p (half j4 dd) -> p half j4 dd", half=2, j4=4)
                            vedge(v, v.tensor_tensor(qde, qxe, qcos, ALU.mult))
                            vedge(v, v.tensor_tensor(qt_, qxo, qsin, ALU.mult))
                            vedge(v, v.tensor_tensor(qde, qde, qt_, ALU.subtract))
                            vedge(v, v.tensor_tensor(qdo, qxo, qcos, ALU.mult))
                            vedge(v, v.tensor_tensor(qt_, qxe, qsin, ALU.mult))
                            ins = v.tensor_tensor(qdo, qdo, qt_, ALU.add)
                            if g2i == 0:
                                vedge(v, ins)
                        ins.then_inc(S_["sv_ropeq"], 1)
                        v.tensor_copy(BFA[:, VTOK_o + i * 256:VTOK_o + (i + 1) * 256],
                                      P_QKV[:, 1280:1536]).then_inc(S_["sv_ropeall"], 1)
                    if j >= 0:
                        nk = 2 if (j == 0 or j == NT - 1) else 3
                        c0, c1 = 0, nk * 128
                        for hl in range(16):
                            pb, half = divmod(hl, 2)
                            h = head_of(pb, half)
                            hh = j * 16 + hl
                            r = hh % 2
                            smr = SM[:, r * 384 + c0:r * 384 + c1]
                            v.wait_ge(S_["sp_s"], hh + 1)
                            v.tensor_tensor(smr, P_S[:, c0:c1], mask_ap(j)[:, c0:c1],
                                            ALU.add).then_inc(S_["sv_mask"], 1)
                            v.wait_ge(S_["sv_mask"], hh + 1)
                            vedge(v, v.reduce_max(STAT[:, 4 + r:5 + r], smr,
                                                  axis=AX.X, negate=True))
                            v.tensor_scalar(STAT[:, 6 + r:7 + r], STAT[:, 4 + r:5 + r],
                                            NEGSINK[:, h:h + 1], None,
                                            ALU.min).then_inc(S_["sv_negm"], 1)
                            v.wait_ge(S_["ss_exp"], hh + 1)
                            vedge(v, v.tensor_tensor(STAT[:, 10 + r:11 + r],
                                                     STAT[:, 8 + r:9 + r],
                                                     STAT[:, 12 + r:13 + r], ALU.add))
                            vedge(v, v.reciprocal(STAT[:, 14 + r:15 + r],
                                                  STAT[:, 10 + r:11 + r]))
                            pbv = PBUF[:, r * 384 + c0:r * 384 + c1]
                            v.tensor_scalar(pbv, pbv, STAT[:, 14 + r:15 + r], None,
                                            ALU.mult).then_inc(S_["sv_pn"], 1)
                        for mh in range(2):
                            v.wait_ge(S_["sp_wo"], 2 * j + mh + 1)
                            v.tensor_tensor(A[:, j * 1024 + mh * 512:j * 1024 + (mh + 1) * 512],
                                            P_WO[:, :],
                                            XR[:, (j % 3) * 1024 + mh * 512:
                                               (j % 3) * 1024 + (mh + 1) * 512],
                                            ALU.add).then_inc(S_["sv_evac"], 1)

            @blk.scalar
            def _(s):
                for i in range(NT + 1):
                    j = i - 1
                    if i < NT:
                        s.wait_ge(S_["sv_eps"], i + 1)
                        sedge(s, s.activation(STAT[:, 3:4], STAT[:, 1:2], AF.Ln))
                        s.activation(STAT[:, 2:3], STAT[:, 3:4], AF.Exp,
                                     scale=-0.5).then_inc(S_["ss_rstd"], 1)
                        s.wait_ge(S_["sp_trx"], i + 1)
                        for dc in range(8):
                            ins = s.copy(XF1(i % 2, dc), P_TR[:, dc * 128:(dc + 1) * 128])
                        ins.then_inc(S_["ss_cpx"], 1)
                        s.wait_ge(S_["sp_trk"], i + 1)
                        s.copy(BFA[:, KFM_o + i * 128:KFM_o + (i + 1) * 128],
                               P_TR[:, 0:128])
                        s.copy(BFA[:, KFM_o + T + i * 128:KFM_o + T + (i + 1) * 128],
                               P_TR[:, 128:256]).then_inc(S_["ss_cpk"], 1)
                    if j >= 0:
                        for hl in range(16):
                            pb, half = divmod(hl, 2)
                            h = head_of(pb, half)
                            hh = j * 16 + hl
                            r = hh % 2
                            nk = 2 if (j == 0 or j == NT - 1) else 3
                            s.wait_ge(S_["sv_negm"], hh + 1)
                            if hh >= 2:
                                s.wait_ge(S_["sp_pt"], hh - 1)
                            kc0, kc1 = 0, nk * 128
                            s.activation(PBUF[:, r * 384 + kc0:r * 384 + kc1],
                                         SM[:, r * 384 + kc0:r * 384 + kc1], AF.Exp,
                                         bias=STAT[:, 6 + r:7 + r],
                                         accum_out=STAT[:, 8 + r:9 + r])
                            s.activation(STAT[:, 12 + r:13 + r], SINKB[:, h:h + 1],
                                         AF.Exp, bias=STAT[:, 6 + r:7 + r]).then_inc(S_["ss_exp"], 1)
                            s.wait_ge(S_["sp_pt"], hh + 1)
                            for k in range(nk):
                                ins = s.copy(PTS(r)[:, k * 128:(k + 1) * 128],
                                             P_PTK[:, k * 128:(k + 1) * 128])
                            ins.then_inc(S_["ss_pt"], 1)
                            if half == 1:
                                s.wait_ge(S_["sp_av"], hh + 1)
                                s.copy(OFMS(pb), P_AV[:, :]).then_inc(S_["ss_ofm"], 1)
                    if i < NT:
                        s.wait_ge(S_["sp_trq"], i + 1)
                        for pb in range(8):
                            ins = s.copy(QRS(i % 2, pb), P_TR[:, pb * 128:(pb + 1) * 128])
                        ins.then_inc(S_["ss_cpq"], 1)

    with contextlib.ExitStack() as ps:
        P_TR2 = ps.enter_context(nc.psum_tensor("P_TR2", [128, 1024], BF16))
        P_RT = ps.enter_context(nc.psum_tensor("P_RT", [128, 8], F32))
        with nc.Block("b2") as blk:
            if debug:
                @blk.sync
                def _(sy):
                    for i in range(NT):
                        sy.dma_start(dbg["dbg_A1"].ap()[i * 128:(i + 1) * 128, :],
                                     A[:, i * 1024:(i + 1) * 1024]).then_inc(S_["s_out"], 16)
                    sy.dma_start(dbg["dbg_SM"].ap(), SM).then_inc(S_["s_out"], 16)
                    sy.wait_ge(S_["s_out"], 16 * (NT + 1))

                @blk.gpsimd
                def _(g):
                    g.dma_start(dbg["dbg_KFM"].ap(), BFA[:, KFM_o:KFM_o + 2 * T]).then_inc(S_["s_dbg"], 16)
                    g.dma_start(dbg["dbg_VT"].ap(), BFA[:, VTOK_o:VTOK_o + NT * 256]).then_inc(S_["s_dbg"], 16)
                    g.dma_start(dbg["dbg_PB"].ap(), PBUF).then_inc(S_["s_dbg"], 16)
                    g.dma_start(dbg["dbg_OF"].ap(), BFA[:, OFMS_o:OFMS_o + 1024]).then_inc(S_["s_dbg"], 16)
                    g.wait_ge(S_["s_dbg"], 64)

            @blk.vector
            def _(v):
                for i in range(NT):
                    at = A[:, i * 1024:(i + 1) * 1024]
                    vedge(v, v.scalar_tensor_tensor(DUMP2[:, :], at, 1.0, at,
                                                    ALU.bypass, ALU.mult,
                                                    accum_out=STAT[:, 0:1]))
                    v.tensor_scalar(STAT[:, 1:2], STAT[:, 0:1], 1.0 / D, EPS,
                                    ALU.mult, ALU.add).then_inc(S_["b2_eps"], 1)
                    v.wait_ge(S_["b2_rstd"], i + 1)
                    if i >= 2:
                        v.wait_ge(S_["b2_trx"], i - 1)
                    v.scalar_tensor_tensor(XH2(i % 2), at, STAT[:, 2:3], G2T,
                                           ALU.mult, ALU.mult).then_inc(S_["b2_xhat"], 1)
                    v.wait_ge(S_["b2_rt"], i + 1)
                    v.tensor_copy(LSC[:, 0:8], P_RT[:, :]).then_inc(S_["b2_sc"], 1)
                    v.wait_ge(S_["b2_sc"], i + 1)
                    vedge(v, v.tensor_copy(LSC[:, 8:16], LSC[:, 0:8]))
                    for it in range(4):
                        vedge(v, v.reduce_max(STAT[:, 4:5], LSC[:, 8:16], axis=AX.X,
                                              negate=True))
                        if it == 0:
                            vedge(v, v.tensor_copy(STAT[:, 5:6], STAT[:, 4:5]))
                        if it < 3:
                            vedge(v, v.tensor_scalar(STAT[:, 6:7], STAT[:, 4:5],
                                                     -1.0, None, ALU.mult))
                            vedge(v, v.tensor_scalar(LSC[:, 16:24], LSC[:, 8:16],
                                                     STAT[:, 6:7], None, ALU.is_ge))
                            vedge(v, v.scalar_tensor_tensor(LSC[:, 8:16], LSC[:, 16:24],
                                                            -BIG, LSC[:, 8:16],
                                                            ALU.mult, ALU.add))
                    vedge(v, v.tensor_scalar(STAT[:, 6:7], STAT[:, 4:5], -1.0, None,
                                             ALU.mult))
                    vedge(v, v.tensor_scalar(LSC[:, 16:24], LSC[:, 0:8], STAT[:, 6:7],
                                             None, ALU.is_ge))
                    vedge(v, v.scalar_tensor_tensor(LSC[:, 24:32], LSC[:, 16:24], BIG,
                                                    LSC[:, 0:8], ALU.mult, ALU.add))
                    v.tensor_scalar(LSC[:, 24:32], LSC[:, 24:32], -BIG, None,
                                    ALU.add).then_inc(S_["b2_lsel"], 1)
                    v.wait_ge(S_["b2_exp"], i + 1)
                    vedge(v, v.reciprocal(STAT[:, 7:8], STAT[:, 9:10]))
                    v.tensor_scalar(SCORES[:, i * 8:(i + 1) * 8], LSC[:, 16:24],
                                    STAT[:, 7:8], 0.25, ALU.mult, ALU.mult)

            @blk.scalar
            def _(s):
                if debug:
                    s.wait_ge(S_["s_dbg"], 64)
                for i in range(NT):
                    s.wait_ge(S_["b2_eps"], i + 1)
                    sedge(s, s.activation(STAT[:, 3:4], STAT[:, 1:2], AF.Ln))
                    s.activation(STAT[:, 2:3], STAT[:, 3:4], AF.Exp,
                                 scale=-0.5).then_inc(S_["b2_rstd"], 1)
                    s.wait_ge(S_["b2_trx"], i + 1)
                    for dc in range(8):
                        ins = s.copy(XF2t(dc, i), P_TR2[:, dc * 128:(dc + 1) * 128])
                    ins.then_inc(S_["b2_cpx"], 1)
                    s.wait_ge(S_["b2_lsel"], i + 1)
                    s.activation(LSC[:, 16:24], LSC[:, 24:32], AF.Exp,
                                 bias=STAT[:, 5:6],
                                 accum_out=STAT[:, 9:10]).then_inc(S_["b2_exp"], 1)

            @blk.tensor
            def _(t):
                for i in range(NT):
                    t.wait_ge(S_["b2_xhat"], i + 1)
                    if i >= 1:
                        t.wait_ge(S_["b2_cpx"], i)
                    for dc in range(8):
                        ins = t.transpose(P_TR2[:, dc * 128:(dc + 1) * 128],
                                          XH2(i % 2)[:, dc * 128:(dc + 1) * 128],
                                          IDENTB[:, :])
                    ins.then_inc(S_["b2_trx"], 1)
                    t.wait_ge(S_["b2_cpx"], i + 1)
                    t.wait_ge(S_["b2_sc"], i)
                    t.matmul(P_RT[:, :], ONESC[0:1, :], ROWS[0:1, 1536:1544],
                             start=True, stop=False)
                    for dc in range(8):
                        ins = t.matmul(P_RT[:, :], XF2t(dc, i),
                                       ROUTW[:, dc * 8:(dc + 1) * 8],
                                       start=False, stop=(dc == 7))
                    ins.then_inc(S_["b2_rt"], 1)

    with contextlib.ExitStack() as ps:
        P_G = ps.enter_context(nc.psum_tensor("P_G", [128, 2048], F32))
        P_U = ps.enter_context(nc.psum_tensor("P_U", [128, 2048], F32))
        with nc.Block("b3") as blk:
            @blk.gpsimd
            def _(g):
                for e in range(1, NE):
                    g.wait_ge(S_["sp_g"], 8 * e)
                    gv = BFA[:, WG_o:WG_o + 8192].rearrange("p (c i) -> p c i", c=8)
                    g.dma_start(gv, dram["gate_up_proj"].ap()[e].rearrange("(c p) i -> p c i", p=128)[:, :, 0:1024]).then_inc(S_["sw_g"], 16)
                    g.wait_ge(S_["sp_u"], 8 * e)
                    uv = BFA[:, WU_o:WU_o + 8192].rearrange("p (c i) -> p c i", c=8)
                    g.dma_start(uv, dram["gate_up_proj"].ap()[e].rearrange("(c p) i -> p c i", p=128)[:, :, 1024:2048]).then_inc(S_["sw_u"], 16)
                    g.wait_ge(S_["sp_d"], 2 * NT * e)
                    dv = BFA[:, WD_o:WD_o + 8192].rearrange("p (c m) -> p c m", c=8)
                    g.dma_start(dv, dram["down_proj"].ap()[e].rearrange("(c p) m -> p c m", p=128)).then_inc(S_["sw_d"], 16)
                    g.dma_start(DBROW[0:1, :], dram["down_bias"].ap()[e].unsqueeze(0)).then_inc(S_["sw_d"], 16)

            @blk.tensor
            def _(t):
                for e in range(NE):
                    t.wait_ge(S_["sw_g"], 16 * (e + 1))
                    t.wait_ge(S_["sv_devac"], 2 * NT * e)
                    for ic in range(8):
                        idx = e * 8 + ic
                        t.wait_ge(S_["sv_gt"], n_tc * idx)
                        for dc in range(8):
                            for tc in range(n_tc):
                                ins = t.matmul(P_G[:, tc * 512:(tc + 1) * 512],
                                               BFA[:, WG_o + dc * 1024 + ic * 128:
                                                   WG_o + dc * 1024 + (ic + 1) * 128],
                                               XF2c(dc, tc * 512, (tc + 1) * 512),
                                               start=(dc == 0), stop=(dc == 7))
                        ins.then_inc(S_["sp_g"], 1)
                        t.wait_ge(S_["sw_u"], 16 * (e + 1))
                        t.wait_ge(S_["sv_up"], n_tc * idx)
                        for dc in range(8):
                            for tc in range(n_tc):
                                ins = t.matmul(P_U[:, tc * 512:(tc + 1) * 512],
                                               BFA[:, WU_o + dc * 1024 + ic * 128:
                                                   WU_o + dc * 1024 + (ic + 1) * 128],
                                               XF2c(dc, tc * 512, (tc + 1) * 512),
                                               start=(dc == 0), stop=(dc == 7))
                        ins.then_inc(S_["sp_u"], 1)
                    t.wait_ge(S_["sv_gated"], 8 * n_tc * (e + 1))
                    t.wait_ge(S_["sw_d"], 32 * (e + 1))
                    for tt in range(NT):
                        t.wait_ge(S_["sv_devac"], 2 * (e * NT + tt))
                        for mh in range(2):
                            t.matmul(P_G[:, mh * 512:(mh + 1) * 512], ONESC[0:1, :],
                                     DBROW[0:1, mh * 512:(mh + 1) * 512],
                                     start=True, stop=False)
                            for ic in range(8):
                                ins = t.matmul(P_G[:, mh * 512:(mh + 1) * 512],
                                               GATED(ic, tt * 128, (tt + 1) * 128),
                                               BFA[:, WD_o + ic * 1024 + mh * 512:
                                                   WD_o + ic * 1024 + (mh + 1) * 512],
                                               start=False, stop=(ic == 7))
                            ins.then_inc(S_["sp_d"], 1)

            @blk.vector
            def _(v):
                for e in range(NE):
                    for ic in range(8):
                        idx = e * 8 + ic
                        for tc in range(n_tc):
                            r = (idx * n_tc + tc) % 2
                            v.wait_ge(S_["sp_g"], idx + 1)
                            if e >= 1:
                                v.wait_ge(S_["sp_d"], 2 * NT * e)
                            gb = GUPB[:, ic * 8 + e:ic * 8 + e + 1]
                            ub = GUPB[:, 64 + ic * 8 + e:64 + ic * 8 + e + 1]
                            v.tensor_scalar(GTS[:, r * 512:(r + 1) * 512],
                                            P_G[:, tc * 512:(tc + 1) * 512],
                                            gb, LIMIT, ALU.add,
                                            ALU.min).then_inc(S_["sv_gt"], 1)
                            v.wait_ge(S_["sp_u"], idx + 1)
                            vedge(v, v.tensor_scalar(UP2S[:, r * 512:(r + 1) * 512],
                                                     P_U[:, tc * 512:(tc + 1) * 512],
                                                     ub, -LIMIT, ALU.add, ALU.max))
                            v.tensor_scalar(UP2S[:, r * 512:(r + 1) * 512],
                                            UP2S[:, r * 512:(r + 1) * 512],
                                            LIMIT, 1.0, ALU.min,
                                            ALU.add).then_inc(S_["sv_up"], 1)
                            v.wait_ge(S_["ss_glu"], idx * n_tc + tc + 1)
                            v.wait_ge(S_["sv_up"], idx * n_tc + tc + 1)
                            v.tensor_tensor(GATED(ic, tc * 512, (tc + 1) * 512),
                                            GLUS[:, r * 512:(r + 1) * 512],
                                            UP2S[:, r * 512:(r + 1) * 512],
                                            ALU.mult).then_inc(S_["sv_gated"], 1)
                    for tt in range(NT):
                        for mh in range(2):
                            v.wait_ge(S_["sp_d"], (e * NT + tt) * 2 + mh + 1)
                            v.scalar_tensor_tensor(
                                A[:, tt * 1024 + mh * 512:tt * 1024 + (mh + 1) * 512],
                                P_G[:, mh * 512:(mh + 1) * 512],
                                SCORES[:, tt * 8 + e:tt * 8 + e + 1],
                                A[:, tt * 1024 + mh * 512:tt * 1024 + (mh + 1) * 512],
                                ALU.mult, ALU.add).then_inc(S_["sv_devac"], 1)

            @blk.scalar
            def _(s):
                for e in range(NE):
                    for ic in range(8):
                        idx = e * 8 + ic
                        for tc in range(n_tc):
                            r = (idx * n_tc + tc) % 2
                            s.wait_ge(S_["sv_gt"], idx * n_tc + tc + 1)
                            s.activation(GLUS[:, r * 512:(r + 1) * 512],
                                         GTS[:, r * 512:(r + 1) * 512],
                                         AF.Gelu_apprx_sigmoid).then_inc(S_["ss_glu"], 1)

    with contextlib.ExitStack() as ps:
        P_TR4 = ps.enter_context(nc.psum_tensor("P_TR4", [128, 1024], BF16))
        P_RT4 = ps.enter_context(nc.psum_tensor("P_RT4", [128, 8], F32))
        with nc.Block("b4") as blk:
            @blk.sync
            def _(sy):
                if debug:
                    for i in range(NT):
                        sy.dma_start(dbg["dbg_A3"].ap()[i * 128:(i + 1) * 128, :],
                                     A[:, i * 1024:(i + 1) * 1024]).then_inc(S_["s_out"], 16)
                    sy.dma_start(dbg["dbg_SC"].ap(), SCORES[:, :]).then_inc(S_["s_out"], 16)
                    sy.wait_ge(S_["s_out"], 16 * (2 * NT + 2))
                for i in range(NT):
                    if i >= 3:
                        sy.wait_ge(S_["b4_xhat"], i - 2)
                    sy.dma_start(XR[:, (i % 3) * 1024:(i % 3 + 1) * 1024],
                                 x_in.ap()[i * 128:(i + 1) * 128, :]).then_inc(S_["b4x%d" % (i % 3)], 16)

            @blk.vector
            def _(v):
                for i in range(NT):
                    xr = XR[:, (i % 3) * 1024:(i % 3 + 1) * 1024]
                    at = A[:, i * 1024:(i + 1) * 1024]
                    v.wait_ge(S_["b4x%d" % (i % 3)], 16 * (i // 3 + 1))
                    vedge(v, v.tensor_tensor(xr, xr, at, ALU.add))
                    vedge(v, v.tensor_scalar(DUMP2[:, :], xr, 1.0, None, ALU.mult,
                                             ALU.add, accum_out=STAT[:, 0:1]))
                    vedge(v, v.scalar_tensor_tensor(DUMP2[:, :], xr, 1.0, xr,
                                                    ALU.bypass, ALU.mult,
                                                    accum_out=STAT[:, 1:2]))
                    vedge(v, v.tensor_scalar(STAT[:, 2:3], STAT[:, 0:1], -1.0 / D,
                                             None, ALU.mult))
                    vedge(v, v.tensor_tensor(STAT[:, 3:4], STAT[:, 2:3], STAT[:, 2:3],
                                             ALU.mult))
                    v.scalar_tensor_tensor(STAT[:, 4:5], STAT[:, 1:2], 1.0 / D,
                                           STAT[:, 3:4], ALU.mult,
                                           ALU.subtract).then_inc(S_["b4_eps"], 1)
                    v.wait_ge(S_["b4_rstd"], i + 1)
                    if i >= 2:
                        v.wait_ge(S_["b4_trx"], i - 1)
                    vedge(v, v.tensor_scalar(STAT[:, 7:8], STAT[:, 6:7], 2.0, None,
                                             ALU.mult))
                    vedge(v, v.tensor_scalar(XH2(i % 2), xr, STAT[:, 2:3],
                                             STAT[:, 6:7], ALU.add, ALU.mult))
                    v.tensor_scalar(at, xr, STAT[:, 2:3], STAT[:, 7:8],
                                    ALU.add, ALU.mult).then_inc(S_["b4_xhat"], 1)
                    v.wait_ge(S_["b4_rt"], i + 1)
                    v.tensor_copy(LSC[:, 0:8], P_RT4[:, :]).then_inc(S_["b4_sc"], 1)
                    v.wait_ge(S_["b4_sc"], i + 1)
                    vedge(v, v.tensor_tensor(LSC[:, 8:16], LSC[:, 0:8], RBB[:, :],
                                             ALU.add))
                    vedge(v, v.reduce_max(STAT[:, 10:11], LSC[:, 8:16], axis=AX.X,
                                          negate=True))
                    vedge(v, v.tensor_scalar(STAT[:, 11:12], STAT[:, 10:11], -1.0,
                                             None, ALU.mult))
                    vedge(v, v.tensor_scalar(LSC[:, 16:24], LSC[:, 8:16],
                                             STAT[:, 11:12], None, ALU.is_ge))
                    vedge(v, v.scalar_tensor_tensor(LSC[:, 24:32], LSC[:, 16:24],
                                                    -BIG, LSC[:, 8:16],
                                                    ALU.mult, ALU.add))
                    vedge(v, v.reduce_max(STAT[:, 12:13], LSC[:, 24:32], axis=AX.X,
                                          negate=True))
                    vedge(v, v.tensor_scalar(STAT[:, 13:14], STAT[:, 12:13], -1.0,
                                             None, ALU.mult))
                    vedge(v, v.tensor_scalar(LSC[:, 16:24], LSC[:, 8:16],
                                             STAT[:, 13:14], None, ALU.is_ge))
                    v.wait_ge(S_["b4_sig"], i + 1)
                    vedge(v, v.scalar_tensor_tensor(LSC[:, 24:32], LSC[:, 16:24], BIG,
                                                    STAT[:, 16:24], ALU.mult, ALU.add))
                    vedge(v, v.tensor_scalar(LSC[:, 24:32], LSC[:, 24:32], -BIG, None,
                                             ALU.add))
                    vedge(v, v.reduce_max(STAT[:, 14:15], LSC[:, 24:32], axis=AX.X,
                                          negate=True))
                    v.engine_nop().then_inc(S_["b4_rsel"], 1)
                    v.wait_ge(S_["b4_exp"], i + 1)
                    vedge(v, v.reciprocal(STAT[:, 25:26], STAT[:, 15:16]))
                    v.tensor_scalar(COMB[:, i * 8:(i + 1) * 8], LSC[:, 16:24],
                                    STAT[:, 25:26], None, ALU.mult)

            @blk.scalar
            def _(s):
                for i in range(NT):
                    s.wait_ge(S_["b4_eps"], i + 1)
                    sedge(s, s.activation(STAT[:, 5:6], STAT[:, 4:5], AF.Ln,
                                          bias=EPSC[:, 0:1]))
                    s.activation(STAT[:, 6:7], STAT[:, 5:6], AF.Exp,
                                 scale=-0.5).then_inc(S_["b4_rstd"], 1)
                    s.wait_ge(S_["b4_trx"], i + 1)
                    for dc in range(8):
                        ins = s.copy(XF2t(dc, i), P_TR4[:, dc * 128:(dc + 1) * 128])
                    ins.then_inc(S_["b4_cpx"], 1)
                    s.wait_ge(S_["b4_sc"], i + 1)
                    s.activation(STAT[:, 16:24], LSC[:, 0:8],
                                 AF.Sigmoid).then_inc(S_["b4_sig"], 1)
                    s.wait_ge(S_["b4_rsel"], i + 1)
                    s.activation(LSC[:, 16:24], LSC[:, 24:32], AF.Exp,
                                 bias=STAT[:, 14:15],
                                 accum_out=STAT[:, 15:16]).then_inc(S_["b4_exp"], 1)

            @blk.tensor
            def _(t):
                t.wait_ge(S_["s_cent"], 1)
                for i in range(NT):
                    t.wait_ge(S_["b4_xhat"], i + 1)
                    if i >= 1:
                        t.wait_ge(S_["b4_cpx"], i)
                    for dc in range(8):
                        ins = t.transpose(P_TR4[:, dc * 128:(dc + 1) * 128],
                                          XH2(i % 2)[:, dc * 128:(dc + 1) * 128],
                                          IDENTB[:, :])
                    ins.then_inc(S_["b4_trx"], 1)
                    t.wait_ge(S_["b4_cpx"], i + 1)
                    t.wait_ge(S_["b4_sc"], i)
                    for dc in range(8):
                        ins = t.matmul(P_RT4[:, :], XF2t(dc, i),
                                       CENTT[:, dc * 8:(dc + 1) * 8],
                                       start=(dc == 0), stop=(dc == 7))
                    ins.then_inc(S_["b4_rt"], 1)

    with contextlib.ExitStack() as ps:
        P_BIG = ps.enter_context(nc.psum_tensor("P_BIG", [128, 4096], F32))
        with nc.Block("b5") as blk:
            @blk.scalar
            def _(s):
                ntot = 16 * NPASS + 32 * NPASS
                for r_ in range(3):
                    s.wait_ge(S_["s_wb" + "abc"[r_]], 16 * len([k for k in range(ntot) if k % 3 == r_]))
                glist = [(p, half, fsb) for p in range(NPASS)
                         for half in range(n_half) for fsb in range(16)]

                def issue_load(g):
                    p_, half_, fsb_ = glist[g]
                    s.dma_start(WFCs(g % 2)[:, :].rearrange("p (c f) -> p c f", c=8),
                                st_fcT.ap()[p_][fsb_]).then_inc(S_["swf%d" % (g % 2)], 16)

                issue_load(0)
                issue_load(1)
                for g in range(len(glist)):
                    if g + 2 < len(glist):
                        s.wait_ge(S_["sp_fc"], g + 1)
                        issue_load(g + 2)
                    s.wait_ge(S_["sp_fc"], g + 1)
                    for fsub in range(2):
                        for tc in range(Tq // 512):
                            bank = (g % 2) * 4 + fsub * 2 + tc
                            s.activation(HR((glist[g][2]) * 2 + fsub, tc * 512,
                                            (tc + 1) * 512),
                                         P_BIG[:, bank * 512:(bank + 1) * 512],
                                         AF.Gelu).then_inc(S_["ss_hr"], 1)

            @blk.sync
            def _(sy):
                if debug:
                    for i in range(NT):
                        sy.dma_start(dbg["dbg_A4"].ap()[i * 128:(i + 1) * 128, :],
                                     A[:, i * 1024:(i + 1) * 1024]).then_inc(S_["s_out"], 16)
                    sy.dma_start(dbg["dbg_CB"].ap(), COMB[:, :]).then_inc(S_["s_out"], 16)
                    sy.wait_ge(S_["s_out"], 16 * (3 * NT + 3))
                ntot = 16 * NPASS + 32 * NPASS
                for r_ in range(3):
                    sy.wait_ge(S_["s_wb" + "abc"[r_]], 16 * len([k for k in range(ntot) if k % 3 == r_]))
                for p in range(NPASS):
                    for half in range(n_half):
                        gh = p * n_half + half
                        for sq in range(Tq // 512):
                            for fc in range(32):
                                pfc = (gh * (Tq // 512) + sq) * 32 + fc
                                if pfc >= 4:
                                    sy.wait_ge(S_["sp_pj"], pfc - 3)
                                sy.dma_start(WPJs(pfc % 4),
                                             st_pjT.ap()[p][fc]).then_inc(S_["swp%d" % (pfc % 4)], 16)

            @blk.tensor
            def _(t):
                for p in range(NPASS):
                    for half in range(n_half):
                        gh = p * n_half + half
                        if gh >= 1:
                            t.wait_ge(S_["sp_pj"], gh * (Tq // 512) * 32)
                        for fsb in range(16):
                            gfsb = gh * 16 + fsb
                            t.wait_ge(S_["swf%d" % (gfsb % 2)], 16 * (gfsb // 2 + 1))
                            if gfsb >= 2:
                                t.wait_ge(S_["ss_hr"], 4 * (gfsb - 1))
                            if gh >= 1 and fsb < 2:
                                t.wait_ge(S_["sv_pevac"], (gh - 1 + 1) * (Tq // 512) * 8)
                            bq_ = (gfsb % 2) * 4
                            for fsub in range(2):
                                for dc in range(8):
                                    for tc in range(Tq // 512):
                                        bank = bq_ + fsub * 2 + tc
                                        ins = t.matmul(
                                            P_BIG[:, bank * 512:(bank + 1) * 512],
                                            WFCs(gfsb % 2)[:, dc * 256 + fsub * 128:
                                                           dc * 256 + (fsub + 1) * 128],
                                            XF2c(dc, half * Tq + tc * 512,
                                                 half * Tq + (tc + 1) * 512),
                                            start=(dc == 0), stop=(dc == 7))
                            ins.then_inc(S_["sp_fc"], 1)
                        t.wait_ge(S_["ss_hr"], 64 * (gh + 1))
                        for sq in range(Tq // 512):
                            gsq = gh * (Tq // 512) + sq
                            for fc in range(32):
                                pfc = gsq * 32 + fc
                                t.wait_ge(S_["swp%d" % (pfc % 4)], 16 * (pfc // 4 + 1))
                                for tt in range(4):
                                    for mh in range(2):
                                        bank = tt * 2 + mh
                                        if fc == 0 and gsq >= 1:
                                            t.wait_ge(S_["sv_pevac"],
                                                      8 * (gsq - 1) + bank + 1)
                                        ins = t.matmul(
                                            P_BIG[:, bank * 512:(bank + 1) * 512],
                                            HR(fc, sq * 512 + tt * 128,
                                               sq * 512 + (tt + 1) * 128),
                                            WPJs(pfc % 4)[:, mh * 512:(mh + 1) * 512],
                                            start=(fc == 0), stop=(fc == 31))
                                ins.then_inc(S_["sp_pj"], 1)


            @blk.vector
            def _(v):
                for p in range(NPASS):
                    for half in range(n_half):
                        gh = p * n_half + half
                        for sq in range(Tq // 512):
                            gsq = gh * (Tq // 512) + sq
                            v.wait_ge(S_["sp_pj"], (gsq + 1) * 32)
                            for tt in range(4):
                                tglob = half * ntt_q + sq * 4 + tt
                                for mh in range(2):
                                    bank = tt * 2 + mh
                                    av = A[:, tglob * 1024 + mh * 512:
                                           tglob * 1024 + (mh + 1) * 512]
                                    if p < NS:
                                        ins = v.tensor_tensor(
                                            av, P_BIG[:, bank * 512:(bank + 1) * 512],
                                            av, ALU.add)
                                    else:
                                        ins = v.scalar_tensor_tensor(
                                            av, P_BIG[:, bank * 512:(bank + 1) * 512],
                                            COMB[:, tglob * 8 + (p - NS):
                                                 tglob * 8 + (p - NS) + 1],
                                            av, ALU.mult, ALU.add)
                                    ins.then_inc(S_["sv_pevac"], 1)

    with nc.Block("b6") as blk:
        @blk.vector
        def _(v):
            for i in range(NT):
                at = A[:, i * 1024:(i + 1) * 1024]
                if i >= 3:
                    v.wait_ge(S_["so%d" % (i % 3)], 16 * (i // 3))
                vedge(v, v.tensor_scalar(DUMP2[:, :], at, 1.0, None, ALU.mult,
                                         ALU.add, accum_out=STAT[:, 0:1]))
                vedge(v, v.scalar_tensor_tensor(DUMP2[:, :], at, 1.0, at,
                                                ALU.bypass, ALU.mult,
                                                accum_out=STAT[:, 1:2]))
                vedge(v, v.tensor_scalar(STAT[:, 2:3], STAT[:, 0:1], -1.0 / D,
                                         None, ALU.mult))
                vedge(v, v.tensor_tensor(STAT[:, 3:4], STAT[:, 2:3], STAT[:, 2:3],
                                         ALU.mult))
                v.scalar_tensor_tensor(STAT[:, 4:5], STAT[:, 1:2], 1.0 / D,
                                       STAT[:, 3:4], ALU.mult,
                                       ALU.subtract).then_inc(S_["b6_eps"], 1)
                v.wait_ge(S_["b6_rstd"], i + 1)
                v.tensor_scalar(XR[:, (i % 3) * 1024:(i % 3 + 1) * 1024], at,
                                STAT[:, 2:3], STAT[:, 6:7], ALU.add,
                                ALU.mult).then_inc(S_["b6_out"], 1)

        @blk.scalar
        def _(s):
            for i in range(NT):
                s.wait_ge(S_["b6_eps"], i + 1)
                sedge(s, s.activation(STAT[:, 5:6], STAT[:, 4:5], AF.Ln, bias=EPSC[:, 0:1]))
                s.activation(STAT[:, 6:7], STAT[:, 5:6], AF.Exp,
                             scale=-0.5).then_inc(S_["b6_rstd"], 1)

        @blk.sync
        def _(sy):
            for i in range(NT):
                sy.wait_ge(S_["b6_out"], i + 1)
                sy.dma_start(out_t.ap()[i * 128:(i + 1) * 128, :],
                             XR[:, (i % 3) * 1024:(i % 3 + 1) * 1024]).then_inc(S_["so%d" % (i % 3)], 16)
            for r in range(3):
                n_r = len([i for i in range(NT) if i % 3 == r])
                sy.wait_ge(S_["so%d" % r], 16 * n_r)

    return nc


def make_consts(T):
    pos = np.arange(T, dtype=np.float64)
    inv = 1.0 / (150000.0 ** (np.arange(32, dtype=np.float64) / 32.0))
    ang = pos[:, None] * inv[None, :]
    cosw = (np.cos(ang) * SCALE).astype(np.float32)
    sinw = (np.sin(ang) * SCALE).astype(np.float32)
    ql = np.arange(128)
    kl = np.arange(384)
    band = (kl[None, :] >= ql[:, None]) & (kl[None, :] <= ql[:, None] + 2 * SW)
    m_int = np.where(band, 0.0, -BIG).astype(np.float32)
    m_first = np.where(kl[None, :] <= ql[:, None] + SW, 0.0, -BIG).astype(np.float32)
    m_first[:, 256:384] = -BIG
    m_last = m_int.copy(); m_last[:, 256:384] = -BIG
    masks = np.stack([m_int, m_first, m_last]).astype(np.float32)
    ident = np.eye(128, dtype=np.float32)
    return cosw, sinw, masks, ident


_NC_CACHE = {}


def kernel(**inputs):
    NT = 16
    T = NT * 128
    if NT not in _NC_CACHE:
        _NC_CACHE[NT] = build_nc(NT)
    nc = _NC_CACHE[NT]
    cosw, sinw, masks, ident = make_consts(T)
    x = np.ascontiguousarray(inputs["L0_attn_hidden_states"])
    base = {
        "attn_norm_scale": inputs["L0_attn_attn_norm_scale"],
        "Wq": np.ascontiguousarray(inputs["L0_attn_Wq"].reshape(D, NQ * HD)),
        "bq": np.ascontiguousarray(inputs["L0_attn_bq"].reshape(NQ * HD)),
        "Wk": np.ascontiguousarray(inputs["L0_attn_Wk"].reshape(D, NKV * HD)),
        "bk": np.ascontiguousarray(inputs["L0_attn_bk"].reshape(NKV * HD)),
        "Wv": np.ascontiguousarray(inputs["L0_attn_Wv"].reshape(D, NKV * HD)),
        "bv": np.ascontiguousarray(inputs["L0_attn_bv"].reshape(NKV * HD)),
        "Wo": np.ascontiguousarray(inputs["L0_attn_Wo"]),
        "bo": inputs["L0_attn_bo"],
        "sinks": inputs["L0_attn_sinks"],
        "moe_norm_scale": inputs["L0_attn_moe_norm_scale"],
        "router_W": np.ascontiguousarray(inputs["L0_attn_router_W"]),
        "router_b": inputs["L0_attn_router_b"],
        "gate_up_proj": np.ascontiguousarray(inputs["L0_attn_gate_up_proj"]),
        "gate_up_bias": np.ascontiguousarray(inputs["L0_attn_gate_up_bias"]),
        "down_proj": np.ascontiguousarray(inputs["L0_attn_down_proj"]),
        "down_bias": np.ascontiguousarray(inputs["L0_attn_down_bias"]),
        "centroids": np.ascontiguousarray(inputs["L0_ffn_centroids"]),
        "routing_bias": inputs["L0_ffn_routing_bias"],
        "Wfc_r": np.ascontiguousarray(inputs["L0_ffn_Wfc_r"]),
        "Wproj_r": np.ascontiguousarray(inputs["L0_ffn_Wproj_r"]),
        "Wfc_sh": np.ascontiguousarray(inputs["L0_ffn_Wfc_sh"]),
        "Wproj_sh": np.ascontiguousarray(inputs["L0_ffn_Wproj_sh"]),
        "COS": cosw, "SIN": sinw, "MASKS": masks, "IDENT": ident,
    }
    base = {k: np.ascontiguousarray(np.asarray(v, dtype=np.float32))
            for k, v in base.items()}
    in_maps = []
    for c in range(8):
        m = dict(base)
        m["x_in"] = np.ascontiguousarray(x[c])
        in_maps.append(m)
    res = run_bass_kernel_spmd(nc, in_maps, core_ids=list(range(8)))
    return np.stack([r["out"] for r in res.results], axis=0)


# version 09
# speedup vs baseline: 1.1411x; 1.1411x over previous
import contextlib
import numpy as np
import concourse.bass as bass
import concourse.mybir as mybir
from concourse.bass_utils import run_bass_kernel_spmd

F32 = mybir.dt.float32
BF16 = mybir.dt.bfloat16
AF = mybir.ActivationFunctionType
ALU = mybir.AluOpType
AX = mybir.AxisListType

D = 1024
NQ, NKV, HD, SW = 16, 4, 64, 128
INTER = 1024
NE = 8
FF = 4096
NS = 2
EPS = 1e-5
LIMIT = 7.0
BIG = 1e9
SCALE = HD ** (-0.25)
NPASS = NS + NE


def head_of(pb, half):
    g2, j4 = divmod(pb, 4)
    return (2 * g2 + half) * 4 + j4


def build_nc(NT=16, debug=False):
    T = NT * 128
    Tq = 1024 if T >= 2048 else T
    n_half = T // Tq
    ntt_q = Tq // 128
    n_tc = T // 512
    nc = bass.Bass("TRN2", target_bir_lowering=False,
                   dynamic_dma_scratch_size=16384)
    dram = {}

    def din(name, shape):
        dram[name] = nc.dram_tensor(name, list(shape), F32, kind="ExternalInput")
        return dram[name]

    x_in = din("x_in", [T, D])
    din("attn_norm_scale", [D])
    din("Wq", [D, NQ * HD]); din("bq", [NQ * HD])
    din("Wk", [D, NKV * HD]); din("bk", [NKV * HD])
    din("Wv", [D, NKV * HD]); din("bv", [NKV * HD])
    din("Wo", [NQ, HD, D]); din("bo", [D])
    din("sinks", [NQ])
    din("moe_norm_scale", [D])
    din("router_W", [D, NE]); din("router_b", [NE])
    din("gate_up_proj", [NE, D, 2 * INTER]); din("gate_up_bias", [NE, 2 * INTER])
    din("down_proj", [NE, INTER, D]); din("down_bias", [NE, D])
    din("centroids", [NE, D]); din("routing_bias", [NE])
    din("Wfc_r", [NE, FF, D]); din("Wproj_r", [NE, D, FF])
    din("Wfc_sh", [NS, FF, D]); din("Wproj_sh", [NS, D, FF])
    din("COS", [T, 32]); din("SIN", [T, 32])
    din("MASKS", [3, 128, 384])
    din("IDENT", [128, 128])

    out_t = nc.dram_tensor("out", [T, D], F32, kind="ExternalOutput")
    dbg = {}
    if debug:
        for n in ["dbg_A1", "dbg_A3", "dbg_A4"]:
            dbg[n] = nc.dram_tensor(n, [T, D], F32, kind="ExternalOutput")
        dbg["dbg_SC"] = nc.dram_tensor("dbg_SC", [128, NT * 8], F32, kind="ExternalOutput")
        dbg["dbg_KFM"] = nc.dram_tensor("dbg_KFM", [128, 2 * T], F32, kind="ExternalOutput")
        dbg["dbg_VT"] = nc.dram_tensor("dbg_VT", [128, NT * 256], F32, kind="ExternalOutput")
        dbg["dbg_SM"] = nc.dram_tensor("dbg_SM", [128, 768], F32, kind="ExternalOutput")
        dbg["dbg_PB"] = nc.dram_tensor("dbg_PB", [128, 768], F32, kind="ExternalOutput")
        dbg["dbg_OF"] = nc.dram_tensor("dbg_OF", [128, 1024], F32, kind="ExternalOutput")
        dbg["dbg_CB"] = nc.dram_tensor("dbg_CB", [128, NT * 8], F32, kind="ExternalOutput")
    st_fc = nc.dram_tensor("st_fc", [NPASS, FF, D], BF16, kind="Internal")
    st_pj = nc.dram_tensor("st_pj", [NPASS, D, FF], BF16, kind="Internal")
    st_fcT = nc.dram_tensor("st_fcT", [NPASS, 16, 128, 8, 256], BF16, kind="Internal")
    st_pjT = nc.dram_tensor("st_pjT", [NPASS, 32, 128, 1024], BF16, kind="Internal")

    SB = nc.alloc_sbuf_tensor
    A = SB("A", [128, NT * 1024], F32)
    BFAC = 57344
    BFA = SB("BFA", [128, BFAC], BF16)

    KFM_o = 0
    VTOK_o = 2 * T
    QROPE_o = VTOK_o + NT * 256
    KROPE_o = QROPE_o + 1024
    QR_o = KROPE_o + 256
    OFMS_o = QR_o + 2048
    XHATS_o = OFMS_o + 1024
    XF1_o = XHATS_o + 2048
    DUMP_o = XF1_o + 2048
    PTS_o = DUMP_o + 1024
    WQKV_off = max(16 * T, PTS_o + 768)
    WO_off = WQKV_off + 12288
    assert WO_off + 8192 <= BFAC
    XF_o = 0
    GATED_o = 8 * T
    WG_o, WU_o, WD_o = WQKV_off, WQKV_off + 8192, WQKV_off + 16384
    assert WD_o + 8192 <= BFAC
    SC2_o = GATED_o
    HR_o = 8 * T
    WFC_o = HR_o + 32 * Tq
    WPJ_o = WFC_o + 4096
    assert WPJ_o + 4096 <= BFAC

    XR = SB("XR", [128, 3 * 1024], F32)
    if WQKV_off - (PTS_o + 768) >= 7680:
        B1S_o = PTS_o + 768
    else:
        B1S_o = WD_o + 8192
        assert BFAC - B1S_o >= 7680
    MASKS = BFA[:, B1S_o:B1S_o + 2304].bitcast(F32)
    COSB = BFA[:, B1S_o + 2304:B1S_o + 2304 + NT * 32]
    SINB = BFA[:, B1S_o + 2816:B1S_o + 2816 + NT * 32]
    SM = BFA[:, B1S_o + 3328:B1S_o + 4864].bitcast(F32)
    PBUF = BFA[:, B1S_o + 4864:B1S_o + 5632]
    G2T = BFA[:, B1S_o + 5632:B1S_o + 6656]
    assert B1S_o + 6656 <= BFAC
    IDENTB = SB("IDENTB", [128, 128], BF16)
    IDENTF = SB("IDENTF", [128, 128], F32)
    NEGSINK = SB("NEGSINK", [128, 16], F32)
    SINKB = SB("SINKB", [128, 16], F32)
    GUPB = SB("GUPB", [128, 128], F32)
    RBB = SB("RBB", [128, 8], F32)
    SCORES = SB("SCORES", [128, NT * 8], F32)
    COMB = SB("COMB", [128, NT * 8], F32)
    STAT = SB("STAT", [128, 48], F32)
    LSC = SB("LSC", [128, 32], F32)
    GTS = SB("GTS", [128, 2 * 512], BF16)
    GLUS = SB("GLUS", [128, 2 * 512], BF16)
    UP2S = SB("UP2S", [128, 2 * 512], BF16)
    ROWS = SB("ROWS", [1, 1544], BF16)
    BOROW = SB("BOROW", [1, 1024], BF16)
    DBROW = SB("DBROW", [1, 1024], BF16)
    SROW = SB("SROW", [1, 64], F32)
    G1C = SB("G1C", [128, 8], F32)
    ONESC = SB("ONESC", [1, 128], BF16)
    ONESCF = SB("ONESCF", [1, 128], F32)
    CENTT = SB("CENTT", [128, 64], BF16)
    ROUTW = SB("ROUTW", [128, 64], BF16)
    EPSC = SB("EPSC", [128, 1], F32)
    G2ROW = XR[32:33, 0:1024]
    SROW = XR[64:65, 0:64]
    CENTN = XR[0:8, 2048:2560].bitcast(BF16)
    ONESF32 = XR[32:33, 1024:1152]
    ONESF64 = XR[64:65, 64:192]

    names = ["sv_eps", "ss_rstd", "sv_xhat", "sp_trx", "ss_cpx", "sp_qkv", "sv_ropek", "sv_ropeq", "sv_ropeall", "sp_trq", "ss_cpq", "sp_trk", "ss_cpk", "sp_s", "sv_mask", "sv_negm", "sp_pt", "ss_pt", "ss_exp", "sv_pn", "sp_av", "ss_ofm", "sp_wo", "sv_evac", "s_wsmall", "s_const", "s_constg", "s_b0", "ev", "es", "b2_eps", "b2_rstd", "b2_xhat", "b2_trx", "b2_cpx", "b2_rt", "b2_sc", "b2_lsel", "b2_exp", "sw_g", "sw_u", "sw_d", "sp_g", "sp_u", "sv_gt", "sv_up", "ss_glu", "sv_gated", "sp_d", "sv_devac", "b4_eps", "b4_rstd", "b4_xhat", "b4_rt", "b4_sc", "b4_sig", "b4_rsel", "b4_exp", "sp_fc", "ss_hr", "sp_pj", "sv_pevac", "st20", "st21", "st22", "st23", "st24", "st25", "swb0", "swb1", "swb2", "swb3", "swb4", "swb5", "s_out", "s_dbg", "swp3", "sx0", "sx1", "sx2", "b4x0", "b4x1", "b4x2", "so0", "so1", "so2", "swf0", "swf1", "swp0", "swp1", "swp2", "s_stg0", "s_stg1", "s_stg2", "s_stg3", "s_stg4", "s_stg5", "s_stg6", "s_stg7", "s_stg8", "s_stg9"]
    S_ = {n: nc.alloc_semaphore(n) for n in names}

    ecnt = {"v": [0], "s": [0]}

    def vedge(v, ins):
        ecnt["v"][0] += 1
        ins.then_inc(S_["ev"], 1)
        v.wait_ge(S_["ev"], ecnt["v"][0])

    def sedge(s, ins):
        ecnt["s"][0] += 1
        ins.then_inc(S_["es"], 1)
        s.wait_ge(S_["es"], ecnt["s"][0])

    def mask_ap(j):
        if j == 0:
            return MASKS[:, 384:768]
        if j == NT - 1:
            return MASKS[:, 768:1152]
        return MASKS[:, 0:384]

    XHATS = lambda r: BFA[:, XHATS_o + r * 1024:XHATS_o + (r + 1) * 1024]
    XF1 = lambda r, dc: BFA[:, XF1_o + r * 1024 + dc * 128:XF1_o + r * 1024 + (dc + 1) * 128]
    QRS = lambda r, pb: BFA[:, QR_o + (pb * 2 + r) * 128:QR_o + (pb * 2 + r + 1) * 128]
    OFMS = lambda pb: BFA[:, OFMS_o + pb * 128:OFMS_o + (pb + 1) * 128]
    QROPE = BFA[:, QROPE_o:QROPE_o + 1024]
    KROPE = BFA[:, KROPE_o:KROPE_o + 256]
    DUMP = BFA[:, DUMP_o:DUMP_o + 1024]
    PTS = lambda r: BFA[:, PTS_o + r * 384:PTS_o + (r + 1) * 384]
    XH2 = lambda r: BFA[:, SC2_o + r * 1024:SC2_o + (r + 1) * 1024]
    DUMP2 = BFA[:, SC2_o + 2048:SC2_o + 3072]
    XF2t = lambda dc, i: BFA[:, XF_o + dc * T + i * 128:XF_o + dc * T + (i + 1) * 128]
    XF2c = lambda dc, c0, c1: BFA[:, XF_o + dc * T + c0:XF_o + dc * T + c1]
    GATED = lambda ic, c0, c1: BFA[:, GATED_o + ic * T + c0:GATED_o + ic * T + c1]
    HR = lambda fc, c0, c1: BFA[:, HR_o + fc * Tq + c0:HR_o + fc * Tq + c1]
    WFCs = lambda r: BFA[:, WFC_o + r * 2048:WFC_o + (r + 1) * 2048]
    WPJs = lambda r: BFA[:, WPJ_o + r * 1024:WPJ_o + (r + 1) * 1024]

    N_CONST = 16 * 14
    N_CONSTG = 16 * 3
    N_WSMALL = 16 * 26

    with nc.Block("b0") as blk:
        @blk.sync
        def _(sy):
            sy.dma_start(MASKS.rearrange("p (m c) -> p m c", m=3),
                         dram["MASKS"].ap().rearrange("m p c -> p m c")).then_inc(S_["s_const"], 16)
            sy.dma_start(SROW[0:1, 0:16], dram["sinks"].ap().unsqueeze(0)).then_inc(S_["s_const"], 16)
            sy.dma_start(SROW[0:1, 16:24], dram["routing_bias"].ap().unsqueeze(0)).then_inc(S_["s_const"], 16)
            sy.dma_start(G2ROW, dram["moe_norm_scale"].ap().unsqueeze(0)).then_inc(S_["s_const"], 16)
            sy.dma_start(IDENTF[:, :], dram["IDENT"].ap()).then_inc(S_["s_const"], 16)
            for c in range(8):
                sy.dma_start(G1C[:, c:c + 1],
                             dram["attn_norm_scale"].ap()[c * 128:(c + 1) * 128].unsqueeze(1)).then_inc(S_["s_const"], 16)
            sy.dma_start(XR[0:8, 0:2048], dram["gate_up_bias"].ap()).then_inc(S_["s_const"], 16)

        @blk.gpsimd
        def _(g):
            g.dma_start(IDENTB[:, :], dram["IDENT"].ap()).then_inc(S_["s_constg"], 16)
            g.dma_start(COSB[:, :].rearrange("p (n j) -> p n j", j=32),
                        dram["COS"].ap().rearrange("(n p) j -> p n j", p=128)).then_inc(S_["s_constg"], 16)
            g.dma_start(SINB[:, :].rearrange("p (n j) -> p n j", j=32),
                        dram["SIN"].ap().rearrange("(n p) j -> p n j", p=128)).then_inc(S_["s_constg"], 16)
            g.dma_start(ROWS[0:1, 0:1024], dram["bq"].ap().unsqueeze(0)).then_inc(S_["s_wsmall"], 16)
            g.dma_start(ROWS[0:1, 1024:1280], dram["bk"].ap().unsqueeze(0)).then_inc(S_["s_wsmall"], 16)
            g.dma_start(ROWS[0:1, 1280:1536], dram["bv"].ap().unsqueeze(0)).then_inc(S_["s_wsmall"], 16)
            g.dma_start(ROWS[0:1, 1536:1544], dram["router_b"].ap().unsqueeze(0)).then_inc(S_["s_wsmall"], 16)
            g.dma_start(BOROW[0:1, :], dram["bo"].ap().unsqueeze(0)).then_inc(S_["s_wsmall"], 16)
            wq_v = BFA[:, WQKV_off:WQKV_off + 12288].rearrange("p (c f) -> p c f", c=8)
            g.dma_start(wq_v[:, :, 0:1024],
                        dram["Wq"].ap().rearrange("(c p) f -> p c f", p=128)).then_inc(S_["s_wsmall"], 16)
            g.dma_start(wq_v[:, :, 1024:1280],
                        dram["Wk"].ap().rearrange("(c p) f -> p c f", p=128)).then_inc(S_["s_wsmall"], 16)
            g.dma_start(wq_v[:, :, 1280:1536],
                        dram["Wv"].ap().rearrange("(c p) f -> p c f", p=128)).then_inc(S_["s_wsmall"], 16)
            for pb in range(8):
                for half in range(2):
                    h = head_of(pb, half)
                    g.dma_start(BFA[half * 64:half * 64 + 64,
                                    WO_off + pb * 1024:WO_off + (pb + 1) * 1024],
                                dram["Wo"].ap()[h, :, :]).then_inc(S_["s_wsmall"], 16)
            g.dma_start(ROUTW[:, :].rearrange("p (c e) -> p c e", c=8),
                        dram["router_W"].ap().rearrange("(c p) e -> p c e", p=128)).then_inc(S_["s_wsmall"], 16)
            g.dma_start(CENTN, dram["centroids"].ap()).then_inc(S_["s_wsmall"], 16)

        @blk.vector
        def _(v):
            v.memset(ONESC[:, :], 1.0)
            v.memset(ONESCF[:, :], 1.0)
            v.memset(ONESF32, 1.0)
            v.memset(ONESF64, 1.0)
            v.memset(EPSC[:, :], EPS)

    with contextlib.ExitStack() as ps:
        P_B0 = ps.enter_context(nc.psum_tensor("P_B0", [128, 128], F32))
        P_B0B = ps.enter_context(nc.psum_tensor("P_B0B", [128, 64], BF16))
        with nc.Block("b0b") as blk:
            @blk.tensor
            def _(t):
                t.wait_ge(S_["s_const"], N_CONST)
                t.wait_ge(S_["s_constg"], N_CONSTG)
                t.matmul(P_B0[:, 0:16], ONESF64, SROW[0:1, 0:16],
                         start=True, stop=True)
                t.matmul(P_B0[:, 16:24], ONESF64, SROW[0:1, 16:24],
                         start=True, stop=True).then_inc(S_["s_b0"], 1)

            @blk.vector
            def _(v):
                v.wait_ge(S_["s_b0"], 1)
                v.tensor_copy(SINKB[:, :], P_B0[:, 0:16])
                v.tensor_scalar(NEGSINK[:, :], P_B0[:, 0:16], -1.0, None, ALU.mult)
                v.tensor_copy(RBB[:, :], P_B0[:, 16:24])
                v.engine_nop().then_inc(S_["s_b0"], 1)

        with nc.Block("b0c") as blk:
            @blk.tensor
            def _(t):
                t.wait_ge(S_["s_b0"], 2)
                for c in range(16):
                    ins = t.transpose(P_B0[:, c * 8:(c + 1) * 8],
                                      XR[0:8, c * 128:(c + 1) * 128],
                                      IDENTF[0:8, 0:8])
                ins.then_inc(S_["s_b0"], 1)
                t.wait_ge(S_["s_wsmall"], N_WSMALL)
                for c in range(8):
                    ins = t.transpose(P_B0B[:, c * 8:(c + 1) * 8],
                                      CENTN[0:8, c * 128:(c + 1) * 128],
                                      IDENTB[0:8, 0:8])
                ins.then_inc(S_["s_b0"], 1)

            @blk.vector
            def _(v):
                v.wait_ge(S_["s_b0"], 3)
                v.tensor_copy(GUPB[:, :], P_B0[:, 0:128])
                v.wait_ge(S_["s_b0"], 4)
                v.tensor_copy(CENTT[:, :], P_B0B[:, :])
                v.engine_nop().then_inc(S_["s_b0"], 1)
                v.engine_nop().then_inc(S_["s_b0"], 1)

        with nc.Block("b0d") as blk:
            @blk.tensor
            def _(t):
                for c in range(8):
                    t.wait_ge(S_["s_b0"], 6 + 2 * c)
                    t.matmul(P_B0[:, :], ONESF32,
                             G2ROW[0:1, c * 128:(c + 1) * 128],
                             start=True, stop=True).then_inc(S_["s_b0"], 1)

            @blk.vector
            def _(v):
                for c in range(8):
                    v.wait_ge(S_["s_b0"], 7 + 2 * c)
                    v.tensor_copy(G2T[:, c * 128:(c + 1) * 128], P_B0[:, :])
                    v.engine_nop().then_inc(S_["s_b0"], 1)

        with nc.Block("b0e") as blk:
            @blk.vector
            def _(v):
                v.wait_ge(S_["s_wsmall"], N_WSMALL)
                v.wait_ge(S_["s_const"], N_CONST)
                v.wait_ge(S_["s_constg"], N_CONSTG)
                for dc in range(8):
                    ins = v.tensor_scalar(
                        BFA[:, WQKV_off + dc * 1536:WQKV_off + (dc + 1) * 1536],
                        BFA[:, WQKV_off + dc * 1536:WQKV_off + (dc + 1) * 1536],
                        G1C[:, dc:dc + 1], None, ALU.mult)
                ins.then_inc(S_["s_b0"], 1)
    N_B0 = 23

    with contextlib.ExitStack() as ps:
        P_QKV = ps.enter_context(nc.psum_tensor("P_QKV", [128, 1536], F32))
        P_TR = ps.enter_context(nc.psum_tensor("P_TR", [128, 1024], BF16))
        P_S = ps.enter_context(nc.psum_tensor("P_S", [128, 384], F32))
        P_PTK = ps.enter_context(nc.psum_tensor("P_PTK", [128, 1024], BF16))
        P_AV = ps.enter_context(nc.psum_tensor("P_AV", [128, 128], F32))
        P_WO = ps.enter_context(nc.psum_tensor("P_WO", [128, 512], F32))

        with nc.Block("b1") as blk:
            @blk.sync
            def _(sy):
                for i in range(NT):
                    if i >= 3:
                        sy.wait_ge(S_["sv_evac"], 2 * (i - 3) + 2)
                    sy.dma_start(XR[:, (i % 3) * 1024:(i % 3 + 1) * 1024],
                                 x_in.ap()[i * 128:(i + 1) * 128, :]).then_inc(S_["sx%d" % (i % 3)], 16)

            @blk.gpsimd
            def _(g):
                def stage(k):
                    p, kind = divmod(k, 2)
                    if p < NS:
                        src = dram["Wfc_sh"].ap()[p] if kind == 0 else dram["Wproj_sh"].ap()[p]
                    else:
                        src = dram["Wfc_r"].ap()[p - NS] if kind == 0 else dram["Wproj_r"].ap()[p - NS]
                    dst = st_fc.ap()[p] if kind == 0 else st_pj.ap()[p]
                    g.dma_start(dst, src).then_inc(S_["s_stg%d" % p], 16)
                for k in range(2 * NPASS):
                    stage(k)
                g.wait_ge(S_["sp_qkv"], NT)
                gv = BFA[:, WG_o:WG_o + 8192].rearrange("p (c i) -> p c i", c=8)
                g.dma_start(gv, dram["gate_up_proj"].ap()[0].rearrange("(c p) i -> p c i", p=128)[:, :, 0:1024]).then_inc(S_["sw_g"], 16)
                g.wait_ge(S_["sp_wo"], 2 * NT)
                uv = BFA[:, WU_o:WU_o + 8192].rearrange("p (c i) -> p c i", c=8)
                g.dma_start(uv, dram["gate_up_proj"].ap()[0].rearrange("(c p) i -> p c i", p=128)[:, :, 1024:2048]).then_inc(S_["sw_u"], 16)
                dv = BFA[:, WD_o:WD_o + 8192].rearrange("p (c m) -> p c m", c=8)
                g.dma_start(dv, dram["down_proj"].ap()[0].rearrange("(c p) m -> p c m", p=128)).then_inc(S_["sw_d"], 16)
                g.dma_start(DBROW[0:1, :], dram["down_bias"].ap()[0].unsqueeze(0)).then_inc(S_["sw_d"], 16)

            @blk.tensor
            def _(t):
                t.wait_ge(S_["s_wsmall"], N_WSMALL)
                t.wait_ge(S_["s_b0"], N_B0)
                for i in range(NT + 1):
                    j = i - 1
                    if i < NT:
                        t.wait_ge(S_["sv_xhat"], i + 1)
                        if i >= 1:
                            t.wait_ge(S_["ss_cpq"], i)
                        for dc in range(8):
                            ins = t.transpose(P_TR[:, dc * 128:(dc + 1) * 128],
                                              XHATS(i % 2)[:, dc * 128:(dc + 1) * 128],
                                              IDENTB[:, :])
                        ins.then_inc(S_["sp_trx"], 1)
                        t.wait_ge(S_["ss_cpx"], i + 1)
                        if i >= 1:
                            t.wait_ge(S_["sv_ropeall"], i)
                        for nb in range(3):
                            t.matmul(P_QKV[:, nb * 512:(nb + 1) * 512], ONESC[0:1, :],
                                     ROWS[0:1, nb * 512:(nb + 1) * 512],
                                     start=True, stop=False)
                        for dc in range(8):
                            for nb in range(3):
                                ins = t.matmul(P_QKV[:, nb * 512:(nb + 1) * 512],
                                               XF1(i % 2, dc),
                                               BFA[:, WQKV_off + dc * 1536 + nb * 512:
                                                   WQKV_off + dc * 1536 + (nb + 1) * 512],
                                               start=False, stop=(dc == 7))
                        ins.then_inc(S_["sp_qkv"], 1)
                        t.wait_ge(S_["sv_ropek"], i + 1)
                        if i >= 1:
                            t.wait_ge(S_["ss_cpk"], i)
                        t.transpose(P_TR[:, 0:128], KROPE[:, 0:128], IDENTB[:, :])
                        t.transpose(P_TR[:, 128:256], KROPE[:, 128:256], IDENTB[:, :]).then_inc(S_["sp_trk"], 1)

                    if j >= 0:
                        KB = 0 if j == 0 else j - 1
                        nk = 2 if (j == 0 or j == NT - 1) else 3

                        def smm(hh):
                            pb2, half2 = divmod(hh - j * 16, 2)
                            h2 = head_of(pb2, half2)
                            g2_ = (h2 // 4) // 2
                            t.wait_ge(S_["ss_cpq"], j + 1)
                            t.wait_ge(S_["ss_cpk"], min(j + 2, NT))
                            t.wait_ge(S_["sv_ropeall"], min(j + 2, NT))
                            if hh > 0:
                                t.wait_ge(S_["sv_mask"], hh)
                            b = half2 * 64
                            t.matmul(P_S[:, 0:nk * 128],
                                     QRS(j % 2, pb2)[b:b + 64, :],
                                     BFA[b:b + 64,
                                         KFM_o + g2_ * T + KB * 128:
                                         KFM_o + g2_ * T + (KB + nk) * 128],
                                     start=True, stop=True).then_inc(S_["sp_s"], 1)

                        smm(j * 16)
                        for hl in range(16):
                            pb, half = divmod(hl, 2)
                            h = head_of(pb, half)
                            gk = h // 4
                            hh = j * 16 + hl
                            if hl + 1 < 16:
                                smm(hh + 1)
                            t.wait_ge(S_["sv_pn"], hh + 1)
                            if hh >= 1:
                                t.wait_ge(S_["ss_pt"], hh)
                            for k in range(nk):
                                ins = t.transpose(P_PTK[:, k * 128:(k + 1) * 128],
                                                  PBUF[:, (hh % 2) * 384 + k * 128:
                                                       (hh % 2) * 384 + (k + 1) * 128],
                                                  IDENTB[:, :])
                            ins.then_inc(S_["sp_pt"], 1)
                            if half == 0:
                                t.wait_ge(S_["ss_ofm"], j * 8 + pb)
                            t.wait_ge(S_["ss_pt"], hh + 1)
                            first = True
                            for k in range(nk):
                                ktg = KB + k
                                outv = P_AV[64:128, :] if half == 1 else P_AV[0:64, :]
                                tp = (0, 64) if half == 1 else None
                                ins = t.matmul(outv,
                                               BFA[:, VTOK_o + ktg * 256 + gk * 64:
                                                   VTOK_o + ktg * 256 + (gk + 1) * 64],
                                               PTS(hh % 2)[:, k * 128:(k + 1) * 128],
                                               start=first, stop=(k == nk - 1),
                                               tile_position=tp)
                                first = False
                            ins.then_inc(S_["sp_av"], 1)
                        t.wait_ge(S_["ss_ofm"], 8 * (j + 1))
                        for mh in range(2):
                            if 2 * j + mh >= 1:
                                t.wait_ge(S_["sv_evac"], 2 * j + mh)
                            t.matmul(P_WO[:, :], ONESC[0:1, :],
                                     BOROW[0:1, mh * 512:(mh + 1) * 512],
                                     start=True, stop=False)
                            for pb in range(8):
                                ins = t.matmul(P_WO[:, :], OFMS(pb),
                                               BFA[:, WO_off + pb * 1024 + mh * 512:
                                                   WO_off + pb * 1024 + (mh + 1) * 512],
                                               start=False, stop=(pb == 7))
                            ins.then_inc(S_["sp_wo"], 1)

                    if i < NT:
                        t.wait_ge(S_["sv_ropeq"], i + 1)
                        t.wait_ge(S_["ss_cpx"], i + 1)
                        t.wait_ge(S_["ss_cpk"], i + 1)
                        for pb in range(8):
                            ins = t.transpose(P_TR[:, pb * 128:(pb + 1) * 128],
                                              QROPE[:, pb * 128:(pb + 1) * 128],
                                              IDENTB[:, :])
                        ins.then_inc(S_["sp_trq"], 1)

            @blk.vector
            def _(v):
                for i in range(NT + 1):
                    j = i - 1
                    if i < NT:
                        xr = XR[:, (i % 3) * 1024:(i % 3 + 1) * 1024]
                        v.wait_ge(S_["sx%d" % (i % 3)], 16 * (i // 3 + 1))
                        if i >= 1:
                            v.wait_ge(S_["sv_ropeq"], i)
                        vedge(v, v.scalar_tensor_tensor(DUMP[:, :], xr, 1.0, xr,
                                                        ALU.bypass, ALU.mult,
                                                        accum_out=STAT[:, 0:1]))
                        v.tensor_scalar(STAT[:, 1:2], STAT[:, 0:1], 1.0 / D, EPS,
                                        ALU.mult, ALU.add).then_inc(S_["sv_eps"], 1)
                        v.wait_ge(S_["ss_rstd"], i + 1)
                        if i >= 2:
                            v.wait_ge(S_["sp_trx"], i - 1)
                        v.tensor_scalar(XHATS(i % 2), xr, STAT[:, 2:3], None,
                                        ALU.mult).then_inc(S_["sv_xhat"], 1)
                        v.wait_ge(S_["sp_qkv"], i + 1)
                        if i >= 1:
                            v.wait_ge(S_["sp_trq"], i)
                            v.wait_ge(S_["sp_trk"], i)
                        cosb = COSB[:, i * 32:(i + 1) * 32]
                        sinb = SINB[:, i * 32:(i + 1) * 32]
                        ksrc = P_QKV[:, 1024:1280].rearrange(
                            "p (gk hk dd two) -> p gk hk dd two", gk=2, hk=2, two=2)
                        kxe = ksrc[:, :, :, :, 0:1].squeeze(4)
                        kxo = ksrc[:, :, :, :, 1:2].squeeze(4)
                        kcos = cosb.unsqueeze(1).unsqueeze(1).broadcast_to([128, 2, 2, 32])
                        ksin = sinb.unsqueeze(1).unsqueeze(1).broadcast_to([128, 2, 2, 32])
                        kdst = KROPE[:, :].rearrange("p (gk hk ha dd) -> p gk hk ha dd",
                                                     gk=2, hk=2, ha=2)
                        kde = kdst[:, :, :, 0:1, :].squeeze(3)
                        kdo = kdst[:, :, :, 1:2, :].squeeze(3)
                        kt_ = DUMP[:, 0:128].rearrange("p (gk hk dd) -> p gk hk dd",
                                                       gk=2, hk=2)
                        vedge(v, v.tensor_tensor(kde, kxe, kcos, ALU.mult))
                        vedge(v, v.tensor_tensor(kt_, kxo, ksin, ALU.mult))
                        vedge(v, v.tensor_tensor(kde, kde, kt_, ALU.subtract))
                        vedge(v, v.tensor_tensor(kdo, kxo, kcos, ALU.mult))
                        vedge(v, v.tensor_tensor(kt_, kxe, ksin, ALU.mult))
                        v.tensor_tensor(kdo, kdo, kt_, ALU.add).then_inc(S_["sv_ropek"], 1)
                        qcos = cosb.unsqueeze(1).unsqueeze(1).broadcast_to([128, 2, 4, 32])
                        qsin = sinb.unsqueeze(1).unsqueeze(1).broadcast_to([128, 2, 4, 32])
                        for g2i in range(2):
                            qsrc = P_QKV[:, g2i * 512:(g2i + 1) * 512].rearrange(
                                "p (half j4 dd two) -> p half j4 dd two",
                                half=2, j4=4, two=2)
                            qxe = qsrc[:, :, :, :, 0:1].squeeze(4)
                            qxo = qsrc[:, :, :, :, 1:2].squeeze(4)
                            qdst = QROPE[:, g2i * 512:(g2i + 1) * 512].rearrange(
                                "p (j4 half ha dd) -> p j4 half ha dd",
                                j4=4, half=2, ha=2)
                            qde = qdst[:, :, :, 0:1, :].squeeze(3).transpose([0, 2, 1, 3])
                            qdo = qdst[:, :, :, 1:2, :].squeeze(3).transpose([0, 2, 1, 3])
                            qt_ = DUMP[:, 0:256].rearrange(
                                "p (half j4 dd) -> p half j4 dd", half=2, j4=4)
                            vedge(v, v.tensor_tensor(qde, qxe, qcos, ALU.mult))
                            vedge(v, v.tensor_tensor(qt_, qxo, qsin, ALU.mult))
                            vedge(v, v.tensor_tensor(qde, qde, qt_, ALU.subtract))
                            vedge(v, v.tensor_tensor(qdo, qxo, qcos, ALU.mult))
                            vedge(v, v.tensor_tensor(qt_, qxe, qsin, ALU.mult))
                            ins = v.tensor_tensor(qdo, qdo, qt_, ALU.add)
                            if g2i == 0:
                                vedge(v, ins)
                        ins.then_inc(S_["sv_ropeq"], 1)
                        v.tensor_copy(BFA[:, VTOK_o + i * 256:VTOK_o + (i + 1) * 256],
                                      P_QKV[:, 1280:1536]).then_inc(S_["sv_ropeall"], 1)
                    if j >= 0:
                        nk = 2 if (j == 0 or j == NT - 1) else 3
                        c0, c1 = 0, nk * 128
                        for hl in range(16):
                            pb, half = divmod(hl, 2)
                            h = head_of(pb, half)
                            hh = j * 16 + hl
                            r = hh % 2
                            smr = SM[:, r * 384 + c0:r * 384 + c1]
                            v.wait_ge(S_["sp_s"], hh + 1)
                            v.tensor_tensor(smr, P_S[:, c0:c1], mask_ap(j)[:, c0:c1],
                                            ALU.add).then_inc(S_["sv_mask"], 1)
                            v.wait_ge(S_["sv_mask"], hh + 1)
                            vedge(v, v.reduce_max(STAT[:, 4 + r:5 + r], smr,
                                                  axis=AX.X, negate=True))
                            v.tensor_scalar(STAT[:, 6 + r:7 + r], STAT[:, 4 + r:5 + r],
                                            NEGSINK[:, h:h + 1], None,
                                            ALU.min).then_inc(S_["sv_negm"], 1)
                            v.wait_ge(S_["ss_exp"], hh + 1)
                            vedge(v, v.tensor_tensor(STAT[:, 10 + r:11 + r],
                                                     STAT[:, 8 + r:9 + r],
                                                     STAT[:, 12 + r:13 + r], ALU.add))
                            vedge(v, v.reciprocal(STAT[:, 14 + r:15 + r],
                                                  STAT[:, 10 + r:11 + r]))
                            pbv = PBUF[:, r * 384 + c0:r * 384 + c1]
                            v.tensor_scalar(pbv, pbv, STAT[:, 14 + r:15 + r], None,
                                            ALU.mult).then_inc(S_["sv_pn"], 1)
                        for mh in range(2):
                            v.wait_ge(S_["sp_wo"], 2 * j + mh + 1)
                            v.tensor_tensor(A[:, j * 1024 + mh * 512:j * 1024 + (mh + 1) * 512],
                                            P_WO[:, :],
                                            XR[:, (j % 3) * 1024 + mh * 512:
                                               (j % 3) * 1024 + (mh + 1) * 512],
                                            ALU.add).then_inc(S_["sv_evac"], 1)

            @blk.scalar
            def _(s):
                for i in range(NT + 1):
                    j = i - 1
                    if i < NT:
                        s.wait_ge(S_["sv_eps"], i + 1)
                        sedge(s, s.activation(STAT[:, 3:4], STAT[:, 1:2], AF.Ln))
                        s.activation(STAT[:, 2:3], STAT[:, 3:4], AF.Exp,
                                     scale=-0.5).then_inc(S_["ss_rstd"], 1)
                        s.wait_ge(S_["sp_trx"], i + 1)
                        for dc in range(8):
                            ins = s.copy(XF1(i % 2, dc), P_TR[:, dc * 128:(dc + 1) * 128])
                        ins.then_inc(S_["ss_cpx"], 1)
                        s.wait_ge(S_["sp_trk"], i + 1)
                        s.copy(BFA[:, KFM_o + i * 128:KFM_o + (i + 1) * 128],
                               P_TR[:, 0:128])
                        s.copy(BFA[:, KFM_o + T + i * 128:KFM_o + T + (i + 1) * 128],
                               P_TR[:, 128:256]).then_inc(S_["ss_cpk"], 1)
                    if j >= 0:
                        for hl in range(16):
                            pb, half = divmod(hl, 2)
                            h = head_of(pb, half)
                            hh = j * 16 + hl
                            r = hh % 2
                            nk = 2 if (j == 0 or j == NT - 1) else 3
                            s.wait_ge(S_["sv_negm"], hh + 1)
                            if hh >= 2:
                                s.wait_ge(S_["sp_pt"], hh - 1)
                            kc0, kc1 = 0, nk * 128
                            s.activation(PBUF[:, r * 384 + kc0:r * 384 + kc1],
                                         SM[:, r * 384 + kc0:r * 384 + kc1], AF.Exp,
                                         bias=STAT[:, 6 + r:7 + r],
                                         accum_out=STAT[:, 8 + r:9 + r])
                            s.activation(STAT[:, 12 + r:13 + r], SINKB[:, h:h + 1],
                                         AF.Exp, bias=STAT[:, 6 + r:7 + r]).then_inc(S_["ss_exp"], 1)
                            s.wait_ge(S_["sp_pt"], hh + 1)
                            for k in range(nk):
                                ins = s.copy(PTS(r)[:, k * 128:(k + 1) * 128],
                                             P_PTK[:, k * 128:(k + 1) * 128])
                            ins.then_inc(S_["ss_pt"], 1)
                            if half == 1:
                                s.wait_ge(S_["sp_av"], hh + 1)
                                s.copy(OFMS(pb), P_AV[:, :]).then_inc(S_["ss_ofm"], 1)
                    if i < NT:
                        s.wait_ge(S_["sp_trq"], i + 1)
                        for pb in range(8):
                            ins = s.copy(QRS(i % 2, pb), P_TR[:, pb * 128:(pb + 1) * 128])
                        ins.then_inc(S_["ss_cpq"], 1)

    with contextlib.ExitStack() as ps:
        P_TR2 = ps.enter_context(nc.psum_tensor("P_TR2", [128, 1024], BF16))
        P_RT = ps.enter_context(nc.psum_tensor("P_RT", [128, 8], F32))
        with nc.Block("b2") as blk:
            if debug:
                @blk.sync
                def _(sy):
                    for i in range(NT):
                        sy.dma_start(dbg["dbg_A1"].ap()[i * 128:(i + 1) * 128, :],
                                     A[:, i * 1024:(i + 1) * 1024]).then_inc(S_["s_out"], 16)
                    sy.dma_start(dbg["dbg_SM"].ap(), SM).then_inc(S_["s_out"], 16)
                    sy.wait_ge(S_["s_out"], 16 * (NT + 1))

                @blk.gpsimd
                def _(g):
                    g.dma_start(dbg["dbg_KFM"].ap(), BFA[:, KFM_o:KFM_o + 2 * T]).then_inc(S_["s_dbg"], 16)
                    g.dma_start(dbg["dbg_VT"].ap(), BFA[:, VTOK_o:VTOK_o + NT * 256]).then_inc(S_["s_dbg"], 16)
                    g.dma_start(dbg["dbg_PB"].ap(), PBUF).then_inc(S_["s_dbg"], 16)
                    g.dma_start(dbg["dbg_OF"].ap(), BFA[:, OFMS_o:OFMS_o + 1024]).then_inc(S_["s_dbg"], 16)
                    g.wait_ge(S_["s_dbg"], 64)

            @blk.vector
            def _(v):
                for i in range(NT):
                    at = A[:, i * 1024:(i + 1) * 1024]
                    vedge(v, v.scalar_tensor_tensor(DUMP2[:, :], at, 1.0, at,
                                                    ALU.bypass, ALU.mult,
                                                    accum_out=STAT[:, 0:1]))
                    v.tensor_scalar(STAT[:, 1:2], STAT[:, 0:1], 1.0 / D, EPS,
                                    ALU.mult, ALU.add).then_inc(S_["b2_eps"], 1)
                    v.wait_ge(S_["b2_rstd"], i + 1)
                    if i >= 2:
                        v.wait_ge(S_["b2_trx"], i - 1)
                    v.scalar_tensor_tensor(XH2(i % 2), at, STAT[:, 2:3], G2T,
                                           ALU.mult, ALU.mult).then_inc(S_["b2_xhat"], 1)
                    v.wait_ge(S_["b2_rt"], i + 1)
                    v.tensor_copy(LSC[:, 0:8], P_RT[:, :]).then_inc(S_["b2_sc"], 1)
                    v.wait_ge(S_["b2_sc"], i + 1)
                    vedge(v, v.tensor_copy(LSC[:, 8:16], LSC[:, 0:8]))
                    for it in range(4):
                        vedge(v, v.reduce_max(STAT[:, 4:5], LSC[:, 8:16], axis=AX.X,
                                              negate=True))
                        if it == 0:
                            vedge(v, v.tensor_copy(STAT[:, 5:6], STAT[:, 4:5]))
                        if it < 3:
                            vedge(v, v.tensor_scalar(STAT[:, 6:7], STAT[:, 4:5],
                                                     -1.0, None, ALU.mult))
                            vedge(v, v.tensor_scalar(LSC[:, 16:24], LSC[:, 8:16],
                                                     STAT[:, 6:7], None, ALU.is_ge))
                            vedge(v, v.scalar_tensor_tensor(LSC[:, 8:16], LSC[:, 16:24],
                                                            -BIG, LSC[:, 8:16],
                                                            ALU.mult, ALU.add))
                    vedge(v, v.tensor_scalar(STAT[:, 6:7], STAT[:, 4:5], -1.0, None,
                                             ALU.mult))
                    vedge(v, v.tensor_scalar(LSC[:, 16:24], LSC[:, 0:8], STAT[:, 6:7],
                                             None, ALU.is_ge))
                    vedge(v, v.scalar_tensor_tensor(LSC[:, 24:32], LSC[:, 16:24], BIG,
                                                    LSC[:, 0:8], ALU.mult, ALU.add))
                    v.tensor_scalar(LSC[:, 24:32], LSC[:, 24:32], -BIG, None,
                                    ALU.add).then_inc(S_["b2_lsel"], 1)
                    v.wait_ge(S_["b2_exp"], i + 1)
                    vedge(v, v.reciprocal(STAT[:, 7:8], STAT[:, 9:10]))
                    v.tensor_scalar(SCORES[:, i * 8:(i + 1) * 8], LSC[:, 16:24],
                                    STAT[:, 7:8], 0.25, ALU.mult, ALU.mult)

            @blk.scalar
            def _(s):
                if debug:
                    s.wait_ge(S_["s_dbg"], 64)
                for i in range(NT):
                    s.wait_ge(S_["b2_eps"], i + 1)
                    sedge(s, s.activation(STAT[:, 3:4], STAT[:, 1:2], AF.Ln))
                    s.activation(STAT[:, 2:3], STAT[:, 3:4], AF.Exp,
                                 scale=-0.5).then_inc(S_["b2_rstd"], 1)
                    s.wait_ge(S_["b2_trx"], i + 1)
                    for dc in range(8):
                        ins = s.copy(XF2t(dc, i), P_TR2[:, dc * 128:(dc + 1) * 128])
                    ins.then_inc(S_["b2_cpx"], 1)
                    s.wait_ge(S_["b2_lsel"], i + 1)
                    s.activation(LSC[:, 16:24], LSC[:, 24:32], AF.Exp,
                                 bias=STAT[:, 5:6],
                                 accum_out=STAT[:, 9:10]).then_inc(S_["b2_exp"], 1)

            @blk.tensor
            def _(t):
                for i in range(NT):
                    t.wait_ge(S_["b2_xhat"], i + 1)
                    if i >= 1:
                        t.wait_ge(S_["b2_cpx"], i)
                    for dc in range(8):
                        ins = t.transpose(P_TR2[:, dc * 128:(dc + 1) * 128],
                                          XH2(i % 2)[:, dc * 128:(dc + 1) * 128],
                                          IDENTB[:, :])
                    ins.then_inc(S_["b2_trx"], 1)
                    t.wait_ge(S_["b2_cpx"], i + 1)
                    t.wait_ge(S_["b2_sc"], i)
                    t.matmul(P_RT[:, :], ONESC[0:1, :], ROWS[0:1, 1536:1544],
                             start=True, stop=False)
                    for dc in range(8):
                        ins = t.matmul(P_RT[:, :], XF2t(dc, i),
                                       ROUTW[:, dc * 8:(dc + 1) * 8],
                                       start=False, stop=(dc == 7))
                    ins.then_inc(S_["b2_rt"], 1)

    with contextlib.ExitStack() as ps:
        P_G = ps.enter_context(nc.psum_tensor("P_G", [128, 2048], F32))
        P_U = ps.enter_context(nc.psum_tensor("P_U", [128, 2048], F32))
        with nc.Block("b3") as blk:
            @blk.sync
            def _(sy):
                XRB = XR[:, :].bitcast(BF16)
                slots = lambda r: XRB[:, r * 1024:(r + 1) * 1024]
                jobs = []
                for p in range(NPASS):
                    for dc in range(8):
                        for fq in range(4):
                            jobs.append(("fc", p, dc, fq))
                for p in range(NPASS):
                    for fc in range(32):
                        jobs.append(("pj", p, fc))
                n_jobs = len(jobs)

                def tr(k):
                    kind = jobs[k]
                    r = k % 6
                    if kind[0] == "fc":
                        _, p, dc, fq = kind
                        sy.wait_ge(S_["s_stg%d" % p], 32)
                        sy.dma_start(slots(r),
                                     st_fc.ap()[p][fq * 1024:(fq + 1) * 1024,
                                                   dc * 128:(dc + 1) * 128],
                                     transpose=True).then_inc(S_["st2%d" % r], 16)
                    else:
                        _, p, fc = kind
                        sy.wait_ge(S_["s_stg%d" % p], 32)
                        sy.dma_start(slots(r),
                                     st_pj.ap()[p][:, fc * 128:(fc + 1) * 128],
                                     transpose=True).then_inc(S_["st2%d" % r], 16)

                def wb(k):
                    kind = jobs[k]
                    r = k % 6
                    if kind[0] == "fc":
                        _, p, dc, fq = kind
                        dst = st_fcT.ap()[p][fq * 4:(fq + 1) * 4, :, dc, :].transpose([1, 0, 2])
                        sy.dma_start(dst, slots(r).rearrange("p (s f) -> p s f", s=4)).then_inc(S_["swb%d" % r], 16)
                    else:
                        _, p, fc = kind
                        sy.dma_start(st_pjT.ap()[p][fc], slots(r)).then_inc(S_["swb%d" % r], 16)

                for k in range(min(6, n_jobs)):
                    tr(k)
                for k in range(n_jobs):
                    sy.wait_ge(S_["st2%d" % (k % 6)], 16 * (k // 6 + 1))
                    wb(k)
                    if k + 6 < n_jobs:
                        sy.wait_ge(S_["swb%d" % (k % 6)], 16 * (k // 6 + 1))
                        tr(k + 6)
                for r_ in range(6):
                    sy.wait_ge(S_["swb%d" % r_],
                               16 * len([k for k in range(n_jobs) if k % 6 == r_]))

            @blk.gpsimd
            def _(g):
                for e in range(1, NE):
                    g.wait_ge(S_["sp_g"], 8 * e)
                    gv = BFA[:, WG_o:WG_o + 8192].rearrange("p (c i) -> p c i", c=8)
                    g.dma_start(gv, dram["gate_up_proj"].ap()[e].rearrange("(c p) i -> p c i", p=128)[:, :, 0:1024]).then_inc(S_["sw_g"], 16)
                    g.wait_ge(S_["sp_u"], 8 * e)
                    uv = BFA[:, WU_o:WU_o + 8192].rearrange("p (c i) -> p c i", c=8)
                    g.dma_start(uv, dram["gate_up_proj"].ap()[e].rearrange("(c p) i -> p c i", p=128)[:, :, 1024:2048]).then_inc(S_["sw_u"], 16)
                    g.wait_ge(S_["sp_d"], 2 * NT * e)
                    dv = BFA[:, WD_o:WD_o + 8192].rearrange("p (c m) -> p c m", c=8)
                    g.dma_start(dv, dram["down_proj"].ap()[e].rearrange("(c p) m -> p c m", p=128)).then_inc(S_["sw_d"], 16)
                    g.dma_start(DBROW[0:1, :], dram["down_bias"].ap()[e].unsqueeze(0)).then_inc(S_["sw_d"], 16)

            @blk.tensor
            def _(t):
                for e in range(NE):
                    t.wait_ge(S_["sw_g"], 16 * (e + 1))
                    t.wait_ge(S_["sv_devac"], 2 * NT * e)
                    for ic in range(8):
                        idx = e * 8 + ic
                        t.wait_ge(S_["sv_gt"], n_tc * idx)
                        for dc in range(8):
                            for tc in range(n_tc):
                                ins = t.matmul(P_G[:, tc * 512:(tc + 1) * 512],
                                               BFA[:, WG_o + dc * 1024 + ic * 128:
                                                   WG_o + dc * 1024 + (ic + 1) * 128],
                                               XF2c(dc, tc * 512, (tc + 1) * 512),
                                               start=(dc == 0), stop=(dc == 7))
                        ins.then_inc(S_["sp_g"], 1)
                        t.wait_ge(S_["sw_u"], 16 * (e + 1))
                        t.wait_ge(S_["sv_up"], n_tc * idx)
                        for dc in range(8):
                            for tc in range(n_tc):
                                ins = t.matmul(P_U[:, tc * 512:(tc + 1) * 512],
                                               BFA[:, WU_o + dc * 1024 + ic * 128:
                                                   WU_o + dc * 1024 + (ic + 1) * 128],
                                               XF2c(dc, tc * 512, (tc + 1) * 512),
                                               start=(dc == 0), stop=(dc == 7))
                        ins.then_inc(S_["sp_u"], 1)
                    t.wait_ge(S_["sv_gated"], 8 * n_tc * (e + 1))
                    t.wait_ge(S_["sw_d"], 32 * (e + 1))
                    for tt in range(NT):
                        t.wait_ge(S_["sv_devac"], 2 * (e * NT + tt))
                        for mh in range(2):
                            t.matmul(P_G[:, mh * 512:(mh + 1) * 512], ONESC[0:1, :],
                                     DBROW[0:1, mh * 512:(mh + 1) * 512],
                                     start=True, stop=False)
                            for ic in range(8):
                                ins = t.matmul(P_G[:, mh * 512:(mh + 1) * 512],
                                               GATED(ic, tt * 128, (tt + 1) * 128),
                                               BFA[:, WD_o + ic * 1024 + mh * 512:
                                                   WD_o + ic * 1024 + (mh + 1) * 512],
                                               start=False, stop=(ic == 7))
                            ins.then_inc(S_["sp_d"], 1)

            @blk.vector
            def _(v):
                for e in range(NE):
                    for ic in range(8):
                        idx = e * 8 + ic
                        for tc in range(n_tc):
                            r = (idx * n_tc + tc) % 2
                            v.wait_ge(S_["sp_g"], idx + 1)
                            if e >= 1:
                                v.wait_ge(S_["sp_d"], 2 * NT * e)
                            gb = GUPB[:, ic * 8 + e:ic * 8 + e + 1]
                            ub = GUPB[:, 64 + ic * 8 + e:64 + ic * 8 + e + 1]
                            v.tensor_scalar(GTS[:, r * 512:(r + 1) * 512],
                                            P_G[:, tc * 512:(tc + 1) * 512],
                                            gb, LIMIT, ALU.add,
                                            ALU.min).then_inc(S_["sv_gt"], 1)
                            v.wait_ge(S_["sp_u"], idx + 1)
                            vedge(v, v.tensor_scalar(UP2S[:, r * 512:(r + 1) * 512],
                                                     P_U[:, tc * 512:(tc + 1) * 512],
                                                     ub, -LIMIT, ALU.add, ALU.max))
                            v.tensor_scalar(UP2S[:, r * 512:(r + 1) * 512],
                                            UP2S[:, r * 512:(r + 1) * 512],
                                            LIMIT, 1.0, ALU.min,
                                            ALU.add).then_inc(S_["sv_up"], 1)
                            v.wait_ge(S_["ss_glu"], idx * n_tc + tc + 1)
                            v.wait_ge(S_["sv_up"], idx * n_tc + tc + 1)
                            v.tensor_tensor(GATED(ic, tc * 512, (tc + 1) * 512),
                                            GLUS[:, r * 512:(r + 1) * 512],
                                            UP2S[:, r * 512:(r + 1) * 512],
                                            ALU.mult).then_inc(S_["sv_gated"], 1)
                    for tt in range(NT):
                        for mh in range(2):
                            v.wait_ge(S_["sp_d"], (e * NT + tt) * 2 + mh + 1)
                            v.scalar_tensor_tensor(
                                A[:, tt * 1024 + mh * 512:tt * 1024 + (mh + 1) * 512],
                                P_G[:, mh * 512:(mh + 1) * 512],
                                SCORES[:, tt * 8 + e:tt * 8 + e + 1],
                                A[:, tt * 1024 + mh * 512:tt * 1024 + (mh + 1) * 512],
                                ALU.mult, ALU.add).then_inc(S_["sv_devac"], 1)

            @blk.scalar
            def _(s):
                for e in range(NE):
                    for ic in range(8):
                        idx = e * 8 + ic
                        for tc in range(n_tc):
                            r = (idx * n_tc + tc) % 2
                            s.wait_ge(S_["sv_gt"], idx * n_tc + tc + 1)
                            s.activation(GLUS[:, r * 512:(r + 1) * 512],
                                         GTS[:, r * 512:(r + 1) * 512],
                                         AF.Gelu_apprx_sigmoid).then_inc(S_["ss_glu"], 1)

    with contextlib.ExitStack() as ps:
        P_TR4 = ps.enter_context(nc.psum_tensor("P_TR4", [128, 1024], BF16))
        P_RT4 = ps.enter_context(nc.psum_tensor("P_RT4", [128, 8], F32))
        with nc.Block("b4") as blk:
            @blk.sync
            def _(sy):
                if debug:
                    for i in range(NT):
                        sy.dma_start(dbg["dbg_A3"].ap()[i * 128:(i + 1) * 128, :],
                                     A[:, i * 1024:(i + 1) * 1024]).then_inc(S_["s_out"], 16)
                    sy.dma_start(dbg["dbg_SC"].ap(), SCORES[:, :]).then_inc(S_["s_out"], 16)
                    sy.wait_ge(S_["s_out"], 16 * (2 * NT + 2))
                for i in range(NT):
                    if i >= 3:
                        sy.wait_ge(S_["b4_xhat"], i - 2)
                    sy.dma_start(XR[:, (i % 3) * 1024:(i % 3 + 1) * 1024],
                                 x_in.ap()[i * 128:(i + 1) * 128, :]).then_inc(S_["b4x%d" % (i % 3)], 16)

            @blk.vector
            def _(v):
                for i in range(NT):
                    xr = XR[:, (i % 3) * 1024:(i % 3 + 1) * 1024]
                    at = A[:, i * 1024:(i + 1) * 1024]
                    v.wait_ge(S_["b4x%d" % (i % 3)], 16 * (i // 3 + 1))
                    vedge(v, v.tensor_tensor(xr, xr, at, ALU.add))
                    vedge(v, v.tensor_scalar(DUMP2[:, :], xr, 1.0, None, ALU.mult,
                                             ALU.add, accum_out=STAT[:, 0:1]))
                    vedge(v, v.scalar_tensor_tensor(DUMP2[:, :], xr, 1.0, xr,
                                                    ALU.bypass, ALU.mult,
                                                    accum_out=STAT[:, 1:2]))
                    vedge(v, v.tensor_scalar(STAT[:, 2:3], STAT[:, 0:1], -1.0 / D,
                                             None, ALU.mult))
                    vedge(v, v.tensor_tensor(STAT[:, 3:4], STAT[:, 2:3], STAT[:, 2:3],
                                             ALU.mult))
                    v.scalar_tensor_tensor(STAT[:, 4:5], STAT[:, 1:2], 1.0 / D,
                                           STAT[:, 3:4], ALU.mult,
                                           ALU.subtract).then_inc(S_["b4_eps"], 1)
                    v.wait_ge(S_["b4_rstd"], i + 1)
                    if i >= 2:
                        v.wait_ge(S_["b2_trx"], NT + i - 1)
                    vedge(v, v.tensor_scalar(STAT[:, 7:8], STAT[:, 6:7], 2.0, None,
                                             ALU.mult))
                    vedge(v, v.tensor_scalar(XH2(i % 2), xr, STAT[:, 2:3],
                                             STAT[:, 6:7], ALU.add, ALU.mult))
                    v.tensor_scalar(at, xr, STAT[:, 2:3], STAT[:, 7:8],
                                    ALU.add, ALU.mult).then_inc(S_["b4_xhat"], 1)
                    v.wait_ge(S_["b4_rt"], i + 1)
                    v.tensor_copy(LSC[:, 0:8], P_RT4[:, :]).then_inc(S_["b4_sc"], 1)
                    v.wait_ge(S_["b4_sc"], i + 1)
                    vedge(v, v.tensor_tensor(LSC[:, 8:16], LSC[:, 0:8], RBB[:, :],
                                             ALU.add))
                    vedge(v, v.reduce_max(STAT[:, 10:11], LSC[:, 8:16], axis=AX.X,
                                          negate=True))
                    vedge(v, v.tensor_scalar(STAT[:, 11:12], STAT[:, 10:11], -1.0,
                                             None, ALU.mult))
                    vedge(v, v.tensor_scalar(LSC[:, 16:24], LSC[:, 8:16],
                                             STAT[:, 11:12], None, ALU.is_ge))
                    vedge(v, v.scalar_tensor_tensor(LSC[:, 24:32], LSC[:, 16:24],
                                                    -BIG, LSC[:, 8:16],
                                                    ALU.mult, ALU.add))
                    vedge(v, v.reduce_max(STAT[:, 12:13], LSC[:, 24:32], axis=AX.X,
                                          negate=True))
                    vedge(v, v.tensor_scalar(STAT[:, 13:14], STAT[:, 12:13], -1.0,
                                             None, ALU.mult))
                    vedge(v, v.tensor_scalar(LSC[:, 16:24], LSC[:, 8:16],
                                             STAT[:, 13:14], None, ALU.is_ge))
                    v.wait_ge(S_["b4_sig"], i + 1)
                    vedge(v, v.scalar_tensor_tensor(LSC[:, 24:32], LSC[:, 16:24], BIG,
                                                    STAT[:, 16:24], ALU.mult, ALU.add))
                    vedge(v, v.tensor_scalar(LSC[:, 24:32], LSC[:, 24:32], -BIG, None,
                                             ALU.add))
                    vedge(v, v.reduce_max(STAT[:, 14:15], LSC[:, 24:32], axis=AX.X,
                                          negate=True))
                    v.engine_nop().then_inc(S_["b4_rsel"], 1)
                    v.wait_ge(S_["b4_exp"], i + 1)
                    vedge(v, v.reciprocal(STAT[:, 25:26], STAT[:, 15:16]))
                    v.tensor_scalar(COMB[:, i * 8:(i + 1) * 8], LSC[:, 16:24],
                                    STAT[:, 25:26], None, ALU.mult)

            @blk.scalar
            def _(s):
                for i in range(NT):
                    s.wait_ge(S_["b4_eps"], i + 1)
                    sedge(s, s.activation(STAT[:, 5:6], STAT[:, 4:5], AF.Ln,
                                          bias=EPSC[:, 0:1]))
                    s.activation(STAT[:, 6:7], STAT[:, 5:6], AF.Exp,
                                 scale=-0.5).then_inc(S_["b4_rstd"], 1)
                    s.wait_ge(S_["b2_trx"], NT + i + 1)
                    for dc in range(8):
                        ins = s.copy(XF2t(dc, i), P_TR4[:, dc * 128:(dc + 1) * 128])
                    ins.then_inc(S_["b2_cpx"], 1)
                    s.wait_ge(S_["b4_sc"], i + 1)
                    s.activation(STAT[:, 16:24], LSC[:, 0:8],
                                 AF.Sigmoid).then_inc(S_["b4_sig"], 1)
                    s.wait_ge(S_["b4_rsel"], i + 1)
                    s.activation(LSC[:, 16:24], LSC[:, 24:32], AF.Exp,
                                 bias=STAT[:, 14:15],
                                 accum_out=STAT[:, 15:16]).then_inc(S_["b4_exp"], 1)

            @blk.tensor
            def _(t):
                t.wait_ge(S_["s_b0"], 23)
                for i in range(NT):
                    t.wait_ge(S_["b4_xhat"], i + 1)
                    if i >= 1:
                        t.wait_ge(S_["b2_cpx"], NT + i)
                    for dc in range(8):
                        ins = t.transpose(P_TR4[:, dc * 128:(dc + 1) * 128],
                                          XH2(i % 2)[:, dc * 128:(dc + 1) * 128],
                                          IDENTB[:, :])
                    ins.then_inc(S_["b2_trx"], 1)
                    t.wait_ge(S_["b2_cpx"], NT + i + 1)
                    t.wait_ge(S_["b4_sc"], i)
                    for dc in range(8):
                        ins = t.matmul(P_RT4[:, :], XF2t(dc, i),
                                       CENTT[:, dc * 8:(dc + 1) * 8],
                                       start=(dc == 0), stop=(dc == 7))
                    ins.then_inc(S_["b4_rt"], 1)

    with contextlib.ExitStack() as ps:
        P_BIG = ps.enter_context(nc.psum_tensor("P_BIG", [128, 4096], F32))
        with nc.Block("b5") as blk:
            @blk.scalar
            def _(s):
                ntot = 32 * NPASS + 32 * NPASS
                for r_ in range(6):
                    s.wait_ge(S_["swb%d" % r_], 16 * len([k for k in range(ntot) if k % 6 == r_]))
                glist = [(p, half, fsb) for p in range(NPASS)
                         for half in range(n_half) for fsb in range(16)]

                def issue_load(g):
                    p_, half_, fsb_ = glist[g]
                    s.dma_start(WFCs(g % 2)[:, :].rearrange("p (c f) -> p c f", c=8),
                                st_fcT.ap()[p_][fsb_]).then_inc(S_["swf%d" % (g % 2)], 16)

                issue_load(0)
                issue_load(1)
                for g in range(len(glist)):
                    if g + 2 < len(glist):
                        s.wait_ge(S_["sp_fc"], g + 1)
                        issue_load(g + 2)
                    s.wait_ge(S_["sp_fc"], g + 1)
                    for fsub in range(2):
                        for tc in range(Tq // 512):
                            bank = (g % 2) * 4 + fsub * 2 + tc
                            s.activation(HR((glist[g][2]) * 2 + fsub, tc * 512,
                                            (tc + 1) * 512),
                                         P_BIG[:, bank * 512:(bank + 1) * 512],
                                         AF.Gelu).then_inc(S_["ss_hr"], 1)

            @blk.sync
            def _(sy):
                if debug:
                    for i in range(NT):
                        sy.dma_start(dbg["dbg_A4"].ap()[i * 128:(i + 1) * 128, :],
                                     A[:, i * 1024:(i + 1) * 1024]).then_inc(S_["s_out"], 16)
                    sy.dma_start(dbg["dbg_CB"].ap(), COMB[:, :]).then_inc(S_["s_out"], 16)
                    sy.wait_ge(S_["s_out"], 16 * (3 * NT + 3))
                ntot = 32 * NPASS + 32 * NPASS
                for r_ in range(6):
                    sy.wait_ge(S_["swb%d" % r_], 16 * len([k for k in range(ntot) if k % 6 == r_]))
                for p in range(NPASS):
                    for half in range(n_half):
                        gh = p * n_half + half
                        for sq in range(Tq // 512):
                            for fc in range(32):
                                pfc = (gh * (Tq // 512) + sq) * 32 + fc
                                if pfc >= 4:
                                    sy.wait_ge(S_["sp_pj"], pfc - 3)
                                sy.dma_start(WPJs(pfc % 4),
                                             st_pjT.ap()[p][fc]).then_inc(S_["swp%d" % (pfc % 4)], 16)

            @blk.tensor
            def _(t):
                for p in range(NPASS):
                    for half in range(n_half):
                        gh = p * n_half + half
                        if gh >= 1:
                            t.wait_ge(S_["sp_pj"], gh * (Tq // 512) * 32)
                        for fsb in range(16):
                            gfsb = gh * 16 + fsb
                            t.wait_ge(S_["swf%d" % (gfsb % 2)], 16 * (gfsb // 2 + 1))
                            if gfsb >= 2:
                                t.wait_ge(S_["ss_hr"], 4 * (gfsb - 1))
                            if gh >= 1 and fsb < 2:
                                t.wait_ge(S_["sv_pevac"], (gh - 1 + 1) * (Tq // 512) * 8)
                            bq_ = (gfsb % 2) * 4
                            for fsub in range(2):
                                for dc in range(8):
                                    for tc in range(Tq // 512):
                                        bank = bq_ + fsub * 2 + tc
                                        ins = t.matmul(
                                            P_BIG[:, bank * 512:(bank + 1) * 512],
                                            WFCs(gfsb % 2)[:, dc * 256 + fsub * 128:
                                                           dc * 256 + (fsub + 1) * 128],
                                            XF2c(dc, half * Tq + tc * 512,
                                                 half * Tq + (tc + 1) * 512),
                                            start=(dc == 0), stop=(dc == 7))
                            ins.then_inc(S_["sp_fc"], 1)
                        t.wait_ge(S_["ss_hr"], 64 * (gh + 1))
                        for sq in range(Tq // 512):
                            gsq = gh * (Tq // 512) + sq
                            for fc in range(32):
                                pfc = gsq * 32 + fc
                                t.wait_ge(S_["swp%d" % (pfc % 4)], 16 * (pfc // 4 + 1))
                                for tt in range(4):
                                    for mh in range(2):
                                        bank = tt * 2 + mh
                                        if fc == 0 and gsq >= 1:
                                            t.wait_ge(S_["sv_pevac"],
                                                      8 * (gsq - 1) + bank + 1)
                                        ins = t.matmul(
                                            P_BIG[:, bank * 512:(bank + 1) * 512],
                                            HR(fc, sq * 512 + tt * 128,
                                               sq * 512 + (tt + 1) * 128),
                                            WPJs(pfc % 4)[:, mh * 512:(mh + 1) * 512],
                                            start=(fc == 0), stop=(fc == 31))
                                ins.then_inc(S_["sp_pj"], 1)


            @blk.vector
            def _(v):
                for p in range(NPASS):
                    for half in range(n_half):
                        gh = p * n_half + half
                        for sq in range(Tq // 512):
                            gsq = gh * (Tq // 512) + sq
                            v.wait_ge(S_["sp_pj"], (gsq + 1) * 32)
                            for tt in range(4):
                                tglob = half * ntt_q + sq * 4 + tt
                                for mh in range(2):
                                    bank = tt * 2 + mh
                                    av = A[:, tglob * 1024 + mh * 512:
                                           tglob * 1024 + (mh + 1) * 512]
                                    if p < NS:
                                        ins = v.tensor_tensor(
                                            av, P_BIG[:, bank * 512:(bank + 1) * 512],
                                            av, ALU.add)
                                    else:
                                        ins = v.scalar_tensor_tensor(
                                            av, P_BIG[:, bank * 512:(bank + 1) * 512],
                                            COMB[:, tglob * 8 + (p - NS):
                                                 tglob * 8 + (p - NS) + 1],
                                            av, ALU.mult, ALU.add)
                                    ins.then_inc(S_["sv_pevac"], 1)

    with nc.Block("b6") as blk:
        @blk.vector
        def _(v):
            for i in range(NT):
                at = A[:, i * 1024:(i + 1) * 1024]
                if i >= 3:
                    v.wait_ge(S_["so%d" % (i % 3)], 16 * (i // 3))
                vedge(v, v.tensor_scalar(DUMP2[:, :], at, 1.0, None, ALU.mult,
                                         ALU.add, accum_out=STAT[:, 0:1]))
                vedge(v, v.scalar_tensor_tensor(DUMP2[:, :], at, 1.0, at,
                                                ALU.bypass, ALU.mult,
                                                accum_out=STAT[:, 1:2]))
                vedge(v, v.tensor_scalar(STAT[:, 2:3], STAT[:, 0:1], -1.0 / D,
                                         None, ALU.mult))
                vedge(v, v.tensor_tensor(STAT[:, 3:4], STAT[:, 2:3], STAT[:, 2:3],
                                         ALU.mult))
                v.scalar_tensor_tensor(STAT[:, 4:5], STAT[:, 1:2], 1.0 / D,
                                       STAT[:, 3:4], ALU.mult,
                                       ALU.subtract).then_inc(S_["b2_eps"], 1)
                v.wait_ge(S_["b2_rstd"], NT + i + 1)
                v.tensor_scalar(XR[:, (i % 3) * 1024:(i % 3 + 1) * 1024], at,
                                STAT[:, 2:3], STAT[:, 6:7], ALU.add,
                                ALU.mult).then_inc(S_["b2_sc"], 1)

        @blk.scalar
        def _(s):
            for i in range(NT):
                s.wait_ge(S_["b2_eps"], NT + i + 1)
                sedge(s, s.activation(STAT[:, 5:6], STAT[:, 4:5], AF.Ln, bias=EPSC[:, 0:1]))
                s.activation(STAT[:, 6:7], STAT[:, 5:6], AF.Exp,
                             scale=-0.5).then_inc(S_["b2_rstd"], 1)

        @blk.sync
        def _(sy):
            for i in range(NT):
                sy.wait_ge(S_["b2_sc"], NT + i + 1)
                sy.dma_start(out_t.ap()[i * 128:(i + 1) * 128, :],
                             XR[:, (i % 3) * 1024:(i % 3 + 1) * 1024]).then_inc(S_["so%d" % (i % 3)], 16)
            for r in range(3):
                n_r = len([i for i in range(NT) if i % 3 == r])
                sy.wait_ge(S_["so%d" % r], 16 * n_r)

    return nc


def make_consts(T):
    pos = np.arange(T, dtype=np.float64)
    inv = 1.0 / (150000.0 ** (np.arange(32, dtype=np.float64) / 32.0))
    ang = pos[:, None] * inv[None, :]
    cosw = (np.cos(ang) * SCALE).astype(np.float32)
    sinw = (np.sin(ang) * SCALE).astype(np.float32)
    ql = np.arange(128)
    kl = np.arange(384)
    band = (kl[None, :] >= ql[:, None]) & (kl[None, :] <= ql[:, None] + 2 * SW)
    m_int = np.where(band, 0.0, -BIG).astype(np.float32)
    m_first = np.where(kl[None, :] <= ql[:, None] + SW, 0.0, -BIG).astype(np.float32)
    m_first[:, 256:384] = -BIG
    m_last = m_int.copy(); m_last[:, 256:384] = -BIG
    masks = np.stack([m_int, m_first, m_last]).astype(np.float32)
    ident = np.eye(128, dtype=np.float32)
    return cosw, sinw, masks, ident


_NC_CACHE = {}


def kernel(**inputs):
    NT = 16
    T = NT * 128
    if NT not in _NC_CACHE:
        _NC_CACHE[NT] = build_nc(NT)
    nc = _NC_CACHE[NT]
    cosw, sinw, masks, ident = make_consts(T)
    x = np.ascontiguousarray(inputs["L0_attn_hidden_states"])
    base = {
        "attn_norm_scale": inputs["L0_attn_attn_norm_scale"],
        "Wq": np.ascontiguousarray(inputs["L0_attn_Wq"].reshape(D, NQ * HD)),
        "bq": np.ascontiguousarray(inputs["L0_attn_bq"].reshape(NQ * HD)),
        "Wk": np.ascontiguousarray(inputs["L0_attn_Wk"].reshape(D, NKV * HD)),
        "bk": np.ascontiguousarray(inputs["L0_attn_bk"].reshape(NKV * HD)),
        "Wv": np.ascontiguousarray(inputs["L0_attn_Wv"].reshape(D, NKV * HD)),
        "bv": np.ascontiguousarray(inputs["L0_attn_bv"].reshape(NKV * HD)),
        "Wo": np.ascontiguousarray(inputs["L0_attn_Wo"]),
        "bo": inputs["L0_attn_bo"],
        "sinks": inputs["L0_attn_sinks"],
        "moe_norm_scale": inputs["L0_attn_moe_norm_scale"],
        "router_W": np.ascontiguousarray(inputs["L0_attn_router_W"]),
        "router_b": inputs["L0_attn_router_b"],
        "gate_up_proj": np.ascontiguousarray(inputs["L0_attn_gate_up_proj"]),
        "gate_up_bias": np.ascontiguousarray(inputs["L0_attn_gate_up_bias"]),
        "down_proj": np.ascontiguousarray(inputs["L0_attn_down_proj"]),
        "down_bias": np.ascontiguousarray(inputs["L0_attn_down_bias"]),
        "centroids": np.ascontiguousarray(inputs["L0_ffn_centroids"]),
        "routing_bias": inputs["L0_ffn_routing_bias"],
        "Wfc_r": np.ascontiguousarray(inputs["L0_ffn_Wfc_r"]),
        "Wproj_r": np.ascontiguousarray(inputs["L0_ffn_Wproj_r"]),
        "Wfc_sh": np.ascontiguousarray(inputs["L0_ffn_Wfc_sh"]),
        "Wproj_sh": np.ascontiguousarray(inputs["L0_ffn_Wproj_sh"]),
        "COS": cosw, "SIN": sinw, "MASKS": masks, "IDENT": ident,
    }
    base = {k: np.ascontiguousarray(np.asarray(v, dtype=np.float32))
            for k, v in base.items()}
    in_maps = []
    for c in range(8):
        m = dict(base)
        m["x_in"] = np.ascontiguousarray(x[c])
        in_maps.append(m)
    res = run_bass_kernel_spmd(nc, in_maps, core_ids=list(range(8)))
    return np.stack([r["out"] for r in res.results], axis=0)


# version 10
# speedup vs baseline: 1.1762x; 1.0308x over previous
import contextlib
import numpy as np
import concourse.bass as bass
import concourse.mybir as mybir
from concourse.bass_utils import run_bass_kernel_spmd

F32 = mybir.dt.float32
BF16 = mybir.dt.bfloat16
AF = mybir.ActivationFunctionType
ALU = mybir.AluOpType
AX = mybir.AxisListType

D = 1024
NQ, NKV, HD, SW = 16, 4, 64, 128
INTER = 1024
NE = 8
FF = 4096
NS = 2
EPS = 1e-5
LIMIT = 7.0
BIG = 1e9
SCALE = HD ** (-0.25)
NPASS = NS + NE


def head_of(pb, half):
    g2, j4 = divmod(pb, 4)
    return (2 * g2 + half) * 4 + j4


def build_nc(NT=16, debug=False):
    T = NT * 128
    Tq = 1024 if T >= 2048 else T
    n_half = T // Tq
    ntt_q = Tq // 128
    n_tc = T // 512
    nc = bass.Bass("TRN2", target_bir_lowering=False,
                   dynamic_dma_scratch_size=16384)
    dram = {}

    def din(name, shape):
        dram[name] = nc.dram_tensor(name, list(shape), F32, kind="ExternalInput")
        return dram[name]

    x_in = din("x_in", [T, D])
    din("attn_norm_scale", [D])
    din("Wq", [D, NQ * HD]); din("bq", [NQ * HD])
    din("Wk", [D, NKV * HD]); din("bk", [NKV * HD])
    din("Wv", [D, NKV * HD]); din("bv", [NKV * HD])
    din("Wo", [NQ, HD, D]); din("bo", [D])
    din("sinks", [NQ])
    din("moe_norm_scale", [D])
    din("router_W", [D, NE]); din("router_b", [NE])
    din("gate_up_proj", [NE, D, 2 * INTER]); din("gate_up_bias", [NE, 2 * INTER])
    din("down_proj", [NE, INTER, D]); din("down_bias", [NE, D])
    din("centroids", [NE, D]); din("routing_bias", [NE])
    din("WfcT", [NPASS, 16, 128, 8, 256])
    din("WpjT", [NPASS, 32, 128, 1024])
    din("COS", [T, 32]); din("SIN", [T, 32])
    din("MASKS", [3, 128, 384])
    din("IDENT", [128, 128])

    out_t = nc.dram_tensor("out", [T, D], F32, kind="ExternalOutput")
    dbg = {}
    if debug:
        for n in ["dbg_A1", "dbg_A3", "dbg_A4"]:
            dbg[n] = nc.dram_tensor(n, [T, D], F32, kind="ExternalOutput")
        dbg["dbg_SC"] = nc.dram_tensor("dbg_SC", [128, NT * 8], F32, kind="ExternalOutput")
        dbg["dbg_KFM"] = nc.dram_tensor("dbg_KFM", [128, 2 * T], F32, kind="ExternalOutput")
        dbg["dbg_VT"] = nc.dram_tensor("dbg_VT", [128, NT * 256], F32, kind="ExternalOutput")
        dbg["dbg_SM"] = nc.dram_tensor("dbg_SM", [128, 768], F32, kind="ExternalOutput")
        dbg["dbg_PB"] = nc.dram_tensor("dbg_PB", [128, 768], F32, kind="ExternalOutput")
        dbg["dbg_OF"] = nc.dram_tensor("dbg_OF", [128, 1024], F32, kind="ExternalOutput")
        dbg["dbg_CB"] = nc.dram_tensor("dbg_CB", [128, NT * 8], F32, kind="ExternalOutput")
    st_fcT = nc.dram_tensor("st_fcT", [NPASS, 16, 128, 8, 256], BF16, kind="Internal")
    st_pjT = nc.dram_tensor("st_pjT", [NPASS, 32, 128, 1024], BF16, kind="Internal")

    SB = nc.alloc_sbuf_tensor
    A = SB("A", [128, NT * 1024], F32)
    BFAC = 57344
    BFA = SB("BFA", [128, BFAC], BF16)

    KFM_o = 0
    VTOK_o = 2 * T
    QROPE_o = VTOK_o + NT * 256
    KROPE_o = QROPE_o + 1024
    QR_o = KROPE_o + 256
    OFMS_o = QR_o + 2048
    XHATS_o = OFMS_o + 1024
    XF1_o = XHATS_o + 2048
    DUMP_o = XF1_o + 2048
    PTS_o = DUMP_o + 1024
    WQKV_off = max(16 * T, PTS_o + 768)
    WO_off = WQKV_off + 12288
    assert WO_off + 8192 <= BFAC
    XF_o = 0
    GATED_o = 8 * T
    WG_o, WU_o, WD_o = WQKV_off, WQKV_off + 8192, WQKV_off + 16384
    assert WD_o + 8192 <= BFAC
    SC2_o = GATED_o
    HR_o = 8 * T
    WFC_o = HR_o + 32 * Tq
    WPJ_o = WFC_o + 4096
    assert WPJ_o + 4096 <= BFAC

    XR = SB("XR", [128, 3 * 1024], F32)
    if WQKV_off - (PTS_o + 768) >= 7680:
        B1S_o = PTS_o + 768
    else:
        B1S_o = WD_o + 8192
        assert BFAC - B1S_o >= 7680
    MASKS = BFA[:, B1S_o:B1S_o + 2304].bitcast(F32)
    COSB = BFA[:, B1S_o + 2304:B1S_o + 2304 + NT * 32]
    SINB = BFA[:, B1S_o + 2816:B1S_o + 2816 + NT * 32]
    SM = BFA[:, B1S_o + 3328:B1S_o + 4864].bitcast(F32)
    PBUF = BFA[:, B1S_o + 4864:B1S_o + 5632]
    G2T = BFA[:, B1S_o + 5632:B1S_o + 6656]
    assert B1S_o + 6656 <= BFAC
    IDENTB = SB("IDENTB", [128, 128], BF16)
    IDENTF = SB("IDENTF", [128, 128], F32)
    NEGSINK = SB("NEGSINK", [128, 16], F32)
    SINKB = SB("SINKB", [128, 16], F32)
    GUPB = SB("GUPB", [128, 128], F32)
    RBB = SB("RBB", [128, 8], F32)
    SCORES = SB("SCORES", [128, NT * 8], F32)
    COMB = SB("COMB", [128, NT * 8], F32)
    STAT = SB("STAT", [128, 48], F32)
    LSC = SB("LSC", [128, 32], F32)
    GTS = SB("GTS", [128, 2 * 512], BF16)
    GLUS = SB("GLUS", [128, 2 * 512], BF16)
    UP2S = SB("UP2S", [128, 2 * 512], BF16)
    ROWS = SB("ROWS", [1, 1544], BF16)
    BOROW = SB("BOROW", [1, 1024], BF16)
    DBROW = SB("DBROW", [1, 1024], BF16)
    SROW = SB("SROW", [1, 64], F32)
    G1C = SB("G1C", [128, 8], F32)
    ONESC = SB("ONESC", [1, 128], BF16)
    ONESCF = SB("ONESCF", [1, 128], F32)
    CENTT = SB("CENTT", [128, 64], BF16)
    ROUTW = SB("ROUTW", [128, 64], BF16)
    EPSC = SB("EPSC", [128, 1], F32)
    G2ROW = XR[32:33, 0:1024]
    SROW = XR[64:65, 0:64]
    CENTN = XR[0:8, 2048:2560].bitcast(BF16)
    ONESF32 = XR[32:33, 1024:1152]
    ONESF64 = XR[64:65, 64:192]

    names = ["sv_eps", "ss_rstd", "sv_xhat", "sp_trx", "ss_cpx", "sp_qkv", "sv_ropek", "sv_ropeq", "sv_ropeall", "sp_trq", "ss_cpq", "sp_trk", "ss_cpk", "sp_s", "sv_mask", "sv_negm", "sp_pt", "ss_pt", "ss_exp", "sv_pn", "sp_av", "ss_ofm", "sp_wo", "sv_evac", "s_wsmall", "s_const", "s_constg", "s_b0", "ev", "es", "b2_eps", "b2_rstd", "b2_xhat", "b2_trx", "b2_cpx", "b2_rt", "b2_sc", "b2_lsel", "b2_exp", "sw_g", "sw_u", "sw_d", "sp_g", "sp_u", "sv_gt", "sv_up", "ss_glu", "sv_gated", "sp_d", "sv_devac", "b4_eps", "b4_rstd", "b4_xhat", "b4_rt", "b4_sc", "b4_sig", "b4_rsel", "b4_exp", "sp_fc", "ss_hr", "sp_pj", "sv_pevac", "s_out", "s_dbg", "swp3", "sx0", "sx1", "sx2", "b4x0", "b4x1", "b4x2", "so0", "so1", "so2", "swf0", "swf1", "swp0", "swp1", "swp2", "s_stg0", "s_stg1", "s_stg2", "s_stg3", "s_stg4", "s_stg5", "s_stg6", "s_stg7", "s_stg8", "s_stg9"]
    S_ = {n: nc.alloc_semaphore(n) for n in names}

    ecnt = {"v": [0], "s": [0]}

    def vedge(v, ins):
        ecnt["v"][0] += 1
        ins.then_inc(S_["ev"], 1)
        v.wait_ge(S_["ev"], ecnt["v"][0])

    def sedge(s, ins):
        ecnt["s"][0] += 1
        ins.then_inc(S_["es"], 1)
        s.wait_ge(S_["es"], ecnt["s"][0])

    def mask_ap(j):
        if j == 0:
            return MASKS[:, 384:768]
        if j == NT - 1:
            return MASKS[:, 768:1152]
        return MASKS[:, 0:384]

    XHATS = lambda r: BFA[:, XHATS_o + r * 1024:XHATS_o + (r + 1) * 1024]
    XF1 = lambda r, dc: BFA[:, XF1_o + r * 1024 + dc * 128:XF1_o + r * 1024 + (dc + 1) * 128]
    QRS = lambda r, pb: BFA[:, QR_o + (pb * 2 + r) * 128:QR_o + (pb * 2 + r + 1) * 128]
    OFMS = lambda pb: BFA[:, OFMS_o + pb * 128:OFMS_o + (pb + 1) * 128]
    QROPE = BFA[:, QROPE_o:QROPE_o + 1024]
    KROPE = BFA[:, KROPE_o:KROPE_o + 256]
    DUMP = BFA[:, DUMP_o:DUMP_o + 1024]
    PTS = lambda r: BFA[:, PTS_o + r * 384:PTS_o + (r + 1) * 384]
    XH2 = lambda r: BFA[:, SC2_o + r * 1024:SC2_o + (r + 1) * 1024]
    DUMP2 = BFA[:, SC2_o + 2048:SC2_o + 3072]
    XF2t = lambda dc, i: BFA[:, XF_o + dc * T + i * 128:XF_o + dc * T + (i + 1) * 128]
    XF2c = lambda dc, c0, c1: BFA[:, XF_o + dc * T + c0:XF_o + dc * T + c1]
    GATED = lambda ic, c0, c1: BFA[:, GATED_o + ic * T + c0:GATED_o + ic * T + c1]
    HR = lambda fc, c0, c1: BFA[:, HR_o + fc * Tq + c0:HR_o + fc * Tq + c1]
    WFCs = lambda r: BFA[:, WFC_o + r * 2048:WFC_o + (r + 1) * 2048]
    WPJs = lambda r: BFA[:, WPJ_o + r * 1024:WPJ_o + (r + 1) * 1024]

    N_CONST = 16 * 14
    N_CONSTG = 16 * 3
    N_WSMALL = 16 * 26

    with nc.Block("b0") as blk:
        @blk.sync
        def _(sy):
            sy.dma_start(MASKS.rearrange("p (m c) -> p m c", m=3),
                         dram["MASKS"].ap().rearrange("m p c -> p m c")).then_inc(S_["s_const"], 16)
            sy.dma_start(SROW[0:1, 0:16], dram["sinks"].ap().unsqueeze(0)).then_inc(S_["s_const"], 16)
            sy.dma_start(SROW[0:1, 16:24], dram["routing_bias"].ap().unsqueeze(0)).then_inc(S_["s_const"], 16)
            sy.dma_start(G2ROW, dram["moe_norm_scale"].ap().unsqueeze(0)).then_inc(S_["s_const"], 16)
            sy.dma_start(IDENTF[:, :], dram["IDENT"].ap()).then_inc(S_["s_const"], 16)
            for c in range(8):
                sy.dma_start(G1C[:, c:c + 1],
                             dram["attn_norm_scale"].ap()[c * 128:(c + 1) * 128].unsqueeze(1)).then_inc(S_["s_const"], 16)
            sy.dma_start(XR[0:8, 0:2048], dram["gate_up_bias"].ap()).then_inc(S_["s_const"], 16)

        @blk.gpsimd
        def _(g):
            g.dma_start(IDENTB[:, :], dram["IDENT"].ap()).then_inc(S_["s_constg"], 16)
            g.dma_start(COSB[:, :].rearrange("p (n j) -> p n j", j=32),
                        dram["COS"].ap().rearrange("(n p) j -> p n j", p=128)).then_inc(S_["s_constg"], 16)
            g.dma_start(SINB[:, :].rearrange("p (n j) -> p n j", j=32),
                        dram["SIN"].ap().rearrange("(n p) j -> p n j", p=128)).then_inc(S_["s_constg"], 16)
            g.dma_start(ROWS[0:1, 0:1024], dram["bq"].ap().unsqueeze(0)).then_inc(S_["s_wsmall"], 16)
            g.dma_start(ROWS[0:1, 1024:1280], dram["bk"].ap().unsqueeze(0)).then_inc(S_["s_wsmall"], 16)
            g.dma_start(ROWS[0:1, 1280:1536], dram["bv"].ap().unsqueeze(0)).then_inc(S_["s_wsmall"], 16)
            g.dma_start(ROWS[0:1, 1536:1544], dram["router_b"].ap().unsqueeze(0)).then_inc(S_["s_wsmall"], 16)
            g.dma_start(BOROW[0:1, :], dram["bo"].ap().unsqueeze(0)).then_inc(S_["s_wsmall"], 16)
            wq_v = BFA[:, WQKV_off:WQKV_off + 12288].rearrange("p (c f) -> p c f", c=8)
            g.dma_start(wq_v[:, :, 0:1024],
                        dram["Wq"].ap().rearrange("(c p) f -> p c f", p=128)).then_inc(S_["s_wsmall"], 16)
            g.dma_start(wq_v[:, :, 1024:1280],
                        dram["Wk"].ap().rearrange("(c p) f -> p c f", p=128)).then_inc(S_["s_wsmall"], 16)
            g.dma_start(wq_v[:, :, 1280:1536],
                        dram["Wv"].ap().rearrange("(c p) f -> p c f", p=128)).then_inc(S_["s_wsmall"], 16)
            for pb in range(8):
                for half in range(2):
                    h = head_of(pb, half)
                    g.dma_start(BFA[half * 64:half * 64 + 64,
                                    WO_off + pb * 1024:WO_off + (pb + 1) * 1024],
                                dram["Wo"].ap()[h, :, :]).then_inc(S_["s_wsmall"], 16)
            g.dma_start(ROUTW[:, :].rearrange("p (c e) -> p c e", c=8),
                        dram["router_W"].ap().rearrange("(c p) e -> p c e", p=128)).then_inc(S_["s_wsmall"], 16)
            g.dma_start(CENTN, dram["centroids"].ap()).then_inc(S_["s_wsmall"], 16)

        @blk.vector
        def _(v):
            v.memset(ONESC[:, :], 1.0)
            v.memset(ONESCF[:, :], 1.0)
            v.memset(ONESF32, 1.0)
            v.memset(ONESF64, 1.0)
            v.memset(EPSC[:, :], EPS)

    with contextlib.ExitStack() as ps:
        P_B0 = ps.enter_context(nc.psum_tensor("P_B0", [128, 128], F32))
        P_B0B = ps.enter_context(nc.psum_tensor("P_B0B", [128, 64], BF16))
        with nc.Block("b0b") as blk:
            @blk.tensor
            def _(t):
                t.wait_ge(S_["s_const"], N_CONST)
                t.wait_ge(S_["s_constg"], N_CONSTG)
                t.matmul(P_B0[:, 0:16], ONESF64, SROW[0:1, 0:16],
                         start=True, stop=True)
                t.matmul(P_B0[:, 16:24], ONESF64, SROW[0:1, 16:24],
                         start=True, stop=True).then_inc(S_["s_b0"], 1)

            @blk.vector
            def _(v):
                v.wait_ge(S_["s_b0"], 1)
                v.tensor_copy(SINKB[:, :], P_B0[:, 0:16])
                v.tensor_scalar(NEGSINK[:, :], P_B0[:, 0:16], -1.0, None, ALU.mult)
                v.tensor_copy(RBB[:, :], P_B0[:, 16:24])
                v.engine_nop().then_inc(S_["s_b0"], 1)

        with nc.Block("b0c") as blk:
            @blk.tensor
            def _(t):
                t.wait_ge(S_["s_b0"], 2)
                for c in range(16):
                    ins = t.transpose(P_B0[:, c * 8:(c + 1) * 8],
                                      XR[0:8, c * 128:(c + 1) * 128],
                                      IDENTF[0:8, 0:8])
                ins.then_inc(S_["s_b0"], 1)
                t.wait_ge(S_["s_wsmall"], N_WSMALL)
                for c in range(8):
                    ins = t.transpose(P_B0B[:, c * 8:(c + 1) * 8],
                                      CENTN[0:8, c * 128:(c + 1) * 128],
                                      IDENTB[0:8, 0:8])
                ins.then_inc(S_["s_b0"], 1)

            @blk.vector
            def _(v):
                v.wait_ge(S_["s_b0"], 3)
                v.tensor_copy(GUPB[:, :], P_B0[:, 0:128])
                v.wait_ge(S_["s_b0"], 4)
                v.tensor_copy(CENTT[:, :], P_B0B[:, :])
                v.engine_nop().then_inc(S_["s_b0"], 1)
                v.engine_nop().then_inc(S_["s_b0"], 1)

        with nc.Block("b0d") as blk:
            @blk.tensor
            def _(t):
                for c in range(8):
                    t.wait_ge(S_["s_b0"], 6 + 2 * c)
                    t.matmul(P_B0[:, :], ONESF32,
                             G2ROW[0:1, c * 128:(c + 1) * 128],
                             start=True, stop=True).then_inc(S_["s_b0"], 1)

            @blk.vector
            def _(v):
                for c in range(8):
                    v.wait_ge(S_["s_b0"], 7 + 2 * c)
                    v.tensor_copy(G2T[:, c * 128:(c + 1) * 128], P_B0[:, :])
                    v.engine_nop().then_inc(S_["s_b0"], 1)

        with nc.Block("b0e") as blk:
            @blk.vector
            def _(v):
                v.wait_ge(S_["s_wsmall"], N_WSMALL)
                v.wait_ge(S_["s_const"], N_CONST)
                v.wait_ge(S_["s_constg"], N_CONSTG)
                for dc in range(8):
                    ins = v.tensor_scalar(
                        BFA[:, WQKV_off + dc * 1536:WQKV_off + (dc + 1) * 1536],
                        BFA[:, WQKV_off + dc * 1536:WQKV_off + (dc + 1) * 1536],
                        G1C[:, dc:dc + 1], None, ALU.mult)
                ins.then_inc(S_["s_b0"], 1)
    N_B0 = 23

    with contextlib.ExitStack() as ps:
        P_QKV = ps.enter_context(nc.psum_tensor("P_QKV", [128, 1536], F32))
        P_TR = ps.enter_context(nc.psum_tensor("P_TR", [128, 1024], BF16))
        P_S = ps.enter_context(nc.psum_tensor("P_S", [128, 384], F32))
        P_PTK = ps.enter_context(nc.psum_tensor("P_PTK", [128, 1024], BF16))
        P_AV = ps.enter_context(nc.psum_tensor("P_AV", [128, 128], F32))
        P_WO = ps.enter_context(nc.psum_tensor("P_WO", [128, 512], F32))

        with nc.Block("b1") as blk:
            @blk.sync
            def _(sy):
                for i in range(NT):
                    if i >= 3:
                        sy.wait_ge(S_["sv_evac"], 2 * (i - 3) + 2)
                    sy.dma_start(XR[:, (i % 3) * 1024:(i % 3 + 1) * 1024],
                                 x_in.ap()[i * 128:(i + 1) * 128, :]).then_inc(S_["sx%d" % (i % 3)], 16)

            @blk.gpsimd
            def _(g):
                def stage(k):
                    p, kind = divmod(k, 2)
                    if kind == 0:
                        g.dma_start(st_fcT.ap()[p], dram["WfcT"].ap()[p]).then_inc(S_["s_stg%d" % p], 16)
                    else:
                        g.dma_start(st_pjT.ap()[p], dram["WpjT"].ap()[p]).then_inc(S_["s_stg%d" % p], 16)
                for k in range(2 * NPASS):
                    stage(k)
                g.wait_ge(S_["sp_qkv"], NT)
                gv = BFA[:, WG_o:WG_o + 8192].rearrange("p (c i) -> p c i", c=8)
                g.dma_start(gv, dram["gate_up_proj"].ap()[0].rearrange("(c p) i -> p c i", p=128)[:, :, 0:1024]).then_inc(S_["sw_g"], 16)
                g.wait_ge(S_["sp_wo"], 2 * NT)
                uv = BFA[:, WU_o:WU_o + 8192].rearrange("p (c i) -> p c i", c=8)
                g.dma_start(uv, dram["gate_up_proj"].ap()[0].rearrange("(c p) i -> p c i", p=128)[:, :, 1024:2048]).then_inc(S_["sw_u"], 16)
                dv = BFA[:, WD_o:WD_o + 8192].rearrange("p (c m) -> p c m", c=8)
                g.dma_start(dv, dram["down_proj"].ap()[0].rearrange("(c p) m -> p c m", p=128)).then_inc(S_["sw_d"], 16)
                g.dma_start(DBROW[0:1, :], dram["down_bias"].ap()[0].unsqueeze(0)).then_inc(S_["sw_d"], 16)

            @blk.tensor
            def _(t):
                t.wait_ge(S_["s_wsmall"], N_WSMALL)
                t.wait_ge(S_["s_b0"], N_B0)
                for i in range(NT + 1):
                    j = i - 1
                    if i < NT:
                        t.wait_ge(S_["sv_xhat"], i + 1)
                        if i >= 1:
                            t.wait_ge(S_["ss_cpq"], i)
                        for dc in range(8):
                            ins = t.transpose(P_TR[:, dc * 128:(dc + 1) * 128],
                                              XHATS(i % 2)[:, dc * 128:(dc + 1) * 128],
                                              IDENTB[:, :])
                        ins.then_inc(S_["sp_trx"], 1)
                        t.wait_ge(S_["ss_cpx"], i + 1)
                        if i >= 1:
                            t.wait_ge(S_["sv_ropeall"], i)
                        for nb in range(3):
                            t.matmul(P_QKV[:, nb * 512:(nb + 1) * 512], ONESC[0:1, :],
                                     ROWS[0:1, nb * 512:(nb + 1) * 512],
                                     start=True, stop=False)
                        for dc in range(8):
                            for nb in range(3):
                                ins = t.matmul(P_QKV[:, nb * 512:(nb + 1) * 512],
                                               XF1(i % 2, dc),
                                               BFA[:, WQKV_off + dc * 1536 + nb * 512:
                                                   WQKV_off + dc * 1536 + (nb + 1) * 512],
                                               start=False, stop=(dc == 7))
                        ins.then_inc(S_["sp_qkv"], 1)
                        t.wait_ge(S_["sv_ropek"], i + 1)
                        if i >= 1:
                            t.wait_ge(S_["ss_cpk"], i)
                        t.transpose(P_TR[:, 0:128], KROPE[:, 0:128], IDENTB[:, :])
                        t.transpose(P_TR[:, 128:256], KROPE[:, 128:256], IDENTB[:, :]).then_inc(S_["sp_trk"], 1)

                    if j >= 0:
                        KB = 0 if j == 0 else j - 1
                        nk = 2 if (j == 0 or j == NT - 1) else 3

                        def smm(hh):
                            pb2, half2 = divmod(hh - j * 16, 2)
                            h2 = head_of(pb2, half2)
                            g2_ = (h2 // 4) // 2
                            t.wait_ge(S_["ss_cpq"], j + 1)
                            t.wait_ge(S_["ss_cpk"], min(j + 2, NT))
                            t.wait_ge(S_["sv_ropeall"], min(j + 2, NT))
                            if hh > 0:
                                t.wait_ge(S_["sv_mask"], hh)
                            b = half2 * 64
                            t.matmul(P_S[:, 0:nk * 128],
                                     QRS(j % 2, pb2)[b:b + 64, :],
                                     BFA[b:b + 64,
                                         KFM_o + g2_ * T + KB * 128:
                                         KFM_o + g2_ * T + (KB + nk) * 128],
                                     start=True, stop=True).then_inc(S_["sp_s"], 1)

                        smm(j * 16)
                        for hl in range(16):
                            pb, half = divmod(hl, 2)
                            h = head_of(pb, half)
                            gk = h // 4
                            hh = j * 16 + hl
                            if hl + 1 < 16:
                                smm(hh + 1)
                            t.wait_ge(S_["sv_pn"], hh + 1)
                            if hh >= 1:
                                t.wait_ge(S_["ss_pt"], hh)
                            for k in range(nk):
                                ins = t.transpose(P_PTK[:, k * 128:(k + 1) * 128],
                                                  PBUF[:, (hh % 2) * 384 + k * 128:
                                                       (hh % 2) * 384 + (k + 1) * 128],
                                                  IDENTB[:, :])
                            ins.then_inc(S_["sp_pt"], 1)
                            if half == 0:
                                t.wait_ge(S_["ss_ofm"], j * 8 + pb)
                            t.wait_ge(S_["ss_pt"], hh + 1)
                            first = True
                            for k in range(nk):
                                ktg = KB + k
                                outv = P_AV[64:128, :] if half == 1 else P_AV[0:64, :]
                                tp = (0, 64) if half == 1 else None
                                ins = t.matmul(outv,
                                               BFA[:, VTOK_o + ktg * 256 + gk * 64:
                                                   VTOK_o + ktg * 256 + (gk + 1) * 64],
                                               PTS(hh % 2)[:, k * 128:(k + 1) * 128],
                                               start=first, stop=(k == nk - 1),
                                               tile_position=tp)
                                first = False
                            ins.then_inc(S_["sp_av"], 1)
                        t.wait_ge(S_["ss_ofm"], 8 * (j + 1))
                        for mh in range(2):
                            if 2 * j + mh >= 1:
                                t.wait_ge(S_["sv_evac"], 2 * j + mh)
                            t.matmul(P_WO[:, :], ONESC[0:1, :],
                                     BOROW[0:1, mh * 512:(mh + 1) * 512],
                                     start=True, stop=False)
                            for pb in range(8):
                                ins = t.matmul(P_WO[:, :], OFMS(pb),
                                               BFA[:, WO_off + pb * 1024 + mh * 512:
                                                   WO_off + pb * 1024 + (mh + 1) * 512],
                                               start=False, stop=(pb == 7))
                            ins.then_inc(S_["sp_wo"], 1)

                    if i < NT:
                        t.wait_ge(S_["sv_ropeq"], i + 1)
                        t.wait_ge(S_["ss_cpx"], i + 1)
                        t.wait_ge(S_["ss_cpk"], i + 1)
                        for pb in range(8):
                            ins = t.transpose(P_TR[:, pb * 128:(pb + 1) * 128],
                                              QROPE[:, pb * 128:(pb + 1) * 128],
                                              IDENTB[:, :])
                        ins.then_inc(S_["sp_trq"], 1)

            @blk.vector
            def _(v):
                for i in range(NT + 1):
                    j = i - 1
                    if i < NT:
                        xr = XR[:, (i % 3) * 1024:(i % 3 + 1) * 1024]
                        v.wait_ge(S_["sx%d" % (i % 3)], 16 * (i // 3 + 1))
                        if i >= 1:
                            v.wait_ge(S_["sv_ropeq"], i)
                        vedge(v, v.scalar_tensor_tensor(DUMP[:, :], xr, 1.0, xr,
                                                        ALU.bypass, ALU.mult,
                                                        accum_out=STAT[:, 0:1]))
                        v.tensor_scalar(STAT[:, 1:2], STAT[:, 0:1], 1.0 / D, EPS,
                                        ALU.mult, ALU.add).then_inc(S_["sv_eps"], 1)
                        v.wait_ge(S_["ss_rstd"], i + 1)
                        if i >= 2:
                            v.wait_ge(S_["sp_trx"], i - 1)
                        v.tensor_scalar(XHATS(i % 2), xr, STAT[:, 2:3], None,
                                        ALU.mult).then_inc(S_["sv_xhat"], 1)
                        v.wait_ge(S_["sp_qkv"], i + 1)
                        if i >= 1:
                            v.wait_ge(S_["sp_trq"], i)
                            v.wait_ge(S_["sp_trk"], i)
                        cosb = COSB[:, i * 32:(i + 1) * 32]
                        sinb = SINB[:, i * 32:(i + 1) * 32]
                        ksrc = P_QKV[:, 1024:1280].rearrange(
                            "p (gk hk dd two) -> p gk hk dd two", gk=2, hk=2, two=2)
                        kxe = ksrc[:, :, :, :, 0:1].squeeze(4)
                        kxo = ksrc[:, :, :, :, 1:2].squeeze(4)
                        kcos = cosb.unsqueeze(1).unsqueeze(1).broadcast_to([128, 2, 2, 32])
                        ksin = sinb.unsqueeze(1).unsqueeze(1).broadcast_to([128, 2, 2, 32])
                        kdst = KROPE[:, :].rearrange("p (gk hk ha dd) -> p gk hk ha dd",
                                                     gk=2, hk=2, ha=2)
                        kde = kdst[:, :, :, 0:1, :].squeeze(3)
                        kdo = kdst[:, :, :, 1:2, :].squeeze(3)
                        kt_ = DUMP[:, 0:128].rearrange("p (gk hk dd) -> p gk hk dd",
                                                       gk=2, hk=2)
                        vedge(v, v.tensor_tensor(kde, kxe, kcos, ALU.mult))
                        vedge(v, v.tensor_tensor(kt_, kxo, ksin, ALU.mult))
                        vedge(v, v.tensor_tensor(kde, kde, kt_, ALU.subtract))
                        vedge(v, v.tensor_tensor(kdo, kxo, kcos, ALU.mult))
                        vedge(v, v.tensor_tensor(kt_, kxe, ksin, ALU.mult))
                        v.tensor_tensor(kdo, kdo, kt_, ALU.add).then_inc(S_["sv_ropek"], 1)
                        qcos = cosb.unsqueeze(1).unsqueeze(1).broadcast_to([128, 2, 4, 32])
                        qsin = sinb.unsqueeze(1).unsqueeze(1).broadcast_to([128, 2, 4, 32])
                        for g2i in range(2):
                            qsrc = P_QKV[:, g2i * 512:(g2i + 1) * 512].rearrange(
                                "p (half j4 dd two) -> p half j4 dd two",
                                half=2, j4=4, two=2)
                            qxe = qsrc[:, :, :, :, 0:1].squeeze(4)
                            qxo = qsrc[:, :, :, :, 1:2].squeeze(4)
                            qdst = QROPE[:, g2i * 512:(g2i + 1) * 512].rearrange(
                                "p (j4 half ha dd) -> p j4 half ha dd",
                                j4=4, half=2, ha=2)
                            qde = qdst[:, :, :, 0:1, :].squeeze(3).transpose([0, 2, 1, 3])
                            qdo = qdst[:, :, :, 1:2, :].squeeze(3).transpose([0, 2, 1, 3])
                            qt_ = DUMP[:, 0:256].rearrange(
                                "p (half j4 dd) -> p half j4 dd", half=2, j4=4)
                            vedge(v, v.tensor_tensor(qde, qxe, qcos, ALU.mult))
                            vedge(v, v.tensor_tensor(qt_, qxo, qsin, ALU.mult))
                            vedge(v, v.tensor_tensor(qde, qde, qt_, ALU.subtract))
                            vedge(v, v.tensor_tensor(qdo, qxo, qcos, ALU.mult))
                            vedge(v, v.tensor_tensor(qt_, qxe, qsin, ALU.mult))
                            ins = v.tensor_tensor(qdo, qdo, qt_, ALU.add)
                            if g2i == 0:
                                vedge(v, ins)
                        ins.then_inc(S_["sv_ropeq"], 1)
                        v.tensor_copy(BFA[:, VTOK_o + i * 256:VTOK_o + (i + 1) * 256],
                                      P_QKV[:, 1280:1536]).then_inc(S_["sv_ropeall"], 1)
                    if j >= 0:
                        nk = 2 if (j == 0 or j == NT - 1) else 3
                        c0, c1 = 0, nk * 128
                        for hl in range(16):
                            pb, half = divmod(hl, 2)
                            h = head_of(pb, half)
                            hh = j * 16 + hl
                            r = hh % 2
                            smr = SM[:, r * 384 + c0:r * 384 + c1]
                            v.wait_ge(S_["sp_s"], hh + 1)
                            v.tensor_tensor(smr, P_S[:, c0:c1], mask_ap(j)[:, c0:c1],
                                            ALU.add).then_inc(S_["sv_mask"], 1)
                            v.wait_ge(S_["sv_mask"], hh + 1)
                            vedge(v, v.reduce_max(STAT[:, 4 + r:5 + r], smr,
                                                  axis=AX.X, negate=True))
                            v.tensor_scalar(STAT[:, 6 + r:7 + r], STAT[:, 4 + r:5 + r],
                                            NEGSINK[:, h:h + 1], None,
                                            ALU.min).then_inc(S_["sv_negm"], 1)
                            v.wait_ge(S_["ss_exp"], hh + 1)
                            vedge(v, v.tensor_tensor(STAT[:, 10 + r:11 + r],
                                                     STAT[:, 8 + r:9 + r],
                                                     STAT[:, 12 + r:13 + r], ALU.add))
                            vedge(v, v.reciprocal(STAT[:, 14 + r:15 + r],
                                                  STAT[:, 10 + r:11 + r]))
                            pbv = PBUF[:, r * 384 + c0:r * 384 + c1]
                            v.tensor_scalar(pbv, pbv, STAT[:, 14 + r:15 + r], None,
                                            ALU.mult).then_inc(S_["sv_pn"], 1)
                        for mh in range(2):
                            v.wait_ge(S_["sp_wo"], 2 * j + mh + 1)
                            v.tensor_tensor(A[:, j * 1024 + mh * 512:j * 1024 + (mh + 1) * 512],
                                            P_WO[:, :],
                                            XR[:, (j % 3) * 1024 + mh * 512:
                                               (j % 3) * 1024 + (mh + 1) * 512],
                                            ALU.add).then_inc(S_["sv_evac"], 1)

            @blk.scalar
            def _(s):
                for i in range(NT + 1):
                    j = i - 1
                    if i < NT:
                        s.wait_ge(S_["sv_eps"], i + 1)
                        sedge(s, s.activation(STAT[:, 3:4], STAT[:, 1:2], AF.Ln))
                        s.activation(STAT[:, 2:3], STAT[:, 3:4], AF.Exp,
                                     scale=-0.5).then_inc(S_["ss_rstd"], 1)
                        s.wait_ge(S_["sp_trx"], i + 1)
                        for dc in range(8):
                            ins = s.copy(XF1(i % 2, dc), P_TR[:, dc * 128:(dc + 1) * 128])
                        ins.then_inc(S_["ss_cpx"], 1)
                        s.wait_ge(S_["sp_trk"], i + 1)
                        s.copy(BFA[:, KFM_o + i * 128:KFM_o + (i + 1) * 128],
                               P_TR[:, 0:128])
                        s.copy(BFA[:, KFM_o + T + i * 128:KFM_o + T + (i + 1) * 128],
                               P_TR[:, 128:256]).then_inc(S_["ss_cpk"], 1)
                    if j >= 0:
                        for hl in range(16):
                            pb, half = divmod(hl, 2)
                            h = head_of(pb, half)
                            hh = j * 16 + hl
                            r = hh % 2
                            nk = 2 if (j == 0 or j == NT - 1) else 3
                            s.wait_ge(S_["sv_negm"], hh + 1)
                            if hh >= 2:
                                s.wait_ge(S_["sp_pt"], hh - 1)
                            kc0, kc1 = 0, nk * 128
                            s.activation(PBUF[:, r * 384 + kc0:r * 384 + kc1],
                                         SM[:, r * 384 + kc0:r * 384 + kc1], AF.Exp,
                                         bias=STAT[:, 6 + r:7 + r],
                                         accum_out=STAT[:, 8 + r:9 + r])
                            s.activation(STAT[:, 12 + r:13 + r], SINKB[:, h:h + 1],
                                         AF.Exp, bias=STAT[:, 6 + r:7 + r]).then_inc(S_["ss_exp"], 1)
                            s.wait_ge(S_["sp_pt"], hh + 1)
                            for k in range(nk):
                                ins = s.copy(PTS(r)[:, k * 128:(k + 1) * 128],
                                             P_PTK[:, k * 128:(k + 1) * 128])
                            ins.then_inc(S_["ss_pt"], 1)
                            if half == 1:
                                s.wait_ge(S_["sp_av"], hh + 1)
                                s.copy(OFMS(pb), P_AV[:, :]).then_inc(S_["ss_ofm"], 1)
                    if i < NT:
                        s.wait_ge(S_["sp_trq"], i + 1)
                        for pb in range(8):
                            ins = s.copy(QRS(i % 2, pb), P_TR[:, pb * 128:(pb + 1) * 128])
                        ins.then_inc(S_["ss_cpq"], 1)

    with contextlib.ExitStack() as ps:
        P_TR2 = ps.enter_context(nc.psum_tensor("P_TR2", [128, 1024], BF16))
        P_RT = ps.enter_context(nc.psum_tensor("P_RT", [128, 8], F32))
        with nc.Block("b2") as blk:
            if debug:
                @blk.sync
                def _(sy):
                    for i in range(NT):
                        sy.dma_start(dbg["dbg_A1"].ap()[i * 128:(i + 1) * 128, :],
                                     A[:, i * 1024:(i + 1) * 1024]).then_inc(S_["s_out"], 16)
                    sy.dma_start(dbg["dbg_SM"].ap(), SM).then_inc(S_["s_out"], 16)
                    sy.wait_ge(S_["s_out"], 16 * (NT + 1))

                @blk.gpsimd
                def _(g):
                    g.dma_start(dbg["dbg_KFM"].ap(), BFA[:, KFM_o:KFM_o + 2 * T]).then_inc(S_["s_dbg"], 16)
                    g.dma_start(dbg["dbg_VT"].ap(), BFA[:, VTOK_o:VTOK_o + NT * 256]).then_inc(S_["s_dbg"], 16)
                    g.dma_start(dbg["dbg_PB"].ap(), PBUF).then_inc(S_["s_dbg"], 16)
                    g.dma_start(dbg["dbg_OF"].ap(), BFA[:, OFMS_o:OFMS_o + 1024]).then_inc(S_["s_dbg"], 16)
                    g.wait_ge(S_["s_dbg"], 64)

            @blk.vector
            def _(v):
                for i in range(NT):
                    at = A[:, i * 1024:(i + 1) * 1024]
                    vedge(v, v.scalar_tensor_tensor(DUMP2[:, :], at, 1.0, at,
                                                    ALU.bypass, ALU.mult,
                                                    accum_out=STAT[:, 0:1]))
                    v.tensor_scalar(STAT[:, 1:2], STAT[:, 0:1], 1.0 / D, EPS,
                                    ALU.mult, ALU.add).then_inc(S_["b2_eps"], 1)
                    v.wait_ge(S_["b2_rstd"], i + 1)
                    if i >= 2:
                        v.wait_ge(S_["b2_trx"], i - 1)
                    v.scalar_tensor_tensor(XH2(i % 2), at, STAT[:, 2:3], G2T,
                                           ALU.mult, ALU.mult).then_inc(S_["b2_xhat"], 1)
                    v.wait_ge(S_["b2_rt"], i + 1)
                    v.tensor_copy(LSC[:, 0:8], P_RT[:, :]).then_inc(S_["b2_sc"], 1)
                    v.wait_ge(S_["b2_sc"], i + 1)
                    vedge(v, v.tensor_copy(LSC[:, 8:16], LSC[:, 0:8]))
                    for it in range(4):
                        vedge(v, v.reduce_max(STAT[:, 4:5], LSC[:, 8:16], axis=AX.X,
                                              negate=True))
                        if it == 0:
                            vedge(v, v.tensor_copy(STAT[:, 5:6], STAT[:, 4:5]))
                        if it < 3:
                            vedge(v, v.tensor_scalar(STAT[:, 6:7], STAT[:, 4:5],
                                                     -1.0, None, ALU.mult))
                            vedge(v, v.tensor_scalar(LSC[:, 16:24], LSC[:, 8:16],
                                                     STAT[:, 6:7], None, ALU.is_ge))
                            vedge(v, v.scalar_tensor_tensor(LSC[:, 8:16], LSC[:, 16:24],
                                                            -BIG, LSC[:, 8:16],
                                                            ALU.mult, ALU.add))
                    vedge(v, v.tensor_scalar(STAT[:, 6:7], STAT[:, 4:5], -1.0, None,
                                             ALU.mult))
                    vedge(v, v.tensor_scalar(LSC[:, 16:24], LSC[:, 0:8], STAT[:, 6:7],
                                             None, ALU.is_ge))
                    vedge(v, v.scalar_tensor_tensor(LSC[:, 24:32], LSC[:, 16:24], BIG,
                                                    LSC[:, 0:8], ALU.mult, ALU.add))
                    v.tensor_scalar(LSC[:, 24:32], LSC[:, 24:32], -BIG, None,
                                    ALU.add).then_inc(S_["b2_lsel"], 1)
                    v.wait_ge(S_["b2_exp"], i + 1)
                    vedge(v, v.reciprocal(STAT[:, 7:8], STAT[:, 9:10]))
                    v.tensor_scalar(SCORES[:, i * 8:(i + 1) * 8], LSC[:, 16:24],
                                    STAT[:, 7:8], 0.25, ALU.mult, ALU.mult)

            @blk.scalar
            def _(s):
                if debug:
                    s.wait_ge(S_["s_dbg"], 64)
                for i in range(NT):
                    s.wait_ge(S_["b2_eps"], i + 1)
                    sedge(s, s.activation(STAT[:, 3:4], STAT[:, 1:2], AF.Ln))
                    s.activation(STAT[:, 2:3], STAT[:, 3:4], AF.Exp,
                                 scale=-0.5).then_inc(S_["b2_rstd"], 1)
                    s.wait_ge(S_["b2_trx"], i + 1)
                    for dc in range(8):
                        ins = s.copy(XF2t(dc, i), P_TR2[:, dc * 128:(dc + 1) * 128])
                    ins.then_inc(S_["b2_cpx"], 1)
                    s.wait_ge(S_["b2_lsel"], i + 1)
                    s.activation(LSC[:, 16:24], LSC[:, 24:32], AF.Exp,
                                 bias=STAT[:, 5:6],
                                 accum_out=STAT[:, 9:10]).then_inc(S_["b2_exp"], 1)

            @blk.tensor
            def _(t):
                for i in range(NT):
                    t.wait_ge(S_["b2_xhat"], i + 1)
                    if i >= 1:
                        t.wait_ge(S_["b2_cpx"], i)
                    for dc in range(8):
                        ins = t.transpose(P_TR2[:, dc * 128:(dc + 1) * 128],
                                          XH2(i % 2)[:, dc * 128:(dc + 1) * 128],
                                          IDENTB[:, :])
                    ins.then_inc(S_["b2_trx"], 1)
                    t.wait_ge(S_["b2_cpx"], i + 1)
                    t.wait_ge(S_["b2_sc"], i)
                    t.matmul(P_RT[:, :], ONESC[0:1, :], ROWS[0:1, 1536:1544],
                             start=True, stop=False)
                    for dc in range(8):
                        ins = t.matmul(P_RT[:, :], XF2t(dc, i),
                                       ROUTW[:, dc * 8:(dc + 1) * 8],
                                       start=False, stop=(dc == 7))
                    ins.then_inc(S_["b2_rt"], 1)

    with contextlib.ExitStack() as ps:
        P_G = ps.enter_context(nc.psum_tensor("P_G", [128, 2048], F32))
        P_U = ps.enter_context(nc.psum_tensor("P_U", [128, 2048], F32))
        with nc.Block("b3") as blk:
            @blk.gpsimd
            def _(g):
                for e in range(1, NE):
                    g.wait_ge(S_["sp_g"], 8 * e)
                    gv = BFA[:, WG_o:WG_o + 8192].rearrange("p (c i) -> p c i", c=8)
                    g.dma_start(gv, dram["gate_up_proj"].ap()[e].rearrange("(c p) i -> p c i", p=128)[:, :, 0:1024]).then_inc(S_["sw_g"], 16)
                    g.wait_ge(S_["sp_u"], 8 * e)
                    uv = BFA[:, WU_o:WU_o + 8192].rearrange("p (c i) -> p c i", c=8)
                    g.dma_start(uv, dram["gate_up_proj"].ap()[e].rearrange("(c p) i -> p c i", p=128)[:, :, 1024:2048]).then_inc(S_["sw_u"], 16)
                    g.wait_ge(S_["sp_d"], 2 * NT * e)
                    dv = BFA[:, WD_o:WD_o + 8192].rearrange("p (c m) -> p c m", c=8)
                    g.dma_start(dv, dram["down_proj"].ap()[e].rearrange("(c p) m -> p c m", p=128)).then_inc(S_["sw_d"], 16)
                    g.dma_start(DBROW[0:1, :], dram["down_bias"].ap()[e].unsqueeze(0)).then_inc(S_["sw_d"], 16)

            @blk.tensor
            def _(t):
                for e in range(NE):
                    t.wait_ge(S_["sw_g"], 16 * (e + 1))
                    t.wait_ge(S_["sv_devac"], 2 * NT * e)
                    for ic in range(8):
                        idx = e * 8 + ic
                        t.wait_ge(S_["sv_gt"], n_tc * idx)
                        for dc in range(8):
                            for tc in range(n_tc):
                                ins = t.matmul(P_G[:, tc * 512:(tc + 1) * 512],
                                               BFA[:, WG_o + dc * 1024 + ic * 128:
                                                   WG_o + dc * 1024 + (ic + 1) * 128],
                                               XF2c(dc, tc * 512, (tc + 1) * 512),
                                               start=(dc == 0), stop=(dc == 7))
                        ins.then_inc(S_["sp_g"], 1)
                        t.wait_ge(S_["sw_u"], 16 * (e + 1))
                        t.wait_ge(S_["sv_up"], n_tc * idx)
                        for dc in range(8):
                            for tc in range(n_tc):
                                ins = t.matmul(P_U[:, tc * 512:(tc + 1) * 512],
                                               BFA[:, WU_o + dc * 1024 + ic * 128:
                                                   WU_o + dc * 1024 + (ic + 1) * 128],
                                               XF2c(dc, tc * 512, (tc + 1) * 512),
                                               start=(dc == 0), stop=(dc == 7))
                        ins.then_inc(S_["sp_u"], 1)
                    t.wait_ge(S_["sv_gated"], 8 * n_tc * (e + 1))
                    t.wait_ge(S_["sw_d"], 32 * (e + 1))
                    for tt in range(NT):
                        t.wait_ge(S_["sv_devac"], 2 * (e * NT + tt))
                        for mh in range(2):
                            t.matmul(P_G[:, mh * 512:(mh + 1) * 512], ONESC[0:1, :],
                                     DBROW[0:1, mh * 512:(mh + 1) * 512],
                                     start=True, stop=False)
                            for ic in range(8):
                                ins = t.matmul(P_G[:, mh * 512:(mh + 1) * 512],
                                               GATED(ic, tt * 128, (tt + 1) * 128),
                                               BFA[:, WD_o + ic * 1024 + mh * 512:
                                                   WD_o + ic * 1024 + (mh + 1) * 512],
                                               start=False, stop=(ic == 7))
                            ins.then_inc(S_["sp_d"], 1)

            @blk.vector
            def _(v):
                for e in range(NE):
                    for ic in range(8):
                        idx = e * 8 + ic
                        for tc in range(n_tc):
                            r = (idx * n_tc + tc) % 2
                            v.wait_ge(S_["sp_g"], idx + 1)
                            if e >= 1:
                                v.wait_ge(S_["sp_d"], 2 * NT * e)
                            gb = GUPB[:, ic * 8 + e:ic * 8 + e + 1]
                            ub = GUPB[:, 64 + ic * 8 + e:64 + ic * 8 + e + 1]
                            v.tensor_scalar(GTS[:, r * 512:(r + 1) * 512],
                                            P_G[:, tc * 512:(tc + 1) * 512],
                                            gb, LIMIT, ALU.add,
                                            ALU.min).then_inc(S_["sv_gt"], 1)
                            v.wait_ge(S_["sp_u"], idx + 1)
                            vedge(v, v.tensor_scalar(UP2S[:, r * 512:(r + 1) * 512],
                                                     P_U[:, tc * 512:(tc + 1) * 512],
                                                     ub, -LIMIT, ALU.add, ALU.max))
                            v.tensor_scalar(UP2S[:, r * 512:(r + 1) * 512],
                                            UP2S[:, r * 512:(r + 1) * 512],
                                            LIMIT, 1.0, ALU.min,
                                            ALU.add).then_inc(S_["sv_up"], 1)
                            v.wait_ge(S_["ss_glu"], idx * n_tc + tc + 1)
                            v.wait_ge(S_["sv_up"], idx * n_tc + tc + 1)
                            v.tensor_tensor(GATED(ic, tc * 512, (tc + 1) * 512),
                                            GLUS[:, r * 512:(r + 1) * 512],
                                            UP2S[:, r * 512:(r + 1) * 512],
                                            ALU.mult).then_inc(S_["sv_gated"], 1)
                    for tt in range(NT):
                        for mh in range(2):
                            v.wait_ge(S_["sp_d"], (e * NT + tt) * 2 + mh + 1)
                            v.scalar_tensor_tensor(
                                A[:, tt * 1024 + mh * 512:tt * 1024 + (mh + 1) * 512],
                                P_G[:, mh * 512:(mh + 1) * 512],
                                SCORES[:, tt * 8 + e:tt * 8 + e + 1],
                                A[:, tt * 1024 + mh * 512:tt * 1024 + (mh + 1) * 512],
                                ALU.mult, ALU.add).then_inc(S_["sv_devac"], 1)

            @blk.scalar
            def _(s):
                for e in range(NE):
                    for ic in range(8):
                        idx = e * 8 + ic
                        for tc in range(n_tc):
                            r = (idx * n_tc + tc) % 2
                            s.wait_ge(S_["sv_gt"], idx * n_tc + tc + 1)
                            s.activation(GLUS[:, r * 512:(r + 1) * 512],
                                         GTS[:, r * 512:(r + 1) * 512],
                                         AF.Gelu_apprx_sigmoid).then_inc(S_["ss_glu"], 1)

    with contextlib.ExitStack() as ps:
        P_TR4 = ps.enter_context(nc.psum_tensor("P_TR4", [128, 1024], BF16))
        P_RT4 = ps.enter_context(nc.psum_tensor("P_RT4", [128, 8], F32))
        with nc.Block("b4") as blk:
            @blk.sync
            def _(sy):
                if debug:
                    for i in range(NT):
                        sy.dma_start(dbg["dbg_A3"].ap()[i * 128:(i + 1) * 128, :],
                                     A[:, i * 1024:(i + 1) * 1024]).then_inc(S_["s_out"], 16)
                    sy.dma_start(dbg["dbg_SC"].ap(), SCORES[:, :]).then_inc(S_["s_out"], 16)
                    sy.wait_ge(S_["s_out"], 16 * (2 * NT + 2))
                for i in range(NT):
                    if i >= 3:
                        sy.wait_ge(S_["b4_xhat"], i - 2)
                    sy.dma_start(XR[:, (i % 3) * 1024:(i % 3 + 1) * 1024],
                                 x_in.ap()[i * 128:(i + 1) * 128, :]).then_inc(S_["b4x%d" % (i % 3)], 16)

            @blk.vector
            def _(v):
                for i in range(NT):
                    xr = XR[:, (i % 3) * 1024:(i % 3 + 1) * 1024]
                    at = A[:, i * 1024:(i + 1) * 1024]
                    v.wait_ge(S_["b4x%d" % (i % 3)], 16 * (i // 3 + 1))
                    vedge(v, v.tensor_tensor(xr, xr, at, ALU.add))
                    vedge(v, v.tensor_scalar(DUMP2[:, :], xr, 1.0, None, ALU.mult,
                                             ALU.add, accum_out=STAT[:, 0:1]))
                    vedge(v, v.scalar_tensor_tensor(DUMP2[:, :], xr, 1.0, xr,
                                                    ALU.bypass, ALU.mult,
                                                    accum_out=STAT[:, 1:2]))
                    vedge(v, v.tensor_scalar(STAT[:, 2:3], STAT[:, 0:1], -1.0 / D,
                                             None, ALU.mult))
                    vedge(v, v.tensor_tensor(STAT[:, 3:4], STAT[:, 2:3], STAT[:, 2:3],
                                             ALU.mult))
                    v.scalar_tensor_tensor(STAT[:, 4:5], STAT[:, 1:2], 1.0 / D,
                                           STAT[:, 3:4], ALU.mult,
                                           ALU.subtract).then_inc(S_["b4_eps"], 1)
                    v.wait_ge(S_["b4_rstd"], i + 1)
                    if i >= 2:
                        v.wait_ge(S_["b2_trx"], NT + i - 1)
                    vedge(v, v.tensor_scalar(STAT[:, 7:8], STAT[:, 6:7], 2.0, None,
                                             ALU.mult))
                    vedge(v, v.tensor_scalar(XH2(i % 2), xr, STAT[:, 2:3],
                                             STAT[:, 6:7], ALU.add, ALU.mult))
                    v.tensor_scalar(at, xr, STAT[:, 2:3], STAT[:, 7:8],
                                    ALU.add, ALU.mult).then_inc(S_["b4_xhat"], 1)
                    v.wait_ge(S_["b4_rt"], i + 1)
                    v.tensor_copy(LSC[:, 0:8], P_RT4[:, :]).then_inc(S_["b4_sc"], 1)
                    v.wait_ge(S_["b4_sc"], i + 1)
                    vedge(v, v.tensor_tensor(LSC[:, 8:16], LSC[:, 0:8], RBB[:, :],
                                             ALU.add))
                    vedge(v, v.reduce_max(STAT[:, 10:11], LSC[:, 8:16], axis=AX.X,
                                          negate=True))
                    vedge(v, v.tensor_scalar(STAT[:, 11:12], STAT[:, 10:11], -1.0,
                                             None, ALU.mult))
                    vedge(v, v.tensor_scalar(LSC[:, 16:24], LSC[:, 8:16],
                                             STAT[:, 11:12], None, ALU.is_ge))
                    vedge(v, v.scalar_tensor_tensor(LSC[:, 24:32], LSC[:, 16:24],
                                                    -BIG, LSC[:, 8:16],
                                                    ALU.mult, ALU.add))
                    vedge(v, v.reduce_max(STAT[:, 12:13], LSC[:, 24:32], axis=AX.X,
                                          negate=True))
                    vedge(v, v.tensor_scalar(STAT[:, 13:14], STAT[:, 12:13], -1.0,
                                             None, ALU.mult))
                    vedge(v, v.tensor_scalar(LSC[:, 16:24], LSC[:, 8:16],
                                             STAT[:, 13:14], None, ALU.is_ge))
                    v.wait_ge(S_["b4_sig"], i + 1)
                    vedge(v, v.scalar_tensor_tensor(LSC[:, 24:32], LSC[:, 16:24], BIG,
                                                    STAT[:, 16:24], ALU.mult, ALU.add))
                    vedge(v, v.tensor_scalar(LSC[:, 24:32], LSC[:, 24:32], -BIG, None,
                                             ALU.add))
                    vedge(v, v.reduce_max(STAT[:, 14:15], LSC[:, 24:32], axis=AX.X,
                                          negate=True))
                    v.engine_nop().then_inc(S_["b4_rsel"], 1)
                    v.wait_ge(S_["b4_exp"], i + 1)
                    vedge(v, v.reciprocal(STAT[:, 25:26], STAT[:, 15:16]))
                    v.tensor_scalar(COMB[:, i * 8:(i + 1) * 8], LSC[:, 16:24],
                                    STAT[:, 25:26], None, ALU.mult)

            @blk.scalar
            def _(s):
                for i in range(NT):
                    s.wait_ge(S_["b4_eps"], i + 1)
                    sedge(s, s.activation(STAT[:, 5:6], STAT[:, 4:5], AF.Ln,
                                          bias=EPSC[:, 0:1]))
                    s.activation(STAT[:, 6:7], STAT[:, 5:6], AF.Exp,
                                 scale=-0.5).then_inc(S_["b4_rstd"], 1)
                    s.wait_ge(S_["b2_trx"], NT + i + 1)
                    for dc in range(8):
                        ins = s.copy(XF2t(dc, i), P_TR4[:, dc * 128:(dc + 1) * 128])
                    ins.then_inc(S_["b2_cpx"], 1)
                    s.wait_ge(S_["b4_sc"], i + 1)
                    s.activation(STAT[:, 16:24], LSC[:, 0:8],
                                 AF.Sigmoid).then_inc(S_["b4_sig"], 1)
                    s.wait_ge(S_["b4_rsel"], i + 1)
                    s.activation(LSC[:, 16:24], LSC[:, 24:32], AF.Exp,
                                 bias=STAT[:, 14:15],
                                 accum_out=STAT[:, 15:16]).then_inc(S_["b4_exp"], 1)

            @blk.tensor
            def _(t):
                t.wait_ge(S_["s_b0"], 23)
                for i in range(NT):
                    t.wait_ge(S_["b4_xhat"], i + 1)
                    if i >= 1:
                        t.wait_ge(S_["b2_cpx"], NT + i)
                    for dc in range(8):
                        ins = t.transpose(P_TR4[:, dc * 128:(dc + 1) * 128],
                                          XH2(i % 2)[:, dc * 128:(dc + 1) * 128],
                                          IDENTB[:, :])
                    ins.then_inc(S_["b2_trx"], 1)
                    t.wait_ge(S_["b2_cpx"], NT + i + 1)
                    t.wait_ge(S_["b4_sc"], i)
                    for dc in range(8):
                        ins = t.matmul(P_RT4[:, :], XF2t(dc, i),
                                       CENTT[:, dc * 8:(dc + 1) * 8],
                                       start=(dc == 0), stop=(dc == 7))
                    ins.then_inc(S_["b4_rt"], 1)

    with contextlib.ExitStack() as ps:
        P_BIG = ps.enter_context(nc.psum_tensor("P_BIG", [128, 4096], F32))
        with nc.Block("b5") as blk:
            @blk.scalar
            def _(s):
                glist = [(p, half, fsb) for p in range(NPASS)
                         for half in range(n_half) for fsb in range(16)]

                def issue_load(g):
                    p_, half_, fsb_ = glist[g]
                    s.wait_ge(S_["s_stg%d" % p_], 32)
                    s.dma_start(WFCs(g % 2)[:, :].rearrange("p (c f) -> p c f", c=8),
                                st_fcT.ap()[p_][fsb_]).then_inc(S_["swf%d" % (g % 2)], 16)

                issue_load(0)
                issue_load(1)
                for g in range(len(glist)):
                    if g + 2 < len(glist):
                        s.wait_ge(S_["sp_fc"], g + 1)
                        issue_load(g + 2)
                    s.wait_ge(S_["sp_fc"], g + 1)
                    for fsub in range(2):
                        for tc in range(Tq // 512):
                            bank = (g % 2) * 4 + fsub * 2 + tc
                            s.activation(HR((glist[g][2]) * 2 + fsub, tc * 512,
                                            (tc + 1) * 512),
                                         P_BIG[:, bank * 512:(bank + 1) * 512],
                                         AF.Gelu).then_inc(S_["ss_hr"], 1)

            @blk.sync
            def _(sy):
                if debug:
                    for i in range(NT):
                        sy.dma_start(dbg["dbg_A4"].ap()[i * 128:(i + 1) * 128, :],
                                     A[:, i * 1024:(i + 1) * 1024]).then_inc(S_["s_out"], 16)
                    sy.dma_start(dbg["dbg_CB"].ap(), COMB[:, :]).then_inc(S_["s_out"], 16)
                    sy.wait_ge(S_["s_out"], 16 * (3 * NT + 3))
                for p in range(NPASS):
                    sy.wait_ge(S_["s_stg%d" % p], 32)
                    for half in range(n_half):
                        gh = p * n_half + half
                        for sq in range(Tq // 512):
                            for fc in range(32):
                                pfc = (gh * (Tq // 512) + sq) * 32 + fc
                                if pfc >= 4:
                                    sy.wait_ge(S_["sp_pj"], pfc - 3)
                                sy.dma_start(WPJs(pfc % 4),
                                             st_pjT.ap()[p][fc]).then_inc(S_["swp%d" % (pfc % 4)], 16)

            @blk.tensor
            def _(t):
                for p in range(NPASS):
                    for half in range(n_half):
                        gh = p * n_half + half
                        if gh >= 1:
                            t.wait_ge(S_["sp_pj"], gh * (Tq // 512) * 32)
                        for fsb in range(16):
                            gfsb = gh * 16 + fsb
                            t.wait_ge(S_["swf%d" % (gfsb % 2)], 16 * (gfsb // 2 + 1))
                            if gfsb >= 2:
                                t.wait_ge(S_["ss_hr"], 4 * (gfsb - 1))
                            if gh >= 1 and fsb < 2:
                                t.wait_ge(S_["sv_pevac"], (gh - 1 + 1) * (Tq // 512) * 8)
                            bq_ = (gfsb % 2) * 4
                            for fsub in range(2):
                                for dc in range(8):
                                    for tc in range(Tq // 512):
                                        bank = bq_ + fsub * 2 + tc
                                        ins = t.matmul(
                                            P_BIG[:, bank * 512:(bank + 1) * 512],
                                            WFCs(gfsb % 2)[:, dc * 256 + fsub * 128:
                                                           dc * 256 + (fsub + 1) * 128],
                                            XF2c(dc, half * Tq + tc * 512,
                                                 half * Tq + (tc + 1) * 512),
                                            start=(dc == 0), stop=(dc == 7))
                            ins.then_inc(S_["sp_fc"], 1)
                        t.wait_ge(S_["ss_hr"], 64 * (gh + 1))
                        for sq in range(Tq // 512):
                            gsq = gh * (Tq // 512) + sq
                            for fc in range(32):
                                pfc = gsq * 32 + fc
                                t.wait_ge(S_["swp%d" % (pfc % 4)], 16 * (pfc // 4 + 1))
                                for tt in range(4):
                                    for mh in range(2):
                                        bank = tt * 2 + mh
                                        if fc == 0 and gsq >= 1:
                                            t.wait_ge(S_["sv_pevac"],
                                                      8 * (gsq - 1) + bank + 1)
                                        ins = t.matmul(
                                            P_BIG[:, bank * 512:(bank + 1) * 512],
                                            HR(fc, sq * 512 + tt * 128,
                                               sq * 512 + (tt + 1) * 128),
                                            WPJs(pfc % 4)[:, mh * 512:(mh + 1) * 512],
                                            start=(fc == 0), stop=(fc == 31))
                                ins.then_inc(S_["sp_pj"], 1)


            @blk.vector
            def _(v):
                for p in range(NPASS):
                    for half in range(n_half):
                        gh = p * n_half + half
                        for sq in range(Tq // 512):
                            gsq = gh * (Tq // 512) + sq
                            v.wait_ge(S_["sp_pj"], (gsq + 1) * 32)
                            for tt in range(4):
                                tglob = half * ntt_q + sq * 4 + tt
                                for mh in range(2):
                                    bank = tt * 2 + mh
                                    av = A[:, tglob * 1024 + mh * 512:
                                           tglob * 1024 + (mh + 1) * 512]
                                    if p < NS:
                                        ins = v.tensor_tensor(
                                            av, P_BIG[:, bank * 512:(bank + 1) * 512],
                                            av, ALU.add)
                                    else:
                                        ins = v.scalar_tensor_tensor(
                                            av, P_BIG[:, bank * 512:(bank + 1) * 512],
                                            COMB[:, tglob * 8 + (p - NS):
                                                 tglob * 8 + (p - NS) + 1],
                                            av, ALU.mult, ALU.add)
                                    ins.then_inc(S_["sv_pevac"], 1)

    with nc.Block("b6") as blk:
        @blk.vector
        def _(v):
            for i in range(NT):
                at = A[:, i * 1024:(i + 1) * 1024]
                if i >= 3:
                    v.wait_ge(S_["so%d" % (i % 3)], 16 * (i // 3))
                vedge(v, v.tensor_scalar(DUMP2[:, :], at, 1.0, None, ALU.mult,
                                         ALU.add, accum_out=STAT[:, 0:1]))
                vedge(v, v.scalar_tensor_tensor(DUMP2[:, :], at, 1.0, at,
                                                ALU.bypass, ALU.mult,
                                                accum_out=STAT[:, 1:2]))
                vedge(v, v.tensor_scalar(STAT[:, 2:3], STAT[:, 0:1], -1.0 / D,
                                         None, ALU.mult))
                vedge(v, v.tensor_tensor(STAT[:, 3:4], STAT[:, 2:3], STAT[:, 2:3],
                                         ALU.mult))
                v.scalar_tensor_tensor(STAT[:, 4:5], STAT[:, 1:2], 1.0 / D,
                                       STAT[:, 3:4], ALU.mult,
                                       ALU.subtract).then_inc(S_["b2_eps"], 1)
                v.wait_ge(S_["b2_rstd"], NT + i + 1)
                v.tensor_scalar(XR[:, (i % 3) * 1024:(i % 3 + 1) * 1024], at,
                                STAT[:, 2:3], STAT[:, 6:7], ALU.add,
                                ALU.mult).then_inc(S_["b2_sc"], 1)

        @blk.scalar
        def _(s):
            for i in range(NT):
                s.wait_ge(S_["b2_eps"], NT + i + 1)
                sedge(s, s.activation(STAT[:, 5:6], STAT[:, 4:5], AF.Ln, bias=EPSC[:, 0:1]))
                s.activation(STAT[:, 6:7], STAT[:, 5:6], AF.Exp,
                             scale=-0.5).then_inc(S_["b2_rstd"], 1)

        @blk.sync
        def _(sy):
            for i in range(NT):
                sy.wait_ge(S_["b2_sc"], NT + i + 1)
                sy.dma_start(out_t.ap()[i * 128:(i + 1) * 128, :],
                             XR[:, (i % 3) * 1024:(i % 3 + 1) * 1024]).then_inc(S_["so%d" % (i % 3)], 16)
            for r in range(3):
                n_r = len([i for i in range(NT) if i % 3 == r])
                sy.wait_ge(S_["so%d" % r], 16 * n_r)

    return nc


def make_consts(T):
    pos = np.arange(T, dtype=np.float64)
    inv = 1.0 / (150000.0 ** (np.arange(32, dtype=np.float64) / 32.0))
    ang = pos[:, None] * inv[None, :]
    cosw = (np.cos(ang) * SCALE).astype(np.float32)
    sinw = (np.sin(ang) * SCALE).astype(np.float32)
    ql = np.arange(128)
    kl = np.arange(384)
    band = (kl[None, :] >= ql[:, None]) & (kl[None, :] <= ql[:, None] + 2 * SW)
    m_int = np.where(band, 0.0, -BIG).astype(np.float32)
    m_first = np.where(kl[None, :] <= ql[:, None] + SW, 0.0, -BIG).astype(np.float32)
    m_first[:, 256:384] = -BIG
    m_last = m_int.copy(); m_last[:, 256:384] = -BIG
    masks = np.stack([m_int, m_first, m_last]).astype(np.float32)
    ident = np.eye(128, dtype=np.float32)
    return cosw, sinw, masks, ident


_NC_CACHE = {}


def _fcT(w):
    n = w.shape[0]
    return np.ascontiguousarray(
        w.reshape(n, 16, 256, 8, 128).transpose(0, 1, 4, 3, 2))


def _pjT(w):
    n = w.shape[0]
    return np.ascontiguousarray(
        w.reshape(n, 1024, 32, 128).transpose(0, 2, 3, 1))


def kernel(**inputs):
    NT = 16
    T = NT * 128
    if NT not in _NC_CACHE:
        _NC_CACHE[NT] = build_nc(NT)
    nc = _NC_CACHE[NT]
    cosw, sinw, masks, ident = make_consts(T)
    x = np.ascontiguousarray(inputs["L0_attn_hidden_states"])
    base = {
        "attn_norm_scale": inputs["L0_attn_attn_norm_scale"],
        "Wq": np.ascontiguousarray(inputs["L0_attn_Wq"].reshape(D, NQ * HD)),
        "bq": np.ascontiguousarray(inputs["L0_attn_bq"].reshape(NQ * HD)),
        "Wk": np.ascontiguousarray(inputs["L0_attn_Wk"].reshape(D, NKV * HD)),
        "bk": np.ascontiguousarray(inputs["L0_attn_bk"].reshape(NKV * HD)),
        "Wv": np.ascontiguousarray(inputs["L0_attn_Wv"].reshape(D, NKV * HD)),
        "bv": np.ascontiguousarray(inputs["L0_attn_bv"].reshape(NKV * HD)),
        "Wo": np.ascontiguousarray(inputs["L0_attn_Wo"]),
        "bo": inputs["L0_attn_bo"],
        "sinks": inputs["L0_attn_sinks"],
        "moe_norm_scale": inputs["L0_attn_moe_norm_scale"],
        "router_W": np.ascontiguousarray(inputs["L0_attn_router_W"]),
        "router_b": inputs["L0_attn_router_b"],
        "gate_up_proj": np.ascontiguousarray(inputs["L0_attn_gate_up_proj"]),
        "gate_up_bias": np.ascontiguousarray(inputs["L0_attn_gate_up_bias"]),
        "down_proj": np.ascontiguousarray(inputs["L0_attn_down_proj"]),
        "down_bias": np.ascontiguousarray(inputs["L0_attn_down_bias"]),
        "centroids": np.ascontiguousarray(inputs["L0_ffn_centroids"]),
        "routing_bias": inputs["L0_ffn_routing_bias"],
        "WfcT": _fcT(np.concatenate([np.asarray(inputs["L0_ffn_Wfc_sh"]),
                                     np.asarray(inputs["L0_ffn_Wfc_r"])], axis=0)),
        "WpjT": _pjT(np.concatenate([np.asarray(inputs["L0_ffn_Wproj_sh"]),
                                     np.asarray(inputs["L0_ffn_Wproj_r"])], axis=0)),
        "COS": cosw, "SIN": sinw, "MASKS": masks, "IDENT": ident,
    }
    base = {k: np.ascontiguousarray(np.asarray(v, dtype=np.float32))
            for k, v in base.items()}
    in_maps = []
    for c in range(8):
        m = dict(base)
        m["x_in"] = np.ascontiguousarray(x[c])
        in_maps.append(m)
    res = run_bass_kernel_spmd(nc, in_maps, core_ids=list(range(8)))
    return np.stack([r["out"] for r in res.results], axis=0)


# version 11
# speedup vs baseline: 1.4602x; 1.2415x over previous
import contextlib
import numpy as np
import concourse.bass as bass
import concourse.mybir as mybir
from concourse.bass_utils import run_bass_kernel_spmd

F32 = mybir.dt.float32
BF16 = mybir.dt.bfloat16
AF = mybir.ActivationFunctionType
ALU = mybir.AluOpType
AX = mybir.AxisListType

D = 1024
NQ, NKV, HD, SW = 16, 4, 64, 128
INTER = 1024
NE = 8
FF = 4096
NS = 2
EPS = 1e-5
LIMIT = 7.0
BIG = 1e9
SCALE = HD ** (-0.25)
NPASS = NS + NE


def head_of(pb, half):
    g2, j4 = divmod(pb, 4)
    return (2 * g2 + half) * 4 + j4


def build_nc(NT=16, debug=False):
    T = NT * 128
    Tq = 1024 if T >= 2048 else T
    n_half = T // Tq
    ntt_q = Tq // 128
    n_tc = T // 512
    C = T // 2
    NCC = C // 128
    XW = 1040
    nc = bass.Bass("TRN2", target_bir_lowering=False,
                   dynamic_dma_scratch_size=16384)
    dram = {}

    def din(name, shape):
        dram[name] = nc.dram_tensor(name, list(shape), F32, kind="ExternalInput")
        return dram[name]

    x_in = din("x_in", [T, D])
    din("attn_norm_scale", [D])
    din("Wq", [D, NQ * HD]); din("bq", [NQ * HD])
    din("Wk", [D, NKV * HD]); din("bk", [NKV * HD])
    din("Wv", [D, NKV * HD]); din("bv", [NKV * HD])
    din("Wo", [NQ, HD, D]); din("bo", [D])
    din("sinks", [NQ])
    din("moe_norm_scale", [D])
    din("router_W", [D, NE]); din("router_b", [NE])
    din("gate_up_proj", [NE, D, 2 * INTER]); din("gate_up_bias", [NE, 2 * INTER])
    din("down_proj", [NE, INTER, D]); din("down_bias", [NE, D])
    din("centroids", [NE, D]); din("routing_bias", [NE])
    din("WfcT", [NPASS, 16, 128, 8, 256])
    din("WpjT", [NPASS, 32, 128, 1024])
    din("COS", [T, 32]); din("SIN", [T, 32])
    din("MASKS", [3, 128, 384])
    din("IDENT", [128, 128])
    din("UTRI", [128, 128])
    din("EOFF", [8])
    din("TIDS", [T, 2])

    out_t = nc.dram_tensor("out", [T, D], F32, kind="ExternalOutput")
    dbg = {}
    if debug:
        for n in ["dbg_A1", "dbg_A3", "dbg_A4"]:
            dbg[n] = nc.dram_tensor(n, [T, D], F32, kind="ExternalOutput")
        dbg["dbg_SC"] = nc.dram_tensor("dbg_SC", [128, NT * 8], F32, kind="ExternalOutput")
        dbg["dbg_KFM"] = nc.dram_tensor("dbg_KFM", [128, 2 * T], F32, kind="ExternalOutput")
        dbg["dbg_VT"] = nc.dram_tensor("dbg_VT", [128, NT * 256], F32, kind="ExternalOutput")
        dbg["dbg_SM"] = nc.dram_tensor("dbg_SM", [128, 768], F32, kind="ExternalOutput")
        dbg["dbg_PB"] = nc.dram_tensor("dbg_PB", [128, 768], F32, kind="ExternalOutput")
        dbg["dbg_OF"] = nc.dram_tensor("dbg_OF", [128, 1024], F32, kind="ExternalOutput")
        dbg["dbg_CB"] = nc.dram_tensor("dbg_CB", [128, NT * 8], F32, kind="ExternalOutput")
    st_fcT = nc.dram_tensor("st_fcT", [NPASS, 16, 128, 8, 256], BF16, kind="Internal")
    st_pjT = nc.dram_tensor("st_pjT", [NPASS, 32, 128, 1024], BF16, kind="Internal")
    gdram = nc.dram_tensor("gdram", [NE * C, XW], BF16, kind="Internal")
    rout3 = nc.dram_tensor("rout3", [2 * T, D], BF16, kind="Internal")

    SB = nc.alloc_sbuf_tensor
    A = SB("A", [128, NT * 1024], F32)
    BFAC = 57344
    BFA = SB("BFA", [128, BFAC], BF16)

    KFM_o = 0
    VTOK_o = 2 * T
    QROPE_o = VTOK_o + NT * 256
    KROPE_o = QROPE_o + 1024
    QR_o = KROPE_o + 256
    OFMS_o = QR_o + 2048
    XHATS_o = OFMS_o + 1024
    XF1_o = XHATS_o + 2048
    DUMP_o = XF1_o + 2048
    PTS_o = DUMP_o + 1024
    WQKV_off = max(16 * T, PTS_o + 768)
    WO_off = WQKV_off + 12288
    assert WO_off + 8192 <= BFAC
    XF_o = 0
    GATED_o = 8 * T
    WG_o, WU_o, WD_o = WQKV_off, WQKV_off + 8192, WQKV_off + 16384
    assert WD_o + 8192 <= BFAC
    SC2_o = GATED_o
    HR_o = 8 * T
    WFC_o = HR_o + 32 * Tq
    WPJ_o = WFC_o + 4096
    assert WPJ_o + 4096 <= BFAC

    XR = SB("XR", [128, 3 * 1024], F32)
    if WQKV_off - (PTS_o + 768) >= 7680:
        B1S_o = PTS_o + 768
    else:
        B1S_o = WD_o + 8192
        assert BFAC - B1S_o >= 7680
    MASKS = BFA[:, B1S_o:B1S_o + 2304].bitcast(F32)
    COSB = BFA[:, B1S_o + 2304:B1S_o + 2304 + NT * 32]
    SINB = BFA[:, B1S_o + 2816:B1S_o + 2816 + NT * 32]
    SM = BFA[:, B1S_o + 3328:B1S_o + 4864].bitcast(F32)
    PBUF = BFA[:, B1S_o + 4864:B1S_o + 5632]
    G2T = BFA[:, B1S_o + 5632:B1S_o + 6656]
    assert B1S_o + 6656 <= BFAC
    IDENTB = SB("IDENTB", [128, 128], BF16)
    IDENTF = SB("IDENTF", [128, 128], F32)
    NEGSINK = SB("NEGSINK", [128, 16], F32)
    SINKB = SB("SINKB", [128, 16], F32)
    GUPB = SB("GUPB", [128, 128], F32)
    RBB = SB("RBB", [128, 8], F32)
    SCORES = SB("SCORES", [128, NT * 8], F32)
    COMB = SB("COMB", [128, NT * 8], F32)
    STAT = SB("STAT", [128, 48], F32)
    LSC = SB("LSC", [128, 32], F32)
    GTS = SB("GTS", [128, 2 * 512], BF16)
    GLUS = SB("GLUS", [128, 2 * 512], BF16)
    UP2S = SB("UP2S", [128, 2 * 512], BF16)
    ROWS = SB("ROWS", [1, 1544], BF16)
    BOROW = SB("BOROW", [1, 1024], BF16)
    DBROW = SB("DBROW", [1, 1024], BF16)
    SROW = SB("SROW", [1, 64], F32)
    G1C = SB("G1C", [128, 8], F32)
    ONESC = SB("ONESC", [1, 128], BF16)
    ONESCF = SB("ONESCF", [1, 128], F32)
    CENTT = SB("CENTT", [128, 64], BF16)
    ROUTW = SB("ROUTW", [128, 64], BF16)
    EPSC = SB("EPSC", [128, 1], F32)
    UTRIB = SB("UTRIB", [128, 128], BF16)
    EOFFB = SB("EOFFB", [128, 8], F32)
    BASE = SB("BASE", [1, 8], F32)
    SELF = SB("SELF", [128, 8], F32)
    SELB = SB("SELB", [128, 8], BF16)
    IDX32 = SB("IDX32", [128, 8 * 16], mybir.dt.int32)
    GCF = SB("GCF", [128, 64], F32)
    ITC = SB("ITC", [128, 64], mybir.dt.int32)
    ZEROT = SB("ZEROT", [128, 16], BF16)
    ONECOL = SB("ONECOL", [128, 1], F32)
    TIDC = SB("TIDC", [128, 2 * NT], BF16)
    GXC = SB("GXC", [128, 256], BF16)
    ERNG8 = SB("ERNG8", [128, 8], F32)
    G2ROW = XR[32:33, 0:1024]
    SROW = XR[64:65, 0:64]
    CENTN = XR[0:8, 2048:2560].bitcast(BF16)
    ONESF32 = XR[32:33, 1024:1152]
    ONESF64 = XR[64:65, 64:192]

    names = ["sv_eps", "ss_rstd", "sv_xhat", "sp_trx", "ss_cpx", "sp_qkv", "sv_ropek", "sv_ropeq", "sv_ropeall", "sp_trq", "ss_cpq", "sp_trk", "ss_cpk", "sp_s", "sv_mask", "sv_negm", "sp_pt", "ss_pt", "ss_exp", "sv_pn", "sp_av", "ss_ofm", "sp_wo", "sv_evac", "s_wsmall", "s_const", "s_constg", "s_b0", "ev", "es", "b2_eps", "b2_rstd", "b2_xhat", "b2_trx", "b2_cpx", "b2_rt", "b2_sc", "b2_lsel", "b2_exp", "sw_g", "sw_u", "sw_d", "sp_g", "sp_u", "sv_gt", "sv_up", "ss_glu", "sv_gated", "sp_d", "sv_devac", "s_eo1", "s_scat1", "b4_rt", "b4_sc", "b4_sig", "b4_rsel", "b4_exp", "sp_fc", "ss_hr", "sp_pj", "sv_pevac", "s_out", "s_dbg", "swp3", "sx0", "sx1", "sx2", "b4x0", "b4x1", "b4x2", "so0", "so1", "so2", "swf0", "swf1", "swp0", "swp1", "swp2", "s_stg0", "s_stg1", "s_stg2", "s_stg3", "s_stg4", "s_stg5", "s_stg6", "s_stg7", "s_stg8", "s_stg9", "b4d_p", "b4d_v", "b4_cmb", "b6a", "b6r", "b6g", "s_zero", "s_scat", "s_gx", "s_gx1", "s_eo"]
    S_ = {n: nc.alloc_semaphore(n) for n in names}

    ecnt = {"v": [0], "s": [0]}

    def vedge(v, ins):
        ecnt["v"][0] += 1
        ins.then_inc(S_["ev"], 1)
        v.wait_ge(S_["ev"], ecnt["v"][0])

    def sedge(s, ins):
        ecnt["s"][0] += 1
        ins.then_inc(S_["es"], 1)
        s.wait_ge(S_["es"], ecnt["s"][0])

    def mask_ap(j):
        if j == 0:
            return MASKS[:, 384:768]
        if j == NT - 1:
            return MASKS[:, 768:1152]
        return MASKS[:, 0:384]

    XHATS = lambda r: BFA[:, XHATS_o + r * 1024:XHATS_o + (r + 1) * 1024]
    XF1 = lambda r, dc: BFA[:, XF1_o + r * 1024 + dc * 128:XF1_o + r * 1024 + (dc + 1) * 128]
    QRS = lambda r, pb: BFA[:, QR_o + (pb * 2 + r) * 128:QR_o + (pb * 2 + r + 1) * 128]
    OFMS = lambda pb: BFA[:, OFMS_o + pb * 128:OFMS_o + (pb + 1) * 128]
    QROPE = BFA[:, QROPE_o:QROPE_o + 1024]
    KROPE = BFA[:, KROPE_o:KROPE_o + 256]
    DUMP = BFA[:, DUMP_o:DUMP_o + 1024]
    PTS = lambda r: BFA[:, PTS_o + r * 384:PTS_o + (r + 1) * 384]
    XH2 = lambda r: BFA[:, SC2_o + r * XW:SC2_o + r * XW + 1024]
    XSe = lambda r: BFA[:, SC2_o + r * XW + 1024:SC2_o + (r + 1) * XW]
    XSrow = lambda r: BFA[:, SC2_o + r * XW:SC2_o + (r + 1) * XW]
    DUMP2 = BFA[:, SC2_o + 2 * XW:SC2_o + 2 * XW + 1024]
    XF2t = lambda dc, i: BFA[:, XF_o + dc * T + i * 128:XF_o + dc * T + (i + 1) * 128]
    XF2c = lambda dc, c0, c1: BFA[:, XF_o + dc * T + c0:XF_o + dc * T + c1]
    GATED = lambda ic, c0, c1: BFA[:, GATED_o + ic * T + c0:GATED_o + ic * T + c1]
    HR = lambda fc, c0, c1: BFA[:, HR_o + fc * Tq + c0:HR_o + fc * Tq + c1]
    WFCs = lambda r: BFA[:, WFC_o + r * 2048:WFC_o + (r + 1) * 2048]
    WPJs = lambda r: BFA[:, WPJ_o + r * 1024:WPJ_o + (r + 1) * 1024]

    N_CONST = 16 * 15
    N_CONSTG = 16 * 5
    N_WSMALL = 16 * 26

    with nc.Block("b0") as blk:
        @blk.sync
        def _(sy):
            sy.dma_start(MASKS.rearrange("p (m c) -> p m c", m=3),
                         dram["MASKS"].ap().rearrange("m p c -> p m c")).then_inc(S_["s_const"], 16)
            sy.dma_start(SROW[0:1, 0:16], dram["sinks"].ap().unsqueeze(0)).then_inc(S_["s_const"], 16)
            sy.dma_start(SROW[0:1, 16:24], dram["routing_bias"].ap().unsqueeze(0)).then_inc(S_["s_const"], 16)
            sy.dma_start(SROW[0:1, 24:32], dram["EOFF"].ap().unsqueeze(0)).then_inc(S_["s_const"], 16)
            sy.dma_start(G2ROW, dram["moe_norm_scale"].ap().unsqueeze(0)).then_inc(S_["s_const"], 16)
            sy.dma_start(IDENTF[:, :], dram["IDENT"].ap()).then_inc(S_["s_const"], 16)
            for c in range(8):
                sy.dma_start(G1C[:, c:c + 1],
                             dram["attn_norm_scale"].ap()[c * 128:(c + 1) * 128].unsqueeze(1)).then_inc(S_["s_const"], 16)
            sy.dma_start(XR[0:8, 0:2048], dram["gate_up_bias"].ap()).then_inc(S_["s_const"], 16)

        @blk.gpsimd
        def _(g):
            g.dma_start(IDENTB[:, :], dram["IDENT"].ap()).then_inc(S_["s_constg"], 16)
            g.dma_start(UTRIB[:, :], dram["UTRI"].ap()).then_inc(S_["s_constg"], 16)
            g.dma_start(TIDC[:, :].rearrange("p (i two) -> p i two", two=2),
                        dram["TIDS"].ap().rearrange("(i p) two -> p i two", p=128)).then_inc(S_["s_constg"], 16)
            g.dma_start(COSB[:, :].rearrange("p (n j) -> p n j", j=32),
                        dram["COS"].ap().rearrange("(n p) j -> p n j", p=128)).then_inc(S_["s_constg"], 16)
            g.dma_start(SINB[:, :].rearrange("p (n j) -> p n j", j=32),
                        dram["SIN"].ap().rearrange("(n p) j -> p n j", p=128)).then_inc(S_["s_constg"], 16)
            g.dma_start(ROWS[0:1, 0:1024], dram["bq"].ap().unsqueeze(0)).then_inc(S_["s_wsmall"], 16)
            g.dma_start(ROWS[0:1, 1024:1280], dram["bk"].ap().unsqueeze(0)).then_inc(S_["s_wsmall"], 16)
            g.dma_start(ROWS[0:1, 1280:1536], dram["bv"].ap().unsqueeze(0)).then_inc(S_["s_wsmall"], 16)
            g.dma_start(ROWS[0:1, 1536:1544], dram["router_b"].ap().unsqueeze(0)).then_inc(S_["s_wsmall"], 16)
            g.dma_start(BOROW[0:1, :], dram["bo"].ap().unsqueeze(0)).then_inc(S_["s_wsmall"], 16)
            wq_v = BFA[:, WQKV_off:WQKV_off + 12288].rearrange("p (c f) -> p c f", c=8)
            g.dma_start(wq_v[:, :, 0:1024],
                        dram["Wq"].ap().rearrange("(c p) f -> p c f", p=128)).then_inc(S_["s_wsmall"], 16)
            g.dma_start(wq_v[:, :, 1024:1280],
                        dram["Wk"].ap().rearrange("(c p) f -> p c f", p=128)).then_inc(S_["s_wsmall"], 16)
            g.dma_start(wq_v[:, :, 1280:1536],
                        dram["Wv"].ap().rearrange("(c p) f -> p c f", p=128)).then_inc(S_["s_wsmall"], 16)
            for pb in range(8):
                for half in range(2):
                    h = head_of(pb, half)
                    g.dma_start(BFA[half * 64:half * 64 + 64,
                                    WO_off + pb * 1024:WO_off + (pb + 1) * 1024],
                                dram["Wo"].ap()[h, :, :]).then_inc(S_["s_wsmall"], 16)
            g.dma_start(ROUTW[:, :].rearrange("p (c e) -> p c e", c=8),
                        dram["router_W"].ap().rearrange("(c p) e -> p c e", p=128)).then_inc(S_["s_wsmall"], 16)
            g.dma_start(CENTN, dram["centroids"].ap()).then_inc(S_["s_wsmall"], 16)

        @blk.vector
        def _(v):
            v.memset(ONESC[:, :], 1.0)
            v.memset(ONESCF[:, :], 1.0)
            v.memset(ONESF32, 1.0)
            v.memset(ONESF64, 1.0)
            v.memset(EPSC[:, :], EPS)
            vedge(v, v.memset(ZEROT[:, :], 0.0))
            v.memset(ZEROT[:, 8:9], float(T // 64))
            v.memset(ONECOL[:, :], 1.0)
            v.memset(BASE[:, :], 0.0)

    with contextlib.ExitStack() as ps:
        P_B0 = ps.enter_context(nc.psum_tensor("P_B0", [128, 128], F32))
        P_B0B = ps.enter_context(nc.psum_tensor("P_B0B", [128, 64], BF16))
        with nc.Block("b0b") as blk:
            @blk.tensor
            def _(t):
                t.wait_ge(S_["s_const"], N_CONST)
                t.wait_ge(S_["s_constg"], N_CONSTG)
                t.matmul(P_B0[:, 0:16], ONESF64, SROW[0:1, 0:16],
                         start=True, stop=True)
                t.matmul(P_B0[:, 16:24], ONESF64, SROW[0:1, 16:24],
                         start=True, stop=True)
                t.matmul(P_B0[:, 32:40], ONESF64, SROW[0:1, 24:32],
                         start=True, stop=True).then_inc(S_["s_b0"], 1)

            @blk.vector
            def _(v):
                v.wait_ge(S_["s_b0"], 1)
                v.tensor_copy(SINKB[:, :], P_B0[:, 0:16])
                v.tensor_scalar(NEGSINK[:, :], P_B0[:, 0:16], -1.0, None, ALU.mult)
                v.tensor_copy(RBB[:, :], P_B0[:, 16:24])
                v.tensor_copy(EOFFB[:, :], P_B0[:, 32:40])
                vedge(v, v.tensor_scalar(ERNG8[:, :], P_B0[:, 32:40], 1.0,
                                         None, ALU.add))
                v.tensor_scalar(ERNG8[:, :], ERNG8[:, :], 1.0 / C, None, ALU.mult)
                v.engine_nop().then_inc(S_["s_b0"], 1)

        with nc.Block("b0c") as blk:
            @blk.tensor
            def _(t):
                t.wait_ge(S_["s_b0"], 2)
                for c in range(16):
                    ins = t.transpose(P_B0[:, c * 8:(c + 1) * 8],
                                      XR[0:8, c * 128:(c + 1) * 128],
                                      IDENTF[0:8, 0:8])
                ins.then_inc(S_["s_b0"], 1)
                t.wait_ge(S_["s_wsmall"], N_WSMALL)
                for c in range(8):
                    ins = t.transpose(P_B0B[:, c * 8:(c + 1) * 8],
                                      CENTN[0:8, c * 128:(c + 1) * 128],
                                      IDENTB[0:8, 0:8])
                ins.then_inc(S_["s_b0"], 1)

            @blk.vector
            def _(v):
                v.wait_ge(S_["s_b0"], 3)
                v.tensor_copy(GUPB[:, :], P_B0[:, 0:128])
                v.wait_ge(S_["s_b0"], 4)
                v.tensor_copy(CENTT[:, :], P_B0B[:, :])
                v.engine_nop().then_inc(S_["s_b0"], 1)
                v.engine_nop().then_inc(S_["s_b0"], 1)

        with nc.Block("b0d") as blk:
            @blk.tensor
            def _(t):
                for c in range(8):
                    t.wait_ge(S_["s_b0"], 6 + 2 * c)
                    t.matmul(P_B0[:, :], ONESF32,
                             G2ROW[0:1, c * 128:(c + 1) * 128],
                             start=True, stop=True).then_inc(S_["s_b0"], 1)

            @blk.vector
            def _(v):
                for c in range(8):
                    v.wait_ge(S_["s_b0"], 7 + 2 * c)
                    v.tensor_copy(G2T[:, c * 128:(c + 1) * 128], P_B0[:, :])
                    v.engine_nop().then_inc(S_["s_b0"], 1)

        with nc.Block("b0e") as blk:
            @blk.vector
            def _(v):
                v.wait_ge(S_["s_wsmall"], N_WSMALL)
                v.wait_ge(S_["s_const"], N_CONST)
                v.wait_ge(S_["s_constg"], N_CONSTG)
                for dc in range(8):
                    ins = v.tensor_scalar(
                        BFA[:, WQKV_off + dc * 1536:WQKV_off + (dc + 1) * 1536],
                        BFA[:, WQKV_off + dc * 1536:WQKV_off + (dc + 1) * 1536],
                        G1C[:, dc:dc + 1], None, ALU.mult)
                ins.then_inc(S_["s_b0"], 1)
    N_B0 = 23

    with contextlib.ExitStack() as ps:
        P_QKV = ps.enter_context(nc.psum_tensor("P_QKV", [128, 1536], F32))
        P_TR = ps.enter_context(nc.psum_tensor("P_TR", [128, 1024], BF16))
        P_S = ps.enter_context(nc.psum_tensor("P_S", [128, 384], F32))
        P_PTK = ps.enter_context(nc.psum_tensor("P_PTK", [128, 1024], BF16))
        P_AV = ps.enter_context(nc.psum_tensor("P_AV", [128, 128], F32))
        P_WO = ps.enter_context(nc.psum_tensor("P_WO", [128, 512], F32))

        with nc.Block("b1") as blk:
            @blk.sync
            def _(sy):
                for i in range(NT):
                    if i >= 3:
                        sy.wait_ge(S_["sv_evac"], 2 * (i - 3) + 2)
                    sy.dma_start(XR[:, (i % 3) * 1024:(i % 3 + 1) * 1024],
                                 x_in.ap()[i * 128:(i + 1) * 128, :]).then_inc(S_["sx%d" % (i % 3)], 16)

            @blk.gpsimd
            def _(g):
                def stage(k):
                    p, kind = divmod(k, 2)
                    if kind == 0:
                        g.dma_start(st_fcT.ap()[p], dram["WfcT"].ap()[p]).then_inc(S_["s_stg%d" % p], 16)
                    else:
                        g.dma_start(st_pjT.ap()[p], dram["WpjT"].ap()[p]).then_inc(S_["s_stg%d" % p], 16)
                for k in range(2 * NPASS):
                    stage(k)
                g.wait_ge(S_["sp_qkv"], NT)
                gv = BFA[:, WG_o:WG_o + 8192].rearrange("p (c i) -> p c i", c=8)
                g.dma_start(gv, dram["gate_up_proj"].ap()[0].rearrange("(c p) i -> p c i", p=128)[:, :, 0:1024]).then_inc(S_["sw_g"], 16)
                g.wait_ge(S_["sp_wo"], 2 * NT)
                uv = BFA[:, WU_o:WU_o + 8192].rearrange("p (c i) -> p c i", c=8)
                g.dma_start(uv, dram["gate_up_proj"].ap()[0].rearrange("(c p) i -> p c i", p=128)[:, :, 1024:2048]).then_inc(S_["sw_u"], 16)
                dv = BFA[:, WD_o:WD_o + 8192].rearrange("p (c m) -> p c m", c=8)
                g.dma_start(dv, dram["down_proj"].ap()[0].rearrange("(c p) m -> p c m", p=128)).then_inc(S_["sw_d"], 16)
                g.dma_start(DBROW[0:1, :], dram["down_bias"].ap()[0].unsqueeze(0)).then_inc(S_["sw_d"], 16)

            @blk.tensor
            def _(t):
                t.wait_ge(S_["s_wsmall"], N_WSMALL)
                t.wait_ge(S_["s_b0"], N_B0)
                for i in range(NT + 1):
                    j = i - 1
                    if i < NT:
                        t.wait_ge(S_["sv_xhat"], i + 1)
                        if i >= 1:
                            t.wait_ge(S_["ss_cpq"], i)
                        for dc in range(8):
                            ins = t.transpose(P_TR[:, dc * 128:(dc + 1) * 128],
                                              XHATS(i % 2)[:, dc * 128:(dc + 1) * 128],
                                              IDENTB[:, :])
                        ins.then_inc(S_["sp_trx"], 1)
                        t.wait_ge(S_["ss_cpx"], i + 1)
                        if i >= 1:
                            t.wait_ge(S_["sv_ropeall"], i)
                        for nb in range(3):
                            t.matmul(P_QKV[:, nb * 512:(nb + 1) * 512], ONESC[0:1, :],
                                     ROWS[0:1, nb * 512:(nb + 1) * 512],
                                     start=True, stop=False)
                        for dc in range(8):
                            for nb in range(3):
                                ins = t.matmul(P_QKV[:, nb * 512:(nb + 1) * 512],
                                               XF1(i % 2, dc),
                                               BFA[:, WQKV_off + dc * 1536 + nb * 512:
                                                   WQKV_off + dc * 1536 + (nb + 1) * 512],
                                               start=False, stop=(dc == 7))
                        ins.then_inc(S_["sp_qkv"], 1)
                        t.wait_ge(S_["sv_ropek"], i + 1)
                        if i >= 1:
                            t.wait_ge(S_["ss_cpk"], i)
                        t.transpose(P_TR[:, 0:128], KROPE[:, 0:128], IDENTB[:, :])
                        t.transpose(P_TR[:, 128:256], KROPE[:, 128:256], IDENTB[:, :]).then_inc(S_["sp_trk"], 1)

                    if j >= 0:
                        KB = 0 if j == 0 else j - 1
                        nk = 2 if (j == 0 or j == NT - 1) else 3

                        def smm(hh):
                            pb2, half2 = divmod(hh - j * 16, 2)
                            h2 = head_of(pb2, half2)
                            g2_ = (h2 // 4) // 2
                            t.wait_ge(S_["ss_cpq"], j + 1)
                            t.wait_ge(S_["ss_cpk"], min(j + 2, NT))
                            t.wait_ge(S_["sv_ropeall"], min(j + 2, NT))
                            if hh > 0:
                                t.wait_ge(S_["sv_mask"], hh)
                            b = half2 * 64
                            t.matmul(P_S[:, 0:nk * 128],
                                     QRS(j % 2, pb2)[b:b + 64, :],
                                     BFA[b:b + 64,
                                         KFM_o + g2_ * T + KB * 128:
                                         KFM_o + g2_ * T + (KB + nk) * 128],
                                     start=True, stop=True).then_inc(S_["sp_s"], 1)

                        smm(j * 16)
                        for hl in range(16):
                            pb, half = divmod(hl, 2)
                            h = head_of(pb, half)
                            gk = h // 4
                            hh = j * 16 + hl
                            if hl + 1 < 16:
                                smm(hh + 1)
                            t.wait_ge(S_["sv_pn"], hh + 1)
                            if hh >= 1:
                                t.wait_ge(S_["ss_pt"], hh)
                            for k in range(nk):
                                ins = t.transpose(P_PTK[:, k * 128:(k + 1) * 128],
                                                  PBUF[:, (hh % 2) * 384 + k * 128:
                                                       (hh % 2) * 384 + (k + 1) * 128],
                                                  IDENTB[:, :])
                            ins.then_inc(S_["sp_pt"], 1)
                            if half == 0:
                                t.wait_ge(S_["ss_ofm"], j * 8 + pb)
                            t.wait_ge(S_["ss_pt"], hh + 1)
                            first = True
                            for k in range(nk):
                                ktg = KB + k
                                outv = P_AV[64:128, :] if half == 1 else P_AV[0:64, :]
                                tp = (0, 64) if half == 1 else None
                                ins = t.matmul(outv,
                                               BFA[:, VTOK_o + ktg * 256 + gk * 64:
                                                   VTOK_o + ktg * 256 + (gk + 1) * 64],
                                               PTS(hh % 2)[:, k * 128:(k + 1) * 128],
                                               start=first, stop=(k == nk - 1),
                                               tile_position=tp)
                                first = False
                            ins.then_inc(S_["sp_av"], 1)
                        t.wait_ge(S_["ss_ofm"], 8 * (j + 1))
                        for mh in range(2):
                            if 2 * j + mh >= 1:
                                t.wait_ge(S_["sv_evac"], 2 * j + mh)
                            t.matmul(P_WO[:, :], ONESC[0:1, :],
                                     BOROW[0:1, mh * 512:(mh + 1) * 512],
                                     start=True, stop=False)
                            for pb in range(8):
                                ins = t.matmul(P_WO[:, :], OFMS(pb),
                                               BFA[:, WO_off + pb * 1024 + mh * 512:
                                                   WO_off + pb * 1024 + (mh + 1) * 512],
                                               start=False, stop=(pb == 7))
                            ins.then_inc(S_["sp_wo"], 1)

                    if i < NT:
                        t.wait_ge(S_["sv_ropeq"], i + 1)
                        t.wait_ge(S_["ss_cpx"], i + 1)
                        t.wait_ge(S_["ss_cpk"], i + 1)
                        for pb in range(8):
                            ins = t.transpose(P_TR[:, pb * 128:(pb + 1) * 128],
                                              QROPE[:, pb * 128:(pb + 1) * 128],
                                              IDENTB[:, :])
                        ins.then_inc(S_["sp_trq"], 1)

            @blk.vector
            def _(v):
                for i in range(NT + 1):
                    j = i - 1
                    if i < NT:
                        xr = XR[:, (i % 3) * 1024:(i % 3 + 1) * 1024]
                        v.wait_ge(S_["sx%d" % (i % 3)], 16 * (i // 3 + 1))
                        if i >= 1:
                            v.wait_ge(S_["sv_ropeq"], i)
                        vedge(v, v.scalar_tensor_tensor(DUMP[:, :], xr, 1.0, xr,
                                                        ALU.bypass, ALU.mult,
                                                        accum_out=STAT[:, 0:1]))
                        v.tensor_scalar(STAT[:, 1:2], STAT[:, 0:1], 1.0 / D, EPS,
                                        ALU.mult, ALU.add).then_inc(S_["sv_eps"], 1)
                        v.wait_ge(S_["ss_rstd"], i + 1)
                        if i >= 2:
                            v.wait_ge(S_["sp_trx"], i - 1)
                        v.tensor_scalar(XHATS(i % 2), xr, STAT[:, 2:3], None,
                                        ALU.mult).then_inc(S_["sv_xhat"], 1)
                        v.wait_ge(S_["sp_qkv"], i + 1)
                        if i >= 1:
                            v.wait_ge(S_["sp_trq"], i)
                            v.wait_ge(S_["sp_trk"], i)
                        cosb = COSB[:, i * 32:(i + 1) * 32]
                        sinb = SINB[:, i * 32:(i + 1) * 32]
                        ksrc = P_QKV[:, 1024:1280].rearrange(
                            "p (gk hk dd two) -> p gk hk dd two", gk=2, hk=2, two=2)
                        kxe = ksrc[:, :, :, :, 0:1].squeeze(4)
                        kxo = ksrc[:, :, :, :, 1:2].squeeze(4)
                        kcos = cosb.unsqueeze(1).unsqueeze(1).broadcast_to([128, 2, 2, 32])
                        ksin = sinb.unsqueeze(1).unsqueeze(1).broadcast_to([128, 2, 2, 32])
                        kdst = KROPE[:, :].rearrange("p (gk hk ha dd) -> p gk hk ha dd",
                                                     gk=2, hk=2, ha=2)
                        kde = kdst[:, :, :, 0:1, :].squeeze(3)
                        kdo = kdst[:, :, :, 1:2, :].squeeze(3)
                        kt_ = DUMP[:, 0:128].rearrange("p (gk hk dd) -> p gk hk dd",
                                                       gk=2, hk=2)
                        vedge(v, v.tensor_tensor(kde, kxe, kcos, ALU.mult))
                        vedge(v, v.tensor_tensor(kt_, kxo, ksin, ALU.mult))
                        vedge(v, v.tensor_tensor(kde, kde, kt_, ALU.subtract))
                        vedge(v, v.tensor_tensor(kdo, kxo, kcos, ALU.mult))
                        vedge(v, v.tensor_tensor(kt_, kxe, ksin, ALU.mult))
                        v.tensor_tensor(kdo, kdo, kt_, ALU.add).then_inc(S_["sv_ropek"], 1)
                        qcos = cosb.unsqueeze(1).unsqueeze(1).broadcast_to([128, 2, 4, 32])
                        qsin = sinb.unsqueeze(1).unsqueeze(1).broadcast_to([128, 2, 4, 32])
                        for g2i in range(2):
                            qsrc = P_QKV[:, g2i * 512:(g2i + 1) * 512].rearrange(
                                "p (half j4 dd two) -> p half j4 dd two",
                                half=2, j4=4, two=2)
                            qxe = qsrc[:, :, :, :, 0:1].squeeze(4)
                            qxo = qsrc[:, :, :, :, 1:2].squeeze(4)
                            qdst = QROPE[:, g2i * 512:(g2i + 1) * 512].rearrange(
                                "p (j4 half ha dd) -> p j4 half ha dd",
                                j4=4, half=2, ha=2)
                            qde = qdst[:, :, :, 0:1, :].squeeze(3).transpose([0, 2, 1, 3])
                            qdo = qdst[:, :, :, 1:2, :].squeeze(3).transpose([0, 2, 1, 3])
                            qt_ = DUMP[:, 0:256].rearrange(
                                "p (half j4 dd) -> p half j4 dd", half=2, j4=4)
                            vedge(v, v.tensor_tensor(qde, qxe, qcos, ALU.mult))
                            vedge(v, v.tensor_tensor(qt_, qxo, qsin, ALU.mult))
                            vedge(v, v.tensor_tensor(qde, qde, qt_, ALU.subtract))
                            vedge(v, v.tensor_tensor(qdo, qxo, qcos, ALU.mult))
                            vedge(v, v.tensor_tensor(qt_, qxe, qsin, ALU.mult))
                            ins = v.tensor_tensor(qdo, qdo, qt_, ALU.add)
                            if g2i == 0:
                                vedge(v, ins)
                        ins.then_inc(S_["sv_ropeq"], 1)
                        v.tensor_copy(BFA[:, VTOK_o + i * 256:VTOK_o + (i + 1) * 256],
                                      P_QKV[:, 1280:1536]).then_inc(S_["sv_ropeall"], 1)
                    if j >= 0:
                        nk = 2 if (j == 0 or j == NT - 1) else 3
                        c0, c1 = 0, nk * 128
                        for hl in range(16):
                            pb, half = divmod(hl, 2)
                            h = head_of(pb, half)
                            hh = j * 16 + hl
                            r = hh % 2
                            smr = SM[:, r * 384 + c0:r * 384 + c1]
                            v.wait_ge(S_["sp_s"], hh + 1)
                            v.tensor_tensor(smr, P_S[:, c0:c1], mask_ap(j)[:, c0:c1],
                                            ALU.add).then_inc(S_["sv_mask"], 1)
                            v.wait_ge(S_["sv_mask"], hh + 1)
                            vedge(v, v.reduce_max(STAT[:, 4 + r:5 + r], smr,
                                                  axis=AX.X, negate=True))
                            v.tensor_scalar(STAT[:, 6 + r:7 + r], STAT[:, 4 + r:5 + r],
                                            NEGSINK[:, h:h + 1], None,
                                            ALU.min).then_inc(S_["sv_negm"], 1)
                            v.wait_ge(S_["ss_exp"], hh + 1)
                            vedge(v, v.tensor_tensor(STAT[:, 10 + r:11 + r],
                                                     STAT[:, 8 + r:9 + r],
                                                     STAT[:, 12 + r:13 + r], ALU.add))
                            vedge(v, v.reciprocal(STAT[:, 14 + r:15 + r],
                                                  STAT[:, 10 + r:11 + r]))
                            pbv = PBUF[:, r * 384 + c0:r * 384 + c1]
                            v.tensor_scalar(pbv, pbv, STAT[:, 14 + r:15 + r], None,
                                            ALU.mult).then_inc(S_["sv_pn"], 1)
                        for mh in range(2):
                            v.wait_ge(S_["sp_wo"], 2 * j + mh + 1)
                            v.tensor_tensor(A[:, j * 1024 + mh * 512:j * 1024 + (mh + 1) * 512],
                                            P_WO[:, :],
                                            XR[:, (j % 3) * 1024 + mh * 512:
                                               (j % 3) * 1024 + (mh + 1) * 512],
                                            ALU.add).then_inc(S_["sv_evac"], 1)

            @blk.scalar
            def _(s):
                for i in range(NT + 1):
                    j = i - 1
                    if i < NT:
                        s.wait_ge(S_["sv_eps"], i + 1)
                        sedge(s, s.activation(STAT[:, 3:4], STAT[:, 1:2], AF.Ln))
                        s.activation(STAT[:, 2:3], STAT[:, 3:4], AF.Exp,
                                     scale=-0.5).then_inc(S_["ss_rstd"], 1)
                        s.wait_ge(S_["sp_trx"], i + 1)
                        for dc in range(8):
                            ins = s.copy(XF1(i % 2, dc), P_TR[:, dc * 128:(dc + 1) * 128])
                        ins.then_inc(S_["ss_cpx"], 1)
                        s.wait_ge(S_["sp_trk"], i + 1)
                        s.copy(BFA[:, KFM_o + i * 128:KFM_o + (i + 1) * 128],
                               P_TR[:, 0:128])
                        s.copy(BFA[:, KFM_o + T + i * 128:KFM_o + T + (i + 1) * 128],
                               P_TR[:, 128:256]).then_inc(S_["ss_cpk"], 1)
                    if j >= 0:
                        for hl in range(16):
                            pb, half = divmod(hl, 2)
                            h = head_of(pb, half)
                            hh = j * 16 + hl
                            r = hh % 2
                            nk = 2 if (j == 0 or j == NT - 1) else 3
                            s.wait_ge(S_["sv_negm"], hh + 1)
                            if hh >= 2:
                                s.wait_ge(S_["sp_pt"], hh - 1)
                            kc0, kc1 = 0, nk * 128
                            s.activation(PBUF[:, r * 384 + kc0:r * 384 + kc1],
                                         SM[:, r * 384 + kc0:r * 384 + kc1], AF.Exp,
                                         bias=STAT[:, 6 + r:7 + r],
                                         accum_out=STAT[:, 8 + r:9 + r])
                            s.activation(STAT[:, 12 + r:13 + r], SINKB[:, h:h + 1],
                                         AF.Exp, bias=STAT[:, 6 + r:7 + r]).then_inc(S_["ss_exp"], 1)
                            s.wait_ge(S_["sp_pt"], hh + 1)
                            for k in range(nk):
                                ins = s.copy(PTS(r)[:, k * 128:(k + 1) * 128],
                                             P_PTK[:, k * 128:(k + 1) * 128])
                            ins.then_inc(S_["ss_pt"], 1)
                            if half == 1:
                                s.wait_ge(S_["sp_av"], hh + 1)
                                s.copy(OFMS(pb), P_AV[:, :]).then_inc(S_["ss_ofm"], 1)
                    if i < NT:
                        s.wait_ge(S_["sp_trq"], i + 1)
                        for pb in range(8):
                            ins = s.copy(QRS(i % 2, pb), P_TR[:, pb * 128:(pb + 1) * 128])
                        ins.then_inc(S_["ss_cpq"], 1)

    with contextlib.ExitStack() as ps:
        P_TR2 = ps.enter_context(nc.psum_tensor("P_TR2", [128, 1024], BF16))
        P_RT = ps.enter_context(nc.psum_tensor("P_RT", [128, 8], F32))
        with nc.Block("b2") as blk:
            @blk.sync
            def _(syz):
                nzr = NE * C // 128
                for k in range(nzr):
                    syz.dma_start(gdram.ap()[k * 128:(k + 1) * 128, 1024:1040],
                                  ZEROT[:, 0:16]).then_inc(S_["s_zero"], 16)
                if debug:
                    for i in range(NT):
                        syz.dma_start(dbg["dbg_A1"].ap()[i * 128:(i + 1) * 128, :],
                                      A[:, i * 1024:(i + 1) * 1024]).then_inc(S_["s_out"], 16)
                    syz.dma_start(dbg["dbg_SM"].ap(), SM).then_inc(S_["s_out"], 16)
                    syz.wait_ge(S_["s_out"], 16 * (NT + 1))
            if debug:
                @blk.gpsimd
                def _(g):
                    g.dma_start(dbg["dbg_KFM"].ap(), BFA[:, KFM_o:KFM_o + 2 * T]).then_inc(S_["s_dbg"], 16)
                    g.dma_start(dbg["dbg_VT"].ap(), BFA[:, VTOK_o:VTOK_o + NT * 256]).then_inc(S_["s_dbg"], 16)
                    g.dma_start(dbg["dbg_PB"].ap(), PBUF).then_inc(S_["s_dbg"], 16)
                    g.dma_start(dbg["dbg_OF"].ap(), BFA[:, OFMS_o:OFMS_o + 1024]).then_inc(S_["s_dbg"], 16)
                    g.wait_ge(S_["s_dbg"], 64)

            @blk.vector
            def _(v):
                for i in range(NT):
                    at = A[:, i * 1024:(i + 1) * 1024]
                    vedge(v, v.scalar_tensor_tensor(DUMP2[:, :], at, 1.0, at,
                                                    ALU.bypass, ALU.mult,
                                                    accum_out=STAT[:, 0:1]))
                    v.tensor_scalar(STAT[:, 1:2], STAT[:, 0:1], 1.0 / D, EPS,
                                    ALU.mult, ALU.add).then_inc(S_["b2_eps"], 1)
                    v.wait_ge(S_["b2_rstd"], i + 1)
                    if i >= 2:
                        v.wait_ge(S_["b2_trx"], i - 1)
                    v.scalar_tensor_tensor(XH2(i % 2), at, STAT[:, 2:3], G2T,
                                           ALU.mult, ALU.mult).then_inc(S_["b2_xhat"], 1)
                    v.wait_ge(S_["b2_rt"], i + 1)
                    v.tensor_copy(LSC[:, 0:8], P_RT[:, :]).then_inc(S_["b2_sc"], 1)
                    v.wait_ge(S_["b2_sc"], i + 1)
                    vedge(v, v.tensor_copy(LSC[:, 8:16], LSC[:, 0:8]))
                    for it in range(4):
                        vedge(v, v.reduce_max(STAT[:, 4:5], LSC[:, 8:16], axis=AX.X,
                                              negate=True))
                        if it == 0:
                            vedge(v, v.tensor_copy(STAT[:, 5:6], STAT[:, 4:5]))
                        if it < 3:
                            vedge(v, v.tensor_scalar(STAT[:, 6:7], STAT[:, 4:5],
                                                     -1.0, None, ALU.mult))
                            vedge(v, v.tensor_scalar(LSC[:, 16:24], LSC[:, 8:16],
                                                     STAT[:, 6:7], None, ALU.is_ge))
                            vedge(v, v.scalar_tensor_tensor(LSC[:, 8:16], LSC[:, 16:24],
                                                            -BIG, LSC[:, 8:16],
                                                            ALU.mult, ALU.add))
                    vedge(v, v.tensor_scalar(STAT[:, 6:7], STAT[:, 4:5], -1.0, None,
                                             ALU.mult))
                    vedge(v, v.tensor_scalar(LSC[:, 16:24], LSC[:, 0:8], STAT[:, 6:7],
                                             None, ALU.is_ge))
                    vedge(v, v.scalar_tensor_tensor(LSC[:, 24:32], LSC[:, 16:24], BIG,
                                                    LSC[:, 0:8], ALU.mult, ALU.add))
                    v.tensor_scalar(LSC[:, 24:32], LSC[:, 24:32], -BIG, None,
                                    ALU.add).then_inc(S_["b2_lsel"], 1)
                    v.wait_ge(S_["b2_exp"], i + 1)
                    vedge(v, v.reciprocal(STAT[:, 7:8], STAT[:, 9:10]))
                    v.tensor_scalar(SCORES[:, i * 8:(i + 1) * 8], LSC[:, 16:24],
                                    STAT[:, 7:8], 0.25, ALU.mult, ALU.mult)

            @blk.scalar
            def _(s):
                if debug:
                    s.wait_ge(S_["s_dbg"], 64)
                for i in range(NT):
                    s.wait_ge(S_["b2_eps"], i + 1)
                    sedge(s, s.activation(STAT[:, 3:4], STAT[:, 1:2], AF.Ln))
                    s.activation(STAT[:, 2:3], STAT[:, 3:4], AF.Exp,
                                 scale=-0.5).then_inc(S_["b2_rstd"], 1)
                    s.wait_ge(S_["b2_trx"], i + 1)
                    for dc in range(8):
                        ins = s.copy(XF2t(dc, i), P_TR2[:, dc * 128:(dc + 1) * 128])
                    ins.then_inc(S_["b2_cpx"], 1)
                    s.wait_ge(S_["b2_lsel"], i + 1)
                    s.activation(LSC[:, 16:24], LSC[:, 24:32], AF.Exp,
                                 bias=STAT[:, 5:6],
                                 accum_out=STAT[:, 9:10]).then_inc(S_["b2_exp"], 1)

            @blk.tensor
            def _(t):
                for i in range(NT):
                    t.wait_ge(S_["b2_xhat"], i + 1)
                    if i >= 1:
                        t.wait_ge(S_["b2_cpx"], i)
                    for dc in range(8):
                        ins = t.transpose(P_TR2[:, dc * 128:(dc + 1) * 128],
                                          XH2(i % 2)[:, dc * 128:(dc + 1) * 128],
                                          IDENTB[:, :])
                    ins.then_inc(S_["b2_trx"], 1)
                    t.wait_ge(S_["b2_cpx"], i + 1)
                    t.wait_ge(S_["b2_sc"], i)
                    t.matmul(P_RT[:, :], ONESC[0:1, :], ROWS[0:1, 1536:1544],
                             start=True, stop=False)
                    for dc in range(8):
                        ins = t.matmul(P_RT[:, :], XF2t(dc, i),
                                       ROUTW[:, dc * 8:(dc + 1) * 8],
                                       start=False, stop=(dc == 7))
                    ins.then_inc(S_["b2_rt"], 1)

    with contextlib.ExitStack() as ps:
        P_G = ps.enter_context(nc.psum_tensor("P_G", [128, 2048], F32))
        P_U = ps.enter_context(nc.psum_tensor("P_U", [128, 2048], F32))
        with nc.Block("b3") as blk:
            @blk.gpsimd
            def _(g):
                for e in range(1, NE):
                    g.wait_ge(S_["sp_g"], 8 * e)
                    gv = BFA[:, WG_o:WG_o + 8192].rearrange("p (c i) -> p c i", c=8)
                    g.dma_start(gv, dram["gate_up_proj"].ap()[e].rearrange("(c p) i -> p c i", p=128)[:, :, 0:1024]).then_inc(S_["sw_g"], 16)
                    g.wait_ge(S_["sp_u"], 8 * e)
                    uv = BFA[:, WU_o:WU_o + 8192].rearrange("p (c i) -> p c i", c=8)
                    g.dma_start(uv, dram["gate_up_proj"].ap()[e].rearrange("(c p) i -> p c i", p=128)[:, :, 1024:2048]).then_inc(S_["sw_u"], 16)
                    g.wait_ge(S_["sp_d"], 2 * NT * e)
                    dv = BFA[:, WD_o:WD_o + 8192].rearrange("p (c m) -> p c m", c=8)
                    g.dma_start(dv, dram["down_proj"].ap()[e].rearrange("(c p) m -> p c m", p=128)).then_inc(S_["sw_d"], 16)
                    g.dma_start(DBROW[0:1, :], dram["down_bias"].ap()[e].unsqueeze(0)).then_inc(S_["sw_d"], 16)

            @blk.tensor
            def _(t):
                for e in range(NE):
                    t.wait_ge(S_["sw_g"], 16 * (e + 1))
                    t.wait_ge(S_["sv_devac"], 2 * NT * e)
                    for ic in range(8):
                        idx = e * 8 + ic
                        t.wait_ge(S_["sv_gt"], n_tc * idx)
                        for dc in range(8):
                            for tc in range(n_tc):
                                ins = t.matmul(P_G[:, tc * 512:(tc + 1) * 512],
                                               BFA[:, WG_o + dc * 1024 + ic * 128:
                                                   WG_o + dc * 1024 + (ic + 1) * 128],
                                               XF2c(dc, tc * 512, (tc + 1) * 512),
                                               start=(dc == 0), stop=(dc == 7))
                        ins.then_inc(S_["sp_g"], 1)
                        t.wait_ge(S_["sw_u"], 16 * (e + 1))
                        t.wait_ge(S_["sv_up"], n_tc * idx)
                        for dc in range(8):
                            for tc in range(n_tc):
                                ins = t.matmul(P_U[:, tc * 512:(tc + 1) * 512],
                                               BFA[:, WU_o + dc * 1024 + ic * 128:
                                                   WU_o + dc * 1024 + (ic + 1) * 128],
                                               XF2c(dc, tc * 512, (tc + 1) * 512),
                                               start=(dc == 0), stop=(dc == 7))
                        ins.then_inc(S_["sp_u"], 1)
                    t.wait_ge(S_["sv_gated"], 8 * n_tc * (e + 1))
                    t.wait_ge(S_["sw_d"], 32 * (e + 1))
                    for tt in range(NT):
                        t.wait_ge(S_["sv_devac"], 2 * (e * NT + tt))
                        for mh in range(2):
                            t.matmul(P_G[:, mh * 512:(mh + 1) * 512], ONESC[0:1, :],
                                     DBROW[0:1, mh * 512:(mh + 1) * 512],
                                     start=True, stop=False)
                            for ic in range(8):
                                ins = t.matmul(P_G[:, mh * 512:(mh + 1) * 512],
                                               GATED(ic, tt * 128, (tt + 1) * 128),
                                               BFA[:, WD_o + ic * 1024 + mh * 512:
                                                   WD_o + ic * 1024 + (mh + 1) * 512],
                                               start=False, stop=(ic == 7))
                            ins.then_inc(S_["sp_d"], 1)

            @blk.vector
            def _(v):
                for e in range(NE):
                    for ic in range(8):
                        idx = e * 8 + ic
                        for tc in range(n_tc):
                            r = (idx * n_tc + tc) % 2
                            v.wait_ge(S_["sp_g"], idx + 1)
                            if e >= 1:
                                v.wait_ge(S_["sp_d"], 2 * NT * e)
                            gb = GUPB[:, ic * 8 + e:ic * 8 + e + 1]
                            ub = GUPB[:, 64 + ic * 8 + e:64 + ic * 8 + e + 1]
                            v.tensor_scalar(GTS[:, r * 512:(r + 1) * 512],
                                            P_G[:, tc * 512:(tc + 1) * 512],
                                            gb, LIMIT, ALU.add,
                                            ALU.min).then_inc(S_["sv_gt"], 1)
                            v.wait_ge(S_["sp_u"], idx + 1)
                            vedge(v, v.tensor_scalar(UP2S[:, r * 512:(r + 1) * 512],
                                                     P_U[:, tc * 512:(tc + 1) * 512],
                                                     ub, -LIMIT, ALU.add, ALU.max))
                            v.tensor_scalar(UP2S[:, r * 512:(r + 1) * 512],
                                            UP2S[:, r * 512:(r + 1) * 512],
                                            LIMIT, 1.0, ALU.min,
                                            ALU.add).then_inc(S_["sv_up"], 1)
                            v.wait_ge(S_["ss_glu"], idx * n_tc + tc + 1)
                            v.wait_ge(S_["sv_up"], idx * n_tc + tc + 1)
                            v.tensor_tensor(GATED(ic, tc * 512, (tc + 1) * 512),
                                            GLUS[:, r * 512:(r + 1) * 512],
                                            UP2S[:, r * 512:(r + 1) * 512],
                                            ALU.mult).then_inc(S_["sv_gated"], 1)
                    for tt in range(NT):
                        for mh in range(2):
                            v.wait_ge(S_["sp_d"], (e * NT + tt) * 2 + mh + 1)
                            v.scalar_tensor_tensor(
                                A[:, tt * 1024 + mh * 512:tt * 1024 + (mh + 1) * 512],
                                P_G[:, mh * 512:(mh + 1) * 512],
                                SCORES[:, tt * 8 + e:tt * 8 + e + 1],
                                A[:, tt * 1024 + mh * 512:tt * 1024 + (mh + 1) * 512],
                                ALU.mult, ALU.add).then_inc(S_["sv_devac"], 1)

            @blk.scalar
            def _(s):
                for e in range(NE):
                    for ic in range(8):
                        idx = e * 8 + ic
                        for tc in range(n_tc):
                            r = (idx * n_tc + tc) % 2
                            s.wait_ge(S_["sv_gt"], idx * n_tc + tc + 1)
                            s.activation(GLUS[:, r * 512:(r + 1) * 512],
                                         GTS[:, r * 512:(r + 1) * 512],
                                         AF.Gelu_apprx_sigmoid).then_inc(S_["ss_glu"], 1)

    with contextlib.ExitStack() as ps:
        P_TR4 = ps.enter_context(nc.psum_tensor("P_TR4", [128, 1024], BF16))
        P_RT4 = ps.enter_context(nc.psum_tensor("P_RT4", [128, 8], F32))
        P_IDX = ps.enter_context(nc.psum_tensor("P_IDX", [128, 24], F32))
        with nc.Block("b4") as blk:
            @blk.sync
            def _(sy):
                if debug:
                    for i in range(NT):
                        sy.dma_start(dbg["dbg_A3"].ap()[i * 128:(i + 1) * 128, :],
                                     A[:, i * 1024:(i + 1) * 1024]).then_inc(S_["s_out"], 16)
                    sy.dma_start(dbg["dbg_SC"].ap(), SCORES[:, :]).then_inc(S_["s_out"], 16)
                    sy.wait_ge(S_["s_out"], 16 * (2 * NT + 2))
                for i in range(NT):
                    if i >= 3:
                        sy.wait_ge(S_["b2_xhat"], NT + i - 2)
                    sy.dma_start(XR[:, (i % 3) * 1024:(i % 3 + 1) * 1024],
                                 x_in.ap()[i * 128:(i + 1) * 128, :]).then_inc(S_["b4x%d" % (i % 3)], 16)

            @blk.gpsimd
            def _(g):
                rbnd = g.alloc_register("b4_bnd")
                g.reg_mov(rbnd, NE * C - 1)
                g.wait_ge(S_["s_zero"], 16 * (NE * C // 128))
                for i in range(NT):
                    g.wait_ge(S_["b4_cmb"], i + 1)
                    g.wait_ge(S_["b4d_v"], 2 * i + 2)
                    for e in range(NE):
                        g.indirect_dma_start(
                            gdram.ap(),
                            bass.IndirectOffsetOnAxis(
                                IDX32[:, i * 8 + e:i * 8 + e + 1], 0),
                            XSrow(i % 2), None,
                            bounds_check=rbnd,
                            oob_is_err=False).then_inc(
                                S_["s_scat" if i % 2 == 0 else "s_scat1"], 16)

            @blk.vector
            def _(v):
                for i in range(NT):
                    xr = XR[:, (i % 3) * 1024:(i % 3 + 1) * 1024]
                    at = A[:, i * 1024:(i + 1) * 1024]
                    v.wait_ge(S_["b4x%d" % (i % 3)], 16 * (i // 3 + 1))
                    vedge(v, v.tensor_tensor(xr, xr, at, ALU.add))
                    vedge(v, v.tensor_scalar(DUMP2[:, :], xr, 1.0, None, ALU.mult,
                                             ALU.add, accum_out=STAT[:, 0:1]))
                    vedge(v, v.scalar_tensor_tensor(DUMP2[:, :], xr, 1.0, xr,
                                                    ALU.bypass, ALU.mult,
                                                    accum_out=STAT[:, 1:2]))
                    vedge(v, v.tensor_scalar(STAT[:, 2:3], STAT[:, 0:1], -1.0 / D,
                                             None, ALU.mult))
                    vedge(v, v.tensor_tensor(STAT[:, 3:4], STAT[:, 2:3], STAT[:, 2:3],
                                             ALU.mult))
                    v.scalar_tensor_tensor(STAT[:, 4:5], STAT[:, 1:2], 1.0 / D,
                                           STAT[:, 3:4], ALU.mult,
                                           ALU.subtract).then_inc(S_["b2_eps"], 1)
                    v.wait_ge(S_["b2_rstd"], NT + i + 1)
                    if i >= 2:
                        v.wait_ge(S_["b2_trx"], NT + i - 1)
                        v.wait_ge(S_["s_scat" if i % 2 == 0 else "s_scat1"],
                                  16 * 8 * (i // 2))
                    vedge(v, v.tensor_scalar(STAT[:, 7:8], STAT[:, 6:7], 2.0, None,
                                             ALU.mult))
                    vedge(v, v.tensor_scalar(XH2(i % 2), xr, STAT[:, 2:3],
                                             STAT[:, 6:7], ALU.add, ALU.mult))
                    v.tensor_scalar(at, xr, STAT[:, 2:3], STAT[:, 7:8],
                                    ALU.add, ALU.mult).then_inc(S_["b2_xhat"], 1)
                    v.wait_ge(S_["b4_rt"], i + 1)
                    v.tensor_copy(LSC[:, 0:8], P_RT4[:, :]).then_inc(S_["b4_sc"], 1)
                    v.wait_ge(S_["b4_sc"], i + 1)
                    vedge(v, v.tensor_tensor(LSC[:, 8:16], LSC[:, 0:8], RBB[:, :],
                                             ALU.add))
                    vedge(v, v.reduce_max(STAT[:, 10:11], LSC[:, 8:16], axis=AX.X,
                                          negate=True))
                    vedge(v, v.tensor_scalar(STAT[:, 11:12], STAT[:, 10:11], -1.0,
                                             None, ALU.mult))
                    vedge(v, v.tensor_scalar(LSC[:, 16:24], LSC[:, 8:16],
                                             STAT[:, 11:12], None, ALU.is_ge))
                    vedge(v, v.scalar_tensor_tensor(LSC[:, 24:32], LSC[:, 16:24],
                                                    1.0, ERNG8[:, :],
                                                    ALU.mult, ALU.mult,
                                                    accum_out=STAT[:, 26:27]))
                    vedge(v, v.scalar_tensor_tensor(LSC[:, 24:32], LSC[:, 16:24],
                                                    -BIG, LSC[:, 8:16],
                                                    ALU.mult, ALU.add))
                    vedge(v, v.reduce_max(STAT[:, 12:13], LSC[:, 24:32], axis=AX.X,
                                          negate=True))
                    vedge(v, v.tensor_scalar(STAT[:, 13:14], STAT[:, 12:13], -1.0,
                                             None, ALU.mult))
                    vedge(v, v.tensor_scalar(LSC[:, 16:24], LSC[:, 8:16],
                                             STAT[:, 13:14], None, ALU.is_ge))
                    v.wait_ge(S_["b4_sig"], i + 1)
                    vedge(v, v.scalar_tensor_tensor(LSC[:, 24:32], LSC[:, 16:24], BIG,
                                                    STAT[:, 16:24], ALU.mult, ALU.add))
                    vedge(v, v.tensor_scalar(LSC[:, 24:32], LSC[:, 24:32], -BIG, None,
                                             ALU.add))
                    vedge(v, v.reduce_max(STAT[:, 14:15], LSC[:, 24:32], axis=AX.X,
                                          negate=True))
                    v.engine_nop().then_inc(S_["b4_rsel"], 1)
                    v.wait_ge(S_["b4_exp"], i + 1)
                    vedge(v, v.reciprocal(STAT[:, 25:26], STAT[:, 15:16]))
                    vedge(v, v.tensor_scalar(COMB[:, i * 8:(i + 1) * 8], LSC[:, 16:24],
                                             STAT[:, 25:26], None, ALU.mult))
                    vedge(v, v.tensor_scalar(XSe(i % 2)[:, 0:8], LSC[:, 16:24],
                                             STAT[:, 25:26], None, ALU.mult))
                    vedge(v, v.tensor_copy(XSe(i % 2)[:, 10:11], STAT[:, 26:27]))
                    v.tensor_copy(XSe(i % 2)[:, 8:10],
                                  TIDC[:, 2 * i:2 * i + 2]).then_inc(S_["b4_cmb"], 1)
                    vedge(v, v.tensor_scalar(SELF[:, :], COMB[:, i * 8:(i + 1) * 8],
                                             0.0, None, ALU.is_gt))
                    v.tensor_copy(SELB[:, :], SELF[:, :]).then_inc(S_["b4d_v"], 1)
                    v.wait_ge(S_["b4d_p"], i + 1)
                    vedge(v, v.tensor_copy(LSC[:, 24:32], P_IDX[:, 8:16]))
                    vedge(v, v.tensor_tensor(LSC[:, 0:8], P_IDX[:, 0:8],
                                             LSC[:, 24:32], ALU.add))
                    vedge(v, v.tensor_scalar(LSC[:, 16:24], LSC[:, 0:8], float(C),
                                             None, ALU.is_le))
                    vedge(v, v.tensor_tensor(LSC[:, 0:8], LSC[:, 0:8], EOFFB[:, :],
                                             ALU.add))
                    vedge(v, v.tensor_scalar(LSC[:, 8:16], LSC[:, 0:8], 2e6, None,
                                             ALU.add))
                    vedge(v, v.scalar_tensor_tensor(LSC[:, 8:16], SELF[:, :], -1e6,
                                                    LSC[:, 8:16], ALU.mult, ALU.add))
                    vedge(v, v.scalar_tensor_tensor(LSC[:, 8:16], LSC[:, 16:24],
                                                    -1e6, LSC[:, 8:16],
                                                    ALU.mult, ALU.add))
                    vedge(v, v.tensor_copy(IDX32[:, i * 8:(i + 1) * 8], LSC[:, 8:16]))
                    v.tensor_tensor(BASE[0:1, :], P_IDX[0:1, 16:24], BASE[0:1, :],
                                    ALU.add).then_inc(S_["b4d_v"], 1)

            @blk.scalar
            def _(s):
                for i in range(NT):
                    s.wait_ge(S_["b2_eps"], NT + i + 1)
                    sedge(s, s.activation(STAT[:, 5:6], STAT[:, 4:5], AF.Ln,
                                          bias=EPSC[:, 0:1]))
                    s.activation(STAT[:, 6:7], STAT[:, 5:6], AF.Exp,
                                 scale=-0.5).then_inc(S_["b2_rstd"], 1)
                    s.wait_ge(S_["b2_trx"], NT + i + 1)
                    for dc in range(8):
                        ins = s.copy(XF2t(dc, i), P_TR4[:, dc * 128:(dc + 1) * 128])
                    ins.then_inc(S_["b2_cpx"], 1)
                    s.wait_ge(S_["b4_sc"], i + 1)
                    s.activation(STAT[:, 16:24], LSC[:, 0:8],
                                 AF.Sigmoid).then_inc(S_["b4_sig"], 1)
                    s.wait_ge(S_["b4_rsel"], i + 1)
                    s.activation(LSC[:, 16:24], LSC[:, 24:32], AF.Exp,
                                 bias=STAT[:, 14:15],
                                 accum_out=STAT[:, 15:16]).then_inc(S_["b4_exp"], 1)

            @blk.tensor
            def _(t):
                t.wait_ge(S_["s_b0"], 23)
                for i in range(NT):
                    t.wait_ge(S_["b2_xhat"], NT + i + 1)
                    if i >= 1:
                        t.wait_ge(S_["b2_cpx"], NT + i)
                    for dc in range(8):
                        ins = t.transpose(P_TR4[:, dc * 128:(dc + 1) * 128],
                                          XH2(i % 2)[:, dc * 128:(dc + 1) * 128],
                                          IDENTB[:, :])
                    ins.then_inc(S_["b2_trx"], 1)
                    t.wait_ge(S_["b2_cpx"], NT + i + 1)
                    t.wait_ge(S_["b4_sc"], i)
                    for dc in range(8):
                        ins = t.matmul(P_RT4[:, :], XF2t(dc, i),
                                       CENTT[:, dc * 8:(dc + 1) * 8],
                                       start=(dc == 0), stop=(dc == 7))
                    ins.then_inc(S_["b4_rt"], 1)
                    t.wait_ge(S_["b4d_v"], 2 * i + 1)
                    t.matmul(P_IDX[:, 0:8], UTRIB[:, :], SELB[:, :],
                             start=True, stop=True)
                    t.matmul(P_IDX[:, 8:16], ONESCF[0:1, :], BASE[0:1, :],
                             start=True, stop=True)
                    t.matmul(P_IDX[0:1, 16:24], ONECOL[:, :], SELF[:, :],
                             start=True, stop=True).then_inc(S_["b4d_p"], 1)

    PJG = 4 if T >= 2048 else 2
    NGX = 9 * NCC
    GXFM_o = 0
    EOB_o = 8 * C
    GXfm = lambda dc, c0, c1: BFA[:, GXFM_o + dc * C + c0:GXFM_o + dc * C + c1]
    EOB = lambda sl: BFA[:, EOB_o + sl * PJG * 1024:EOB_o + (sl + 1) * PJG * 1024]
    HRs = lambda fc, c0, c1: BFA[:, HR_o + fc * C + c0:HR_o + fc * C + c1]
    assert EOB_o + 2 * PJG * 1024 <= 8 * T

    fcg = []
    for p in range(NPASS):
        dense = p < NS
        toks = Tq if dense else C
        for half in range(n_half if dense else 1):
            for fsb in range(16):
                tcs = [(o, min(512, toks - o)) for o in range(0, toks, 512)]
                fcg.append((p, dense, half, fsb, tcs))
    gel_pre = [0]
    for (_, _, _, _, tcs) in fcg:
        gel_pre.append(gel_pre[-1] + 2 * len(tcs))
    half_start = {}
    for g, (p, dense, half, fsb, _) in enumerate(fcg):
        if fsb == 0:
            half_start[(p, half)] = g
    NDFC = NS * n_half * 16

    sqg = []
    for p in range(NPASS):
        dense = p < NS
        toks = Tq if dense else C
        nch = toks // 128
        for half in range(n_half if dense else 1):
            chunk_ids = list(range(nch))
            step = 4 if dense else PJG
            grps = [chunk_ids[k:k + step] for k in range(0, nch, step)]
            for sq, chs in enumerate(grps):
                sqg.append((p, dense, half, sq, chs))
    pev_pre = [0]
    pfc_pre = [0]
    for (p_, d_, _, _, chs) in sqg:
        pev_pre.append(pev_pre[-1] + 2 * len(chs))
        pfc_pre.append(pfc_pre[-1] + 32)
    projstart_gsq = {}
    passstart_pfc = {}
    for gsq, (p, dense, half, sq, _) in enumerate(sqg):
        if sq == 0:
            projstart_gsq.setdefault((p, half), gsq)
        if half == 0 and sq == 0:
            passstart_pfc.setdefault(p, pfc_pre[gsq])
    N_GSQ = len(sqg)
    GSQ0 = projstart_gsq[(NS, 0)]
    N_EO = NCC * NE

    with contextlib.ExitStack() as ps:
        P_BIG = ps.enter_context(nc.psum_tensor("P_BIG", [128, 4096], F32))
        with nc.Block("b5") as blk:
            @blk.gpsimd
            def _(g):
                rbnd5 = g.alloc_register("b5_bnd")
                g.reg_mov(rbnd5, 2 * T - 1)
                for gsq in range(GSQ0, N_GSQ):
                    p, d_, h_, sq_, chs_ = sqg[gsq]
                    e = p - NS
                    for ci, ch in enumerate(chs_):
                        g.wait_ge(S_["sv_pevac"], pev_pre[gsq] + 2 * (ci + 1))
                        g.indirect_dma_start(
                            rout3.ap(),
                            bass.IndirectOffsetOnAxis(
                                ITC[:, e * 8 + ch:e * 8 + ch + 1], 0),
                            EOB(gsq % 2)[:, ci * 1024:(ci + 1) * 1024], None,
                            bounds_check=rbnd5,
                            oob_is_err=False).then_inc(
                                S_["s_eo" if gsq % 2 == 0 else "s_eo1"], 16)

            @blk.scalar
            def _(s):
                def issue_load(g):
                    p_ = fcg[g][0]
                    fsb_ = fcg[g][3]
                    s.wait_ge(S_["s_stg%d" % p_], 32)
                    s.dma_start(WFCs(g % 2)[:, :].rearrange("p (c f) -> p c f", c=8),
                                st_fcT.ap()[p_][fsb_]).then_inc(S_["swf%d" % (g % 2)], 16)

                issue_load(0)
                issue_load(1)
                s.wait_ge(S_["s_scat"], 16 * 8 * (NT - NT // 2))
                s.wait_ge(S_["s_scat1"], 16 * 8 * (NT // 2))
                for g in range(len(fcg)):
                    if g + 2 < len(fcg):
                        s.wait_ge(S_["sp_fc"], g + 1)
                        issue_load(g + 2)
                    s.wait_ge(S_["sp_fc"], g + 1)
                    p_, dense_, half_, fsb_, tcs_ = fcg[g]
                    for fsub in range(2):
                        for tci, (o_, l_) in enumerate(tcs_):
                            bank = (g % 2) * 4 + fsub * 2 + tci
                            if dense_:
                                hrv = HR(fsb_ * 2 + fsub, o_, o_ + l_)
                            else:
                                hrv = HRs(fsb_ * 2 + fsub, o_, o_ + l_)
                            s.activation(hrv, P_BIG[:, bank * 512:bank * 512 + l_],
                                         AF.Gelu).then_inc(S_["ss_hr"], 1)

            @blk.sync
            def _(sy):
                if debug:
                    for i in range(NT):
                        sy.dma_start(dbg["dbg_A4"].ap()[i * 128:(i + 1) * 128, :],
                                     A[:, i * 1024:(i + 1) * 1024]).then_inc(S_["s_out"], 16)
                    sy.dma_start(dbg["dbg_CB"].ap(), COMB[:, :]).then_inc(S_["s_out"], 16)
                    sy.wait_ge(S_["s_out"], 16 * (3 * NT + 3))
                gsq_iter2 = 0
                for p in range(NPASS):
                    dense = p < NS
                    if not dense:
                        e = p - NS
                        if e == 0:
                            sy.wait_ge(S_["s_zero"], 16 * (NE * C // 128))
                            sy.wait_ge(S_["s_scat"], 16 * 8 * (NT - NT // 2))
                            sy.wait_ge(S_["s_scat1"], 16 * 8 * (NT // 2))
                        sy.wait_ge(S_["sp_fc"], NDFC + 16 * e)
                        sgx = S_["s_gx" if e % 2 == 0 else "s_gx1"]
                        for cc in range(NCC):
                            for dc in range(8):
                                sy.dma_start_transpose(
                                    GXfm(dc, cc * 128, (cc + 1) * 128),
                                    gdram.ap()[e * C + cc * 128:
                                               e * C + (cc + 1) * 128,
                                               dc * 128:(dc + 1) * 128]).then_inc(sgx, 16)
                            sy.dma_start(
                                GXC[:, (e % 2) * 128 + cc * 16:
                                    (e % 2) * 128 + (cc + 1) * 16],
                                gdram.ap()[e * C + cc * 128:e * C + (cc + 1) * 128,
                                           1024:1040]).then_inc(sgx, 16)
                    for half in range(n_half if dense else 1):
                        nsq = len([1 for s_ in sqg
                                   if s_[0] == p and s_[2] == half])
                        for sq in range(nsq):
                            gsq = gsq_iter2
                            gsq_iter2 += 1
                            sy.wait_ge(S_["s_stg%d" % p], 32)
                            for fc in range(32):
                                pfc = gsq * 32 + fc
                                if pfc >= 4:
                                    sy.wait_ge(S_["sp_pj"], pfc - 3)
                                sy.dma_start(WPJs(pfc % 4),
                                             st_pjT.ap()[p][fc]).then_inc(S_["swp%d" % (pfc % 4)], 16)

            @blk.tensor
            def _(t):
                gsq_iter = 0
                for p in range(NPASS):
                    dense = p < NS
                    for half in range(n_half if dense else 1):
                        gh0 = half_start[(p, half)]
                        gsq0 = projstart_gsq[(p, half)]
                        if not dense:
                            e = p - NS
                            t.wait_ge(S_["s_gx" if e % 2 == 0 else "s_gx1"],
                                      16 * NGX * (e // 2 + 1))
                        for fsb in range(16):
                            g = gh0 + fsb
                            p_, dense_, half_, fsb_, tcs_ = fcg[g]
                            t.wait_ge(S_["swf%d" % (g % 2)], 16 * (g // 2 + 1))
                            if fsb < 2:
                                t.wait_ge(S_["sv_pevac"], pev_pre[gsq0])
                            if g >= 2:
                                t.wait_ge(S_["ss_hr"], gel_pre[g - 1])
                            bq_ = (g % 2) * 4
                            for fsub in range(2):
                                for dc in range(8):
                                    for tci, (o_, l_) in enumerate(tcs_):
                                        bank = bq_ + fsub * 2 + tci
                                        if dense_:
                                            rhsv = XF2c(dc, half * Tq + o_,
                                                        half * Tq + o_ + l_)
                                        else:
                                            rhsv = GXfm(dc, o_, o_ + l_)
                                        ins = t.matmul(
                                            P_BIG[:, bank * 512:bank * 512 + l_],
                                            WFCs(g % 2)[:, dc * 256 + fsub * 128:
                                                        dc * 256 + (fsub + 1) * 128],
                                            rhsv,
                                            start=(dc == 0), stop=(dc == 7))
                            ins.then_inc(S_["sp_fc"], 1)
                        t.wait_ge(S_["ss_hr"], gel_pre[gh0 + 16])
                        nsq = len([1 for s_ in sqg
                                   if s_[0] == p and s_[2] == half])
                        for sq in range(nsq):
                            p2, d2, h2, sq2, chs = sqg[gsq_iter]
                            gsq = gsq_iter
                            gsq_iter += 1
                            for fc in range(32):
                                pfc = gsq * 32 + fc
                                t.wait_ge(S_["swp%d" % (pfc % 4)], 16 * (pfc // 4 + 1))
                                if fc == 0 and gsq >= 1:
                                    t.wait_ge(S_["sv_pevac"], pev_pre[gsq])
                                for ci, ch in enumerate(chs):
                                    for mh in range(2):
                                        bank = ci * 2 + mh
                                        if dense:
                                            hrv = HR(fc, ch * 128, (ch + 1) * 128)
                                        else:
                                            hrv = HRs(fc, ch * 128, (ch + 1) * 128)
                                        ins = t.matmul(
                                            P_BIG[:, bank * 512:(bank + 1) * 512],
                                            hrv,
                                            WPJs(pfc % 4)[:, mh * 512:(mh + 1) * 512],
                                            start=(fc == 0), stop=(fc == 31))
                                ins.then_inc(S_["sp_pj"], 1)

            @blk.vector
            def _(v):
                gsq_iter = 0
                for p in range(NPASS):
                    dense = p < NS
                    if not dense:
                        e = p - NS
                        v.wait_ge(S_["s_gx" if e % 2 == 0 else "s_gx1"],
                                  16 * NGX * (e // 2 + 1))
                        for cc in range(NCC):
                            col = lambda c: GXC[:, (e % 2) * 128 + cc * 16 + c:
                                                (e % 2) * 128 + cc * 16 + c + 1]
                            vedge(v, v.tensor_scalar(GCF[:, e * 8 + cc:
                                                         e * 8 + cc + 1],
                                                     col(e), 1.0, None,
                                                     ALU.mult))
                            vedge(v, v.tensor_scalar(LSC[:, 0:1], col(8), 128.0,
                                                     None, ALU.mult))
                            vedge(v, v.scalar_tensor_tensor(LSC[:, 1:2], col(9),
                                                            2.0, LSC[:, 0:1],
                                                            ALU.mult, ALU.add))
                            vedge(v, v.tensor_scalar(LSC[:, 2:3], col(10),
                                                     float(e), None, ALU.is_equal))
                            vedge(v, v.tensor_scalar(LSC[:, 3:4], LSC[:, 1:2], 1.0,
                                                     None, ALU.add))
                            vedge(v, v.scalar_tensor_tensor(LSC[:, 4:5], LSC[:, 2:3],
                                                            -1.0, LSC[:, 3:4],
                                                            ALU.mult, ALU.add))
                            vedge(v, v.tensor_copy(ITC[:, e * 8 + cc:
                                                       e * 8 + cc + 1],
                                                   LSC[:, 4:5]))
                    for half in range(n_half if dense else 1):
                        nsq = len([1 for s_ in sqg
                                   if s_[0] == p and s_[2] == half])
                        for sq in range(nsq):
                            p2, d2, h2, sq2, chs = sqg[gsq_iter]
                            gsq = gsq_iter
                            gsq_iter += 1
                            v.wait_ge(S_["sp_pj"], (gsq + 1) * 32)
                            if not dense and gsq >= GSQ0 + 2:
                                npar = sum(len(sqg[g2][4])
                                           for g2 in range(GSQ0, gsq - 1)
                                           if g2 % 2 == gsq % 2)
                                if npar:
                                    v.wait_ge(S_["s_eo" if gsq % 2 == 0
                                                 else "s_eo1"], 16 * npar)
                            for ci, ch in enumerate(chs):
                                for mh in range(2):
                                    bank = ci * 2 + mh
                                    if dense:
                                        tglob = half * ntt_q + ch
                                        av = A[:, tglob * 1024 + mh * 512:
                                               tglob * 1024 + (mh + 1) * 512]
                                        ins = v.tensor_tensor(
                                            av, P_BIG[:, bank * 512:(bank + 1) * 512],
                                            av, ALU.add)
                                    else:
                                        e = p - NS
                                        ins = v.tensor_scalar(
                                            EOB(gsq % 2)[:, ci * 1024 + mh * 512:
                                                         ci * 1024 + (mh + 1) * 512],
                                            P_BIG[:, bank * 512:(bank + 1) * 512],
                                            GCF[:, e * 8 + ch:
                                                e * 8 + ch + 1],
                                            None, ALU.mult)
                                    ins.then_inc(S_["sv_pevac"], 1)

    ROUTB = lambda r: BFA[:, r * 2048:(r + 1) * 2048]
    with nc.Block("b6") as blk:
        @blk.gpsimd
        def _(g):
            NEO0 = sum(len(sqg[g2][4]) for g2 in range(GSQ0, N_GSQ)
                       if g2 % 2 == 0)
            NEO1 = N_EO - NEO0
            g.wait_ge(S_["s_eo"], 16 * NEO0)
            if NEO1:
                g.wait_ge(S_["s_eo1"], 16 * NEO1)
            for i in range(1, NT, 2):
                if i >= 2:
                    g.wait_ge(S_["b6a"], i - 1)
                g.dma_start(ROUTB(i % 2).rearrange("p (two d) -> p two d", two=2),
                            rout3.ap()[2 * i * 128:2 * (i + 1) * 128, :]
                            .rearrange("(p two) d -> p two d", p=128)).then_inc(S_["b6g"], 16)

        @blk.vector
        def _(v):
            for i in range(NT):
                at = A[:, i * 1024:(i + 1) * 1024]
                if i >= 3:
                    v.wait_ge(S_["so%d" % (i % 3)], 16 * (i // 3))
                v.wait_ge(S_["b6r" if i % 2 == 0 else "b6g"], 16 * (i // 2 + 1))
                vedge(v, v.tensor_tensor(at, ROUTB(i % 2)[:, 0:1024], at, ALU.add))
                vedge(v, v.tensor_tensor(at, ROUTB(i % 2)[:, 1024:2048], at,
                                         ALU.add))
                v.engine_nop().then_inc(S_["b6a"], 1)
                vedge(v, v.tensor_scalar(DUMP2[:, :], at, 1.0, None, ALU.mult,
                                         ALU.add, accum_out=STAT[:, 0:1]))
                vedge(v, v.scalar_tensor_tensor(DUMP2[:, :], at, 1.0, at,
                                                ALU.bypass, ALU.mult,
                                                accum_out=STAT[:, 1:2]))
                vedge(v, v.tensor_scalar(STAT[:, 2:3], STAT[:, 0:1], -1.0 / D,
                                         None, ALU.mult))
                vedge(v, v.tensor_tensor(STAT[:, 3:4], STAT[:, 2:3], STAT[:, 2:3],
                                         ALU.mult))
                v.scalar_tensor_tensor(STAT[:, 4:5], STAT[:, 1:2], 1.0 / D,
                                       STAT[:, 3:4], ALU.mult,
                                       ALU.subtract).then_inc(S_["b2_eps"], 1)
                v.wait_ge(S_["b2_rstd"], 2 * NT + i + 1)
                v.tensor_scalar(XR[:, (i % 3) * 1024:(i % 3 + 1) * 1024], at,
                                STAT[:, 2:3], STAT[:, 6:7], ALU.add,
                                ALU.mult).then_inc(S_["b2_sc"], 1)

        @blk.scalar
        def _(s):
            NEO0 = sum(len(sqg[g2][4]) for g2 in range(GSQ0, N_GSQ)
                       if g2 % 2 == 0)
            NEO1 = N_EO - NEO0
            s.wait_ge(S_["s_eo"], 16 * NEO0)
            if NEO1:
                s.wait_ge(S_["s_eo1"], 16 * NEO1)
            for i in range(NT):
                if i % 2 == 0:
                    if i >= 2:
                        s.wait_ge(S_["b6a"], i - 1)
                    s.dma_start(ROUTB(i % 2).rearrange("p (two d) -> p two d", two=2),
                                rout3.ap()[2 * i * 128:2 * (i + 1) * 128, :]
                                .rearrange("(p two) d -> p two d", p=128)).then_inc(S_["b6r"], 16)
                s.wait_ge(S_["b2_eps"], 2 * NT + i + 1)
                sedge(s, s.activation(STAT[:, 5:6], STAT[:, 4:5], AF.Ln, bias=EPSC[:, 0:1]))
                s.activation(STAT[:, 6:7], STAT[:, 5:6], AF.Exp,
                             scale=-0.5).then_inc(S_["b2_rstd"], 1)

        @blk.sync
        def _(sy):
            for i in range(NT):
                sy.wait_ge(S_["b2_sc"], NT + i + 1)
                sy.dma_start(out_t.ap()[i * 128:(i + 1) * 128, :],
                             XR[:, (i % 3) * 1024:(i % 3 + 1) * 1024]).then_inc(S_["so%d" % (i % 3)], 16)
            for r in range(3):
                n_r = len([i for i in range(NT) if i % 3 == r])
                sy.wait_ge(S_["so%d" % r], 16 * n_r)

    return nc


def make_consts(T):
    pos = np.arange(T, dtype=np.float64)
    inv = 1.0 / (150000.0 ** (np.arange(32, dtype=np.float64) / 32.0))
    ang = pos[:, None] * inv[None, :]
    cosw = (np.cos(ang) * SCALE).astype(np.float32)
    sinw = (np.sin(ang) * SCALE).astype(np.float32)
    ql = np.arange(128)
    kl = np.arange(384)
    band = (kl[None, :] >= ql[:, None]) & (kl[None, :] <= ql[:, None] + 2 * SW)
    m_int = np.where(band, 0.0, -BIG).astype(np.float32)
    m_first = np.where(kl[None, :] <= ql[:, None] + SW, 0.0, -BIG).astype(np.float32)
    m_first[:, 256:384] = -BIG
    m_last = m_int.copy(); m_last[:, 256:384] = -BIG
    masks = np.stack([m_int, m_first, m_last]).astype(np.float32)
    ident = np.eye(128, dtype=np.float32)
    return cosw, sinw, masks, ident


_NC_CACHE = {}


def _fcT(w):
    n = w.shape[0]
    return np.ascontiguousarray(
        w.reshape(n, 16, 256, 8, 128).transpose(0, 1, 4, 3, 2))


def _pjT(w):
    n = w.shape[0]
    return np.ascontiguousarray(
        w.reshape(n, 1024, 32, 128).transpose(0, 2, 3, 1))


def kernel(**inputs):
    NT = 16
    T = NT * 128
    if NT not in _NC_CACHE:
        _NC_CACHE[NT] = build_nc(NT)
    nc = _NC_CACHE[NT]
    cosw, sinw, masks, ident = make_consts(T)
    x = np.ascontiguousarray(inputs["L0_attn_hidden_states"])
    base = {
        "attn_norm_scale": inputs["L0_attn_attn_norm_scale"],
        "Wq": np.ascontiguousarray(inputs["L0_attn_Wq"].reshape(D, NQ * HD)),
        "bq": np.ascontiguousarray(inputs["L0_attn_bq"].reshape(NQ * HD)),
        "Wk": np.ascontiguousarray(inputs["L0_attn_Wk"].reshape(D, NKV * HD)),
        "bk": np.ascontiguousarray(inputs["L0_attn_bk"].reshape(NKV * HD)),
        "Wv": np.ascontiguousarray(inputs["L0_attn_Wv"].reshape(D, NKV * HD)),
        "bv": np.ascontiguousarray(inputs["L0_attn_bv"].reshape(NKV * HD)),
        "Wo": np.ascontiguousarray(inputs["L0_attn_Wo"]),
        "bo": inputs["L0_attn_bo"],
        "sinks": inputs["L0_attn_sinks"],
        "moe_norm_scale": inputs["L0_attn_moe_norm_scale"],
        "router_W": np.ascontiguousarray(inputs["L0_attn_router_W"]),
        "router_b": inputs["L0_attn_router_b"],
        "gate_up_proj": np.ascontiguousarray(inputs["L0_attn_gate_up_proj"]),
        "gate_up_bias": np.ascontiguousarray(inputs["L0_attn_gate_up_bias"]),
        "down_proj": np.ascontiguousarray(inputs["L0_attn_down_proj"]),
        "down_bias": np.ascontiguousarray(inputs["L0_attn_down_bias"]),
        "centroids": np.ascontiguousarray(inputs["L0_ffn_centroids"]),
        "routing_bias": inputs["L0_ffn_routing_bias"],
        "WfcT": _fcT(np.concatenate([np.asarray(inputs["L0_ffn_Wfc_sh"]),
                                     np.asarray(inputs["L0_ffn_Wfc_r"])], axis=0)),
        "WpjT": _pjT(np.concatenate([np.asarray(inputs["L0_ffn_Wproj_sh"]),
                                     np.asarray(inputs["L0_ffn_Wproj_r"])], axis=0)),
        "COS": cosw, "SIN": sinw, "MASKS": masks, "IDENT": ident,
        "UTRI": np.triu(np.ones((128, 128), np.float32)),
        "EOFF": (np.arange(8) * (T // 2) - 1).astype(np.float32),
        "TIDS": np.stack([np.arange(T) // 64, np.arange(T) % 64],
                         axis=1).astype(np.float32),
    }
    base = {k: np.ascontiguousarray(np.asarray(v, dtype=np.float32))
            for k, v in base.items()}
    in_maps = []
    for c in range(8):
        m = dict(base)
        m["x_in"] = np.ascontiguousarray(x[c])
        in_maps.append(m)
    res = run_bass_kernel_spmd(nc, in_maps, core_ids=list(range(8)))
    return np.stack([r["out"] for r in res.results], axis=0)
